# version 11
# baseline (speedup 1.0000x reference)
import math
import numpy as np
import concourse.bass as bass
import concourse.mybir as mybir
from concourse.bass_utils import run_bass_kernel_spmd

F32 = mybir.dt.float32
BF16 = mybir.dt.bfloat16
AF = mybir.ActivationFunctionType
ALU = mybir.AluOpType
AX = mybir.AxisListType

S = 2048
D = 2048
HD = 64
NH = 32
NKV = 8
FE = 768
NE = 32
FSH = 1536
QB = 512
HALF = 1024
EPS = 1e-5
SCALE = HD ** -0.5
MAGIC = 12582912.0

_B0 = [h for h in range(NH) if (h >> 2) % 2 == 0]
_B64 = [h for h in range(NH) if (h >> 2) % 2 == 1]
Q_TILE = {}
Q_BASE = {}
for _i, _h in enumerate(_B0):
    Q_TILE[_h], Q_BASE[_h] = _i, 0
for _i, _h in enumerate(_B64):
    Q_TILE[_h], Q_BASE[_h] = _i, 64

ENGINES = ("sync", "gpsimd", "tensor", "vector", "scalar")


class Sched:

    def __init__(self, nc, blk, semnames):
        self.nc = nc
        self.blk = blk
        self.sems = {}
        self.cnt = {}
        self._cms = []
        for n in semnames:
            cm = nc.semaphore(n)
            self.sems[n] = cm.__enter__()
            self._cms.append(cm)
            self.cnt[n] = 0
        self.ev = {e: [] for e in ENGINES}

    def on(self, eng, fn, waits=(), inc=None, inck=1):
        w = [(s, v) for (s, v) in waits if v > 0]
        if inc is not None:
            self.cnt[inc] += inck
        self.ev[eng].append((w, fn, inc, inck))
        return self.cnt[inc] if inc is not None else None

    def flush(self):
        for eng in ENGINES:
            evs = self.ev[eng]
            if not evs:
                continue

            def section(e, evs=evs):
                for (w, fn, inc, inck) in evs:
                    for (s, v) in w:
                        e.wait_ge(self.sems[s], v)
                    ins = fn(e)
                    if inc is not None:
                        ins.then_inc(self.sems[inc], inck)
            getattr(self.blk, eng)(section)
            self.ev[eng] = []

    def barrier(self):
        self.flush()
        snap = {s: v for s, v in self.cnt.items() if v > 0}
        for eng in ENGINES:
            def f(e, snap=snap):
                for s, v in snap.items():
                    e.wait_ge(self.sems[s], v)
            getattr(self.blk, eng)(f)


def build_program(debug=False):
    nc = bass.Bass("TRN2", target_bir_lowering=False, debug=False)

    x_d = nc.dram_tensor("x", [S, D], F32, kind="ExternalInput").ap()
    wq_d = nc.dram_tensor("wq", [D, D], F32, kind="ExternalInput").ap()
    wk_d = nc.dram_tensor("wk", [NKV * HD, D], F32, kind="ExternalInput").ap()
    wv_d = nc.dram_tensor("wv", [NKV * HD, D], F32, kind="ExternalInput").ap()
    wo_d = nc.dram_tensor("wo", [D, D], F32, kind="ExternalInput").ap()
    gw_d = nc.dram_tensor("gw", [D, NE], F32, kind="ExternalInput").ap()
    wg_d = nc.dram_tensor("wg", [NE, D, FE], F32, kind="ExternalInput").ap()
    wu_d = nc.dram_tensor("wu", [NE, D, FE], F32, kind="ExternalInput").ap()
    wd_d = nc.dram_tensor("wd", [NE, FE, D], F32, kind="ExternalInput").ap()
    sg_d = nc.dram_tensor("sg", [D, FSH], F32, kind="ExternalInput").ap()
    su_d = nc.dram_tensor("su", [D, FSH], F32, kind="ExternalInput").ap()
    sd_d = nc.dram_tensor("sd", [FSH, D], F32, kind="ExternalInput").ap()
    y_d = nc.dram_tensor("y", [S, D], F32, kind="ExternalOutput").ap()

    xbf_scr = nc.dram_tensor("xbf_scr", [16, 128, S], BF16, kind="Internal").ap()
    _sk = "ExternalOutput" if debug else "Internal"
    xT_scr = nc.dram_tensor("xT_scr", [16, 128, S], F32, kind=_sk).ap()
    x1b_scr = nc.dram_tensor("x1b_scr", [16, 128, S], BF16, kind="Internal").ap()
    x1f_scr = nc.dram_tensor("x1f_scr", [16, 128, S], F32, kind=_sk).ap()
    woT_scr = nc.dram_tensor("woT_scr", [16, 128, D], BF16, kind="Internal").ap()

    PS0 = nc.alloc_psum_tensor("PS0", [128, 1024], F32)
    PS1 = nc.alloc_psum_tensor("PS1", [128, 1024], F32)
    PS2 = nc.alloc_psum_tensor("PS2", [128, 1024], F32)
    PS3 = nc.alloc_psum_tensor("PS3", [128, 1024], F32)

    FLOOR = 22528
    CEIL = 229376

    def at(name, shape, dtype, off):
        nbytes = shape[-1] * {F32: 4, BF16: 2}[dtype]
        assert off >= FLOOR and off + nbytes <= CEIL, (name, off, nbytes)
        return nc.alloc_sbuf_tensor_at(name, shape, dtype, offset=off)

    ones_f = at("ones_f", [128, 128], F32, 28672)
    id_f = at("id_f", [128, 128], F32, 29184)
    onesK1 = at("onesK1", [1, 128], F32, 29696)
    i128 = at("i128", [128, 1], F32, 30208)
    nl2pi = at("nl2pi", [128, 1], F32, 30240)
    invf = at("invf", [128, 1], F32, 30272)
    epsT = at("epsT", [128, 1], F32, 30304)
    biasB = at("biasB", [128, NH], F32, 30336)
    kTr = at("kTr", [128, 4 * S], BF16, 30720)
    v_on = at("v_on", [128, 16 * 520], BF16, 47104)
    qTr = at("qTr", [128, 16 * S], BF16, 63744)
    cos_t = at("cos_t", [128, S], F32, 129280)
    sin_p = at("sin_p", [128, S], F32, 137472)
    sin_m = at("sin_m", [128, S], F32, 145664)
    sbc = at("sbc", [128, S], F32, 153856)
    tmpA = at("tmpA", [128, S], F32, 162048)
    tmpB = at("tmpB", [128, S], F32, 170240)
    srow = at("srow", [1, S], F32, 178432)
    xstage = [at(f"xstage{i}", [128, D], F32, 153856 + i * 8192) for i in range(2)]
    xtst = [at(f"xtst{i}", [128, 128], F32, 170240 + i * 512) for i in range(4)]
    xbblk = [at(f"xbblk{i}", [128, 128], BF16, 172288 + i * 256) for i in range(4)]
    rtmp = at("rtmp", [128, QB], F32, 153856)
    rtmp2 = at("rtmp2", [128, QB], F32, 155904)
    xq = [at(f"xq{i}", [128, 16 * QB], BF16, (157952, 174336)[i]) for i in range(2)]
    wstage = [at(f"wstage{i}", [128, D], F32, 174336 + i * 8192) for i in range(2)]
    wkT = at("wkT", [128, 64 * 128], BF16, 190720)
    wvT = at("wvT", [128, 64 * 128], BF16, 207104)
    wqT = at("wqT", [128, 128 * 128], BF16, 190720)
    nsq = at("nsq", [128, S], F32, 153856)
    nred = at("nred", [1, 96], F32, 162048)
    nsc = at("nsc", [1, 2], F32, 162560)
    wostg = [at(f"wostg{i}", [128, D], BF16, 153856 + i * 4096) for i in range(2)]
    PT = at("PT", [128, 16 * QB], BF16, 129280)
    avT = at("avT", [128, 16 * QB], BF16, 145664)
    zb = at("zb", [128, 16 * QB], F32, 162048)
    wot = [at(f"wot{i}", [128, D], BF16, 194816 + i * 4096) for i in range(2)]
    x0t = [at(f"x0t{i}", [128, QB], F32, 203008 + i * 2048) for i in range(2)]
    meanT = at("meanT", [128, QB], F32, 207104)
    ex2T = at("ex2T", [128, QB], F32, 209152)
    slotB = at("slotB", [128, QB], F32, 211200)
    rstdT = at("rstdT", [128, QB], F32, 213248)
    zsq = [at(f"zsq{i}", [128, QB], F32, 215296 + i * 2048) for i in range(2)]
    x1fst = [at(f"x1fst{i}", [128, QB], F32, 219392 + i * 2048) for i in range(2)]
    x1bst = [at(f"x1bst{i}", [128, QB], BF16, 223488 + i * 1024) for i in range(2)]
    recT = at("recT", [1, QB], F32, 225536)
    recT2 = at("recT2", [1, QB], F32, 24576)
    rbs2 = at("rbs2", [64, QB], F32, 22528)
    rbs = at("rbs", [64, QB], F32, 26624)
    acc = at("acc", [128, 16 * HALF], F32, 30720)
    x1bf = at("x1bf", [128, 16 * HALF], BF16, 96256)
    ring = [at(f"ring{i}", [128, 16 * FE], BF16, 129280 + i * 24576) for i in range(2)]
    ucb = at("ucb", [128, 6 * HALF], BF16, 178432)
    tsil = [at(f"tsil{i}", [128, QB], F32, 190720 + i * 2048) for i in range(2)]
    cbT = at("cbT", [128, HALF], F32, 194816)
    cmbT = at("cmbT", [32, S], F32, 198912)
    x1ft = [at(f"x1ft{i}", [128, HALF], F32, 207104 + i * 4096) for i in range(2)]
    mean2 = at("mean2", [128, HALF], F32, 215296)
    cbrow = at("cbrow", [1, HALF], F32, 219392 - 4096)
    slB2 = at("slB2", [128, HALF], F32, 219392)
    rstd2 = at("rstd2", [128, HALF], F32, 223488)
    gwbf = at("gwbf", [128, 16 * NE], BF16, 129280)
    scT = at("scT", [32, S], F32, 130304)
    scN = at("scN", [128, 16 * NE], F32, 138496)
    m8T = at("m8T", [128, 8], F32, 140544)
    wnT = at("wnT", [128, 8], F32, 141056)
    mskT = at("mskT", [128, NE], F32, 141568)
    g1T = at("g1T", [128, 1], F32, 141696)
    gstg = at("gstg", [128, 16 * QB], BF16, 141824)
    cmbN = at("cmbN", [128, 16 * NE], F32, 158208)
    ostg = [at(f"ostg{i}", [128, D], F32, (178432, 190720)[i]) for i in range(2)]

    sems = ["dsA", "dsB", "dsC", "dsD", "dsE", "dsF", "dgA", "dgB", "dW", "dOut",
            "tT", "tS", "tV", "tG",
            "dsW0", "dsW1", "dsW2", "dsW3", "dsX0", "dsX1", "dsX2", "dsX3"]
    blk_cm = nc.Block()
    blk = blk_cm.__enter__()
    C = Sched(nc, blk, sems)

    C.on("gpsimd", lambda e: e.iota(i128[0:32, :], pattern=[[0, 1]], base=0,
                                    channel_multiplier=1,
                                    allow_small_or_imprecise_dtypes=True), inc="tG")
    C.on("gpsimd", lambda e: e.iota(srow[:], pattern=[[1, S]], base=0,
                                    channel_multiplier=0,
                                    allow_small_or_imprecise_dtypes=True), inc="tG")
    og = C.on("gpsimd", lambda e: e.memset(ones_f[:], 1.0), inc="tG")
    C.on("gpsimd", lambda e: e.memset(onesK1[:], 1.0), inc="tG")
    C.on("gpsimd", lambda e: e.memset(nl2pi[:], -math.log(2 * math.pi)), inc="tG")
    C.on("gpsimd", lambda e: e.memset(epsT[:], EPS), inc="tG")
    tg = C.on("gpsimd", lambda e: e.affine_select(id_f[:], ones_f[:],
                                                  pattern=[[1, 128]],
                                                  compare_op=ALU.is_equal, fill=0.0,
                                                  base=0, channel_multiplier=-1),
              waits=[("tG", og)], inc="tG")
    dv = C.on("sync", lambda e: e.dma_start(out=i128[32:64, :], in_=i128[0:32, :]),
              waits=[("tG", tg)], inc="dW", inck=16)
    dv = C.on("sync", lambda e: e.dma_start(out=i128[64:128, :], in_=i128[0:64, :]),
              waits=[("dW", dv)], inc="dW", inck=16)
    dv = C.on("sync", lambda e: e.dma_start(out=sbc[0:1, :], in_=srow[:]),
              waits=[("dW", dv)], inc="dW", inck=16)
    p = 1
    while p < 128:
        dv = C.on("sync", lambda e, p=p: e.dma_start(out=sbc[p:2 * p, :],
                                                     in_=sbc[0:p, :]),
                  waits=[("dW", dv)], inc="dW", inck=16)
        p *= 2
    dw0 = C.cnt["dW"]
    ts0 = C.on("scalar", lambda e: e.activation(out=invf[:], in_=i128[:], func=AF.Exp,
                                                bias=nl2pi[:],
                                                scale=-math.log(10000.0) / 32.0),
               waits=[("dW", dw0)], inc="tS")
    C.on("vector", lambda e: e.tensor_scalar(out=sbc[:], in0=sbc[:], scalar1=invf[:],
                                             scalar2=None, op0=ALU.mult),
         waits=[("tS", ts0), ("dW", dw0)])
    C.on("vector", lambda e: e.drain())
    C.on("vector", lambda e: e.tensor_scalar(out=tmpA[:], in0=sbc[:], scalar1=MAGIC,
                                             scalar2=-MAGIC, op0=ALU.add, op1=ALU.add))
    C.on("vector", lambda e: e.drain())
    C.on("vector", lambda e: e.tensor_tensor(out=tmpA[:], in0=sbc[:], in1=tmpA[:],
                                             op=ALU.subtract))
    C.on("vector", lambda e: e.tensor_scalar(out=tmpB[:], in0=sbc[:], scalar1=0.25,
                                             scalar2=None, op0=ALU.add))
    C.on("vector", lambda e: e.drain())
    C.on("vector", lambda e: e.tensor_scalar(out=sbc[:], in0=tmpB[:], scalar1=MAGIC,
                                             scalar2=-MAGIC, op0=ALU.add, op1=ALU.add))
    C.on("vector", lambda e: e.drain())
    tv0 = C.on("vector", lambda e: e.tensor_tensor(out=tmpB[:], in0=tmpB[:], in1=sbc[:],
                                                   op=ALU.subtract), inc="tV")
    C.on("scalar", lambda e: e.activation(out=sin_p[:], in_=tmpA[:], func=AF.Sin,
                                          bias=0.0, scale=2 * math.pi),
         waits=[("tV", tv0)])
    ts1 = C.on("scalar", lambda e: e.activation(out=cos_t[:], in_=tmpB[:], func=AF.Sin,
                                                bias=0.0, scale=2 * math.pi), inc="tS")
    C.on("vector", lambda e: e.tensor_scalar(out=sin_m[:], in0=sin_p[:], scalar1=-1.0,
                                             scalar2=None, op0=ALU.mult),
         waits=[("tS", ts1)])
    C.barrier()

    ldv = {}
    evv = {}
    wrv = {}
    wbv = {}
    bi = 0
    for st in range(16):
        sl = ("dsA", "dsB")[st % 2]
        w = [("tT", ldv[st - 2])] if st >= 2 else []
        dv = C.on("sync", lambda e, st=st, sl=sl: e.dma_start(
            out=xstage[st % 2][:], in_=x_d[st * 128:(st + 1) * 128, :]),
            waits=w, inc=sl, inck=16)
        for dt in range(16):
            ps = (PS0, PS1)[bi % 2][:, 0:128]
            w = [(sl, dv)] if dt == 0 else []
            if bi >= 2:
                w.append(("tV", evv[bi - 2]))
            ldv[st] = C.on("tensor", lambda e, st=st, dt=dt, ps=ps: e.transpose(
                ps, xstage[st % 2][:, dt * 128:(dt + 1) * 128], id_f[:]),
                waits=w, inc="tT")
            w = [("tT", ldv[st])]
            if bi >= 4:
                w.append((f"dsW{bi % 4}", wrv[bi - 4]))
                w.append((f"dsX{bi % 4}", wbv[bi - 4]))
            C.on("vector", lambda e, bi=bi, ps=ps: e.tensor_copy(
                out=xtst[bi % 4][:], in_=ps), waits=w)
            evv[bi] = C.on("vector", lambda e, bi=bi, ps=ps: e.tensor_copy(
                out=xbblk[bi % 4][:], in_=ps), inc="tV")
            wrv[bi] = C.on("sync", lambda e, st=st, dt=dt, bi=bi: e.dma_start(
                out=xT_scr[dt][:, st * 128:(st + 1) * 128], in_=xtst[bi % 4][:]),
                waits=[("tV", evv[bi])], inc=f"dsW{bi % 4}", inck=16)
            wbv[bi] = C.on("scalar", lambda e, st=st, dt=dt, bi=bi: e.dma_start(
                out=xbf_scr[dt][:, st * 128:(st + 1) * 128], in_=xbblk[bi % 4][:]),
                waits=[("tV", evv[bi])], inc=f"dsX{bi % 4}", inck=16)
            bi += 1
    C.barrier()

    ldv = {}
    evv = {}
    bi = 0
    for r in range(8):
        sl = ("dsA", "dsB")[r % 2]
        w = [("tT", ldv[r - 2])] if r >= 2 else []
        src, rr = (wk_d, r) if r < 4 else (wv_d, r - 4)
        dv = C.on("sync", lambda e, src=src, rr=rr, r=r: e.dma_start(
            out=wstage[r % 2][:], in_=src[rr * 128:(rr + 1) * 128, :]),
            waits=w, inc=sl, inck=16)
        for dt in range(16):
            ps = (PS0, PS1)[bi % 2][:, 0:128]
            w = [(sl, dv)] if dt == 0 else []
            if bi >= 2:
                w.append(("tV", evv[bi - 2]))
            ldv[r] = C.on("tensor", lambda e, r=r, dt=dt, ps=ps: e.transpose(
                ps, wstage[r % 2][:, dt * 128:(dt + 1) * 128], id_f[:]),
                waits=w, inc="tT")
            if r < 4:
                dst = wkT[:, (r * 16 + dt) * 128:(r * 16 + dt + 1) * 128]
            else:
                dst = wvT[:, (dt * 4 + (r - 4)) * 128:(dt * 4 + (r - 4) + 1) * 128]
            evv[bi] = C.on("vector", lambda e, dst=dst, ps=ps: e.tensor_copy(
                out=dst, in_=ps), waits=[("tT", ldv[r])], inc="tV")
            bi += 1
    C.barrier()

    def rope_emit(eng_list, ps, dsts, sc):
        cs = slice(sc * QB, (sc + 1) * QB)
        C.on("vector", lambda e: e.tensor_tensor(out=rtmp[:], in0=ps,
                                                 in1=cos_t[:, cs], op=ALU.mult),
             waits=list(eng_list) + [("tV", C.cnt["tV"])])
        for b in (0, 64):
            lo, hi = b, b + 32
            C.on("vector", lambda e, lo=lo, hi=hi: e.tensor_tensor(
                out=rtmp2[lo:lo + 32, :], in0=ps[hi:hi + 32, :],
                in1=sin_m[hi:hi + 32, cs], op=ALU.mult))
            C.on("vector", lambda e, lo=lo, hi=hi: e.tensor_tensor(
                out=rtmp2[hi:hi + 32, :], in0=ps[lo:lo + 32, :],
                in1=sin_p[lo:lo + 32, cs], op=ALU.mult))
        C.on("vector", lambda e: e.drain())
        if isinstance(dsts, tuple):
            dA, dB = dsts
            C.on("vector", lambda e: e.tensor_tensor(out=dA, in0=rtmp[0:64, :],
                                                     in1=rtmp2[0:64, :], op=ALU.add))
            return C.on("vector", lambda e: e.tensor_tensor(
                out=dB, in0=rtmp[64:128, :], in1=rtmp2[64:128, :], op=ALU.add),
                inc="tV")
        return C.on("vector", lambda e: e.tensor_tensor(out=dsts, in0=rtmp[:],
                                                        in1=rtmp2[:], op=ALU.add),
                    inc="tV")

    evv = {}
    ci = 0
    ldq = {}
    for sc in range(4):
        sl = ("dsA", "dsB")[sc % 2]
        xqb = xq[sc % 2]
        for dt in range(16):
            w = []
            if sc >= 2 and dt == 0:
                w.append(("tT", ldq[sc - 2]))
            dv = C.on("sync", lambda e, dt=dt, sc=sc, xqb=xqb: e.dma_start(
                out=xqb[:, dt * QB:(dt + 1) * QB],
                in_=xbf_scr[dt][:, sc * QB:(sc + 1) * QB]),
                waits=w, inc=sl, inck=16)
        for ec in range(4):
            ps = (PS0, PS1)[ci % 2][:, 0:512]
            w = [(sl, dv)] if ec == 0 else []
            if ci >= 2:
                w.append(("tV", evv[ci - 2]))
            for dt in range(16):
                tv = C.on("tensor", lambda e, ec=ec, dt=dt, ps=ps, xqb=xqb: e.matmul(
                    ps, wkT[:, (ec * 16 + dt) * 128:(ec * 16 + dt + 1) * 128],
                    xqb[:, dt * QB:(dt + 1) * QB],
                    start=(dt == 0), stop=(dt == 15)),
                    waits=w if dt == 0 else [], inc="tT" if dt == 15 else None)
            ldq[sc] = tv
            dst = kTr[:, ec * S + sc * QB: ec * S + (sc + 1) * QB]
            evv[ci] = rope_emit([("tT", tv)], ps, dst, sc)
            ci += 1
        for lq in range(4):
            lc = sc * 4 + lq
            ps = (PS0, PS1)[ci % 2][:, 0:512]
            w = []
            if ci >= 2:
                w.append(("tV", evv[ci - 2]))
            for dt in range(16):
                tv = C.on("tensor", lambda e, lc=lc, lq=lq, dt=dt, ps=ps, xqb=xqb:
                          e.matmul(
                    ps, xqb[:, dt * QB + lq * 128: dt * QB + (lq + 1) * 128],
                    wvT[:, dt * 512:(dt + 1) * 512],
                    start=(dt == 0), stop=(dt == 15)),
                    waits=w if dt == 0 else [], inc="tT" if dt == 15 else None)
            ldq[sc] = tv
            vv = v_on.ap().rearrange("p (l h c) -> p l h c", l=16, c=65)
            C.on("vector", lambda e, lc=lc, ps=ps: e.tensor_copy(
                out=vv[:, lc, :, 0:64],
                in_=ps.rearrange("p (h c) -> p h c", c=64)),
                waits=[("tT", tv)])
            evv[ci] = C.on("vector", lambda e, lc=lc: e.memset(
                vv[:, lc, :, 64:65], 1.0), inc="tV")
            ci += 1
    C.barrier()

    for hec in range(2):
        ldv = {}
        evv2 = {}
        bi = 0
        for r in range(8):
            sl = ("dsA", "dsB")[r % 2]
            w = [("tT", ldv[r - 2])] if r >= 2 else []
            rr = hec * 8 + r
            dv = C.on("sync", lambda e, rr=rr, r=r: e.dma_start(
                out=wstage[r % 2][:], in_=wq_d[rr * 128:(rr + 1) * 128, :]),
                waits=w, inc=sl, inck=16)
            for dt in range(16):
                ps = (PS0, PS1)[bi % 2][:, 0:128]
                w = [(sl, dv)] if dt == 0 else []
                if bi >= 2:
                    w.append(("tV", evv2[bi - 2]))
                ldv[r] = C.on("tensor", lambda e, r=r, dt=dt, ps=ps: e.transpose(
                    ps, wstage[r % 2][:, dt * 128:(dt + 1) * 128], id_f[:]),
                    waits=w, inc="tT")
                dst = wqT[:, (r * 16 + dt) * 128:(r * 16 + dt + 1) * 128]
                evv2[bi] = C.on("vector", lambda e, dst=dst, ps=ps: e.tensor_copy(
                    out=dst, in_=ps), waits=[("tT", ldv[r])], inc="tV")
                bi += 1
        C.barrier()
        evv = {}
        ci = 0
        ldq = {}
        for sc in range(4):
            sl = ("dsA", "dsB")[sc % 2]
            xqb = xq[sc % 2]
            for dt in range(16):
                w = []
                if sc >= 2 and dt == 0:
                    w.append(("tT", ldq[sc - 2]))
                dv = C.on("sync", lambda e, dt=dt, sc=sc, xqb=xqb: e.dma_start(
                    out=xqb[:, dt * QB:(dt + 1) * QB],
                    in_=xbf_scr[dt][:, sc * QB:(sc + 1) * QB]),
                    waits=w, inc=sl, inck=16)
            for ec in range(8):
                ps = (PS0, PS1)[ci % 2][:, 0:512]
                w = [(sl, dv)] if ec == 0 else []
                if ci >= 2:
                    w.append(("tV", evv[ci - 2]))
                for dt in range(16):
                    tv = C.on("tensor", lambda e, ec=ec, dt=dt, ps=ps, xqb=xqb:
                              e.matmul(
                        ps, wqT[:, (ec * 16 + dt) * 128:(ec * 16 + dt + 1) * 128],
                        xqb[:, dt * QB:(dt + 1) * QB],
                        start=(dt == 0), stop=(dt == 15)),
                        waits=w if dt == 0 else [], inc="tT" if dt == 15 else None)
                ldq[sc] = tv
                ecg = hec * 8 + ec
                hA, hB = 2 * ecg, 2 * ecg + 1
                dA = qTr[Q_BASE[hA]:Q_BASE[hA] + 64,
                         Q_TILE[hA] * S + sc * QB: Q_TILE[hA] * S + (sc + 1) * QB]
                dB = qTr[Q_BASE[hB]:Q_BASE[hB] + 64,
                         Q_TILE[hB] * S + sc * QB: Q_TILE[hB] * S + (sc + 1) * QB]
                evv[ci] = rope_emit([("tT", tv)], ps, (dA, dB), sc)
                ci += 1
        C.barrier()

    tiles = [kTr[:, i * S:(i + 1) * S] for i in range(4)] + \
            [qTr[:, i * S:(i + 1) * S] for i in range(16)]
    sqv = {}
    mmv = {}
    rdv = {}
    for t, tile in enumerate(tiles):
        w = [("tT", mmv[(t - 1, 3)])] if t >= 1 else []
        sqv[t] = C.on("scalar", lambda e, tile=tile: e.square(
            nsq[:, 0:2048], tile), waits=w, inc="tS")
        for c in range(4):
            gi = t * 4 + c
            ps = PS2[0:1, (gi % 2) * 512:(gi % 2) * 512 + 512]
            w2 = [("tS", sqv[t])] if c == 0 else []
            if gi >= 2:
                w2.append(("tV", rdv[gi - 2]))
            mmv[(t, c)] = C.on("tensor", lambda e, c=c, ps=ps: e.matmul(
                ps, ones_f[:, 0:1], nsq[:, c * 512:(c + 1) * 512],
                start=True, stop=True), waits=w2, inc="tT")
            rdv[gi] = C.on("vector", lambda e, gi=gi, ps=ps: e.reduce_max(
                out=nred[0:1, gi:gi + 1], in_=ps, axis=AX.X),
                waits=[("tT", mmv[(t, c)])], inc="tV")
    C.on("vector", lambda e: e.drain())
    C.on("vector", lambda e: e.reduce_max(out=nsc[0:1, 0:1], in_=nred[0:1, 0:16],
                                          axis=AX.X))
    C.on("vector", lambda e: e.drain())
    C.on("vector", lambda e: e.reduce_max(out=nsc[0:1, 1:2], in_=nred[0:1, 16:80],
                                          axis=AX.X))
    C.on("vector", lambda e: e.drain())
    mv = C.on("vector", lambda e: e.tensor_tensor(out=nsc[0:1, 1:2],
                                                  in0=nsc[0:1, 1:2],
                                                  in1=nsc[0:1, 0:1], op=ALU.mult),
              inc="tV")
    sv = C.on("scalar", lambda e: e.activation(out=nsc[0:1, 1:2], in_=nsc[0:1, 1:2],
                                               func=AF.Sqrt, bias=0.0, scale=1.0),
              waits=[("tV", mv)], inc="tS")
    nv = C.on("vector", lambda e: e.tensor_scalar(out=nsc[0:1, 1:2],
                                                  in0=nsc[0:1, 1:2], scalar1=-SCALE,
                                                  scalar2=None, op0=ALU.mult),
              waits=[("tS", sv)], inc="tV")
    tb = C.on("tensor", lambda e: e.matmul(PS2[0:128, 512:513], onesK1[0:1, :],
                                           nsc[0:1, 1:2], start=True, stop=True),
              waits=[("tV", nv)], inc="tT")
    C.on("vector", lambda e: e.tensor_copy(out=biasB[0:128, 0:1],
                                           in_=PS2[0:128, 512:513]),
         waits=[("tT", tb)], inc="tV")
    C.on("vector", lambda e: e.drain())
    C.on("vector", lambda e: e.tensor_copy(
        out=biasB[:, 1:NH], in_=biasB[0:128, 0:1].broadcast_to([128, NH - 1])),
        inc="tV")
    C.barrier()

    ldv = {}
    evv = {}
    wsl = {}
    bi = 0
    for epc in range(16):
        sl = ("dsA", "dsB")[epc % 2]
        w = [("tT", ldv[epc - 2])] if epc >= 2 else []
        dv = C.on("sync", lambda e, epc=epc: e.dma_start(
            out=wstage[epc % 2][:], in_=wo_d[epc * 128:(epc + 1) * 128, :]),
            waits=w, inc=sl, inck=16)
        for dt in range(16):
            ps = (PS0, PS1)[bi % 2][:, 0:128]
            w = [(sl, dv)] if dt == 0 else []
            if bi >= 2:
                w.append(("tV", evv[bi - 2]))
            ldv[epc] = C.on("tensor", lambda e, epc=epc, dt=dt, ps=ps: e.transpose(
                ps, wstage[epc % 2][:, dt * 128:(dt + 1) * 128], id_f[:]),
                waits=w, inc="tT")
            w = [("tT", ldv[epc])]
            if epc >= 2 and dt == 0:
                w.append((("dsE", "dsF")[epc % 2], wsl[epc - 2]))
            hA, hB = 2 * dt, 2 * dt + 1
            C.on("vector", lambda e, epc=epc, dt=dt, ps=ps, hA=hA: e.tensor_copy(
                out=wostg[epc % 2][Q_BASE[hA]:Q_BASE[hA] + 64,
                                   Q_TILE[hA] * 128:(Q_TILE[hA] + 1) * 128],
                in_=ps[0:64, :]), waits=w)
            evv[bi] = C.on("vector", lambda e, epc=epc, dt=dt, ps=ps, hB=hB:
                           e.tensor_copy(
                out=wostg[epc % 2][Q_BASE[hB]:Q_BASE[hB] + 64,
                                   Q_TILE[hB] * 128:(Q_TILE[hB] + 1) * 128],
                in_=ps[64:128, :]), inc="tV")
            bi += 1
        wsl[epc] = C.on("sync", lambda e, epc=epc: e.dma_start(
            out=woT_scr[epc][:], in_=wostg[epc % 2][:]),
            waits=[("tV", evv[bi - 1])], inc=("dsE", "dsF")[epc % 2], inck=16)
    C.barrier()

    if debug:
        cos_d = nc.dram_tensor("cos_d", [128, S], F32, kind="ExternalOutput").ap()
        sin_d = nc.dram_tensor("sin_d", [128, S], F32, kind="ExternalOutput").ap()
        invf_d = nc.dram_tensor("invf_d", [128, 1], F32, kind="ExternalOutput").ap()
        C.on("gpsimd", lambda e: e.dma_start(out=cos_d[:], in_=cos_t[:]),
             inc="dgB", inck=16)
        C.on("gpsimd", lambda e: e.dma_start(out=sin_d[:], in_=sin_p[:]),
             inc="dgB", inck=16)
        C.on("gpsimd", lambda e: e.dma_start(out=invf_d[:], in_=invf[:]),
             inc="dgB", inck=16)
        C.barrier()

    spv, expv, avpv, recv, rbv, copyv, multv = {}, {}, {}, {}, {}, {}, {}
    oT, szT, sz2T, zV, sqS = {}, {}, {}, {}, {}
    wosl, x0sl = {}, {}
    fin_prev = {"tm": 0, "fv": 0}
    for q in range(4):
        q0 = q * QB
        bT = C.cnt["tT"]
        bS = C.cnt["tS"]
        bV = C.cnt["tV"]
        kq = q * NH
        rb_ps = PS3[0:64, 0:512]
        pairs = list(zip(_B0, _B64))
        k_of = {}
        for _p, (_a, _b) in enumerate(pairs):
            k_of[_a], k_of[_b] = kq + 2 * _p, kq + 2 * _p + 1
        stA = [PS0[:, 0:512], PS0[:, 512:1024]]
        stB = [PS1[:, 0:512], PS3[:, 512:1024]]
        avp3 = [PS1[0:65, 512:1024], PS2[0:65, 0:512], PS2[0:65, 512:1024]]

        def pt_slot(h, lc):
            base = 0 if Q_BASE[h] == 0 else 4
            return PT[:, (base + lc % 4) * 512:(base + lc % 4 + 1) * 512]

        def emit_av(h, lc, do_inc):
            kh = h // 4
            k = k_of[h]
            w = [("tS", expv[(h, lc)])]
            if lc == 0 and k >= 3:
                w.append(("tV", multv[k - 3]))
            if lc == 0 and q >= 1 and (k - kq) < 3 and k % 3 in (1, 2):
                w.append(("tS", fin_prev["tm"]))
            avpv[(h, lc)] = C.on("tensor", lambda e, h=h, lc=lc, kh=kh, k=k:
                                 e.matmul(
                avp3[k % 3],
                v_on[:, lc * 520 + kh * 65: lc * 520 + (kh + 1) * 65],
                pt_slot(h, lc),
                start=(lc == 0), stop=(lc == 15)), waits=w,
                inc="tT" if do_inc else None)

        for pp in range(17):
            for lc in range(16):
                if pp < 16:
                    for si, h in enumerate(pairs[pp]):
                        kh = h // 4
                        w = []
                        if lc >= 2:
                            w.append(("tS", expv[(h, lc - 2)]))
                        elif pp >= 1:
                            w.append(("tS", expv[(pairs[pp - 1][si], lc + 14)]))
                        elif q >= 1:
                            w.append(("tS", expv[(pairs[15][si], lc + 14)]))
                            if si == 1 and lc == 1:
                                w.append(("tT", oT[15]))
                        kap = kTr[(kh % 2) * 64:(kh % 2) * 64 + 64,
                                  (kh // 2) * S + lc * 128:(kh // 2) * S + (lc + 1) * 128]
                        qap = qTr[Q_BASE[h]:Q_BASE[h] + 64,
                                  Q_TILE[h] * S + q0:Q_TILE[h] * S + q0 + QB]
                        st = (stA, stB)[Q_BASE[h] // 64][lc % 2]
                        jv = C.on("tensor", lambda e, kap=kap, qap=qap,
                                  st=st: e.matmul(
                            st, kap, qap, start=True, stop=True),
                            waits=w, inc="tT" if si == 1 else None)
                    spv[(pairs[pp][0], lc)] = jv
                    spv[(pairs[pp][1], lc)] = jv
                if lc == 2 and pp >= 1:
                    hA, hB = pairs[pp - 1]
                    rcA, rcB = recT, recT2
                    rsA, rsB = rbs, rbs2
                    for h, rc in ((hA, rcA), (hB, rcB)):
                        k = k_of[h]
                        w = [("tT", avpv[(h, 15)])]
                        if k >= 4:
                            w.append(("tT", rbv[k - 2]))
                        recv[k] = C.on("vector", lambda e, k=k, rc=rc: e.reciprocal(
                            rc[0:1, :], avp3[k % 3][64:65, 0:512]), waits=w, inc="tV")
                    kA, kB = k_of[hA], k_of[hB]
                    rbv[kA] = C.on("tensor", lambda e, rcA=rcA: e.matmul(
                        rb_ps, onesK1[0:1, 0:64], rcA[0:1, :], start=True, stop=True),
                        waits=[("tV", recv[kA])] +
                              ([("tV", copyv[kB - 2])] if kB >= 3 else []), inc="tT")
                    copyv[kA] = C.on("vector", lambda e, rsA=rsA: e.tensor_copy(
                        out=rsA[:], in_=rb_ps),
                        waits=[("tT", rbv[kA])] +
                              ([("tV", multv[kA - 2])] if kA >= 2 else []), inc="tV")
                    rbv[kB] = C.on("tensor", lambda e, rcB=rcB: e.matmul(
                        rb_ps, onesK1[0:1, 0:64], rcB[0:1, :], start=True, stop=True),
                        waits=[("tV", recv[kB]), ("tV", copyv[kA])], inc="tT")
                    copyv[kB] = C.on("vector", lambda e, rsB=rsB: e.tensor_copy(
                        out=rsB[:], in_=rb_ps),
                        waits=[("tT", rbv[kB])] +
                              ([("tV", multv[kB - 2])] if kB >= 2 else []), inc="tV")
                    for h, rs in ((hA, rsA), (hB, rsB)):
                        k = k_of[h]
                        dst = avT[Q_BASE[h]:Q_BASE[h] + 64,
                                  Q_TILE[h] * 512:(Q_TILE[h] + 1) * 512]
                        multv[k] = C.on("vector", lambda e, k=k, dst=dst, rs=rs:
                                        e.tensor_tensor(
                            out=dst, in0=avp3[k % 3][0:64, 0:512], in1=rs[:],
                            op=ALU.mult), waits=[("tV", copyv[k])], inc="tV")
                if lc >= 2 and pp < 16:
                    emit_av(pairs[pp][0], lc - 2, False)
                    emit_av(pairs[pp][1], lc - 2, True)
                    avpv[(pairs[pp][0], lc - 2)] = avpv[(pairs[pp][1], lc - 2)]
                elif lc < 2 and pp >= 1:
                    emit_av(pairs[pp - 1][0], lc + 14, False)
                    emit_av(pairs[pp - 1][1], lc + 14, True)
                    avpv[(pairs[pp - 1][0], lc + 14)] = avpv[(pairs[pp - 1][1], lc + 14)]
                if pp < 16:
                    for h in pairs[pp]:
                        w = [("tT", spv[(h, lc)])]
                        if lc >= 4:
                            w.append(("tT", avpv[(h, lc - 4)]))
                        elif pp >= 1:
                            hp = pairs[pp - 1][Q_BASE[h] // 64]
                            w.append(("tT", avpv[(hp, lc + 12)]))
                        elif q >= 1:
                            w.append(("tT", avpv[(pairs[15][Q_BASE[h] // 64],
                                                  lc + 12)]))
                        st = (stA, stB)[Q_BASE[h] // 64][lc % 2]
                        expv[(h, lc)] = C.on("scalar", lambda e, h=h, lc=lc, st=st:
                                             e.activation(
                            out=pt_slot(h, lc), in_=st, func=AF.Exp,
                            bias=biasB[:, h:h + 1], scale=SCALE),
                            waits=w, inc="tS")
        assert C.cnt["tS"] == bS + 512
        assert C.cnt["tV"] == bV + 96
        bV96 = multv[kq + 31]
        bT2 = C.cnt["tT"]
        dsA0, dsB0 = C.cnt["dsA"], C.cnt["dsB"]
        dsC0, dsD0 = C.cnt["dsC"], C.cnt["dsD"]
        dsE0, dsF0 = C.cnt["dsE"], C.cnt["dsF"]

        def stats_pair(epc):
            w = [("tV", zV[epc])]
            if epc == 0:
                w.append(("tS", fin_prev["tm"]))
            szT[epc] = C.on("tensor", lambda e, epc=epc: e.matmul(
                PS2[:, 0:512], ones_f[:], zb[:, epc * 512:(epc + 1) * 512],
                start=(epc == 0), stop=(epc == 15)),
                waits=w, inc="tT")
            sz2T[epc] = C.on("tensor", lambda e, epc=epc: e.matmul(
                PS2[:, 512:1024], ones_f[:], zsq[epc % 2][:],
                start=(epc == 0), stop=(epc == 15)),
                waits=[("tS", sqS[epc])], inc="tT")

        for epc in range(16):
            slw = ("dsA", "dsB")[epc % 2]
            if epc >= 2:
                w = [("tT", oT[epc - 2])]
            elif q >= 1:
                w = [("tT", oT[epc + 14])]
            else:
                w = []
            wosl[epc] = C.on("sync", lambda e, epc=epc: e.dma_start(
                out=wot[epc % 2][:], in_=woT_scr[epc][:]), waits=w, inc=slw, inck=16)
            slx = ("dsC", "dsD")[epc % 2]
            if epc >= 2:
                w = [("tV", zV[epc - 2])]
            elif q >= 1:
                w = [("tV", zV[epc + 14])]
            else:
                w = []
            x0sl[epc] = C.on("sync", lambda e, epc=epc, q0=q0: e.dma_start(
                out=x0t[epc % 2][:], in_=xT_scr[epc][:, q0:q0 + QB]),
                waits=w, inc=slx, inck=16)
            op = PS3[:, (epc % 2) * 512:(epc % 2) * 512 + 512]
            w = [(slw, wosl[epc])]
            if epc == 0:
                w.append(("tV", bV96))
            if epc >= 2:
                w.append(("tV", zV[epc - 2]))
            elif q >= 1:
                w.append(("tV", zV[epc + 14]))
            for et in range(16):
                oT[epc] = C.on("tensor", lambda e, epc=epc, et=et, op=op: e.matmul(
                    op, wot[epc % 2][:, et * 128:(et + 1) * 128],
                    avT[:, et * 512:(et + 1) * 512],
                    start=(et == 0), stop=(et == 15)),
                    waits=w if et == 0 else [], inc="tT" if et == 15 else None)
            zV[epc] = C.on("vector", lambda e, epc=epc, op=op: e.tensor_tensor(
                out=zb[:, epc * 512:(epc + 1) * 512], in0=op, in1=x0t[epc % 2][:],
                op=ALU.add), waits=[("tT", oT[epc]), (slx, x0sl[epc])], inc="tV")
            w = [("tV", zV[epc])]
            if epc >= 2:
                w.append(("tT", sz2T[epc - 2]))
            elif q >= 1:
                w.append(("tT", sz2T[epc + 14]))
            sqS[epc] = C.on("scalar", lambda e, epc=epc: e.square(
                zsq[epc % 2][:], zb[:, epc * 512:(epc + 1) * 512]), waits=w, inc="tS")
            if epc >= 1:
                stats_pair(epc - 1)
        stats_pair(15)
        tm = C.on("scalar", lambda e: e.activation(out=meanT[:], in_=PS2[:, 0:512],
                                                   func=AF.Copy, bias=0.0,
                                                   scale=1.0 / D),
                  waits=[("tT", sz2T[15]), ("tV", fin_prev["fv"])], inc="tS")
        tm = C.on("scalar", lambda e: e.activation(out=ex2T[:], in_=PS2[:, 512:1024],
                                                   func=AF.Copy, bias=0.0,
                                                   scale=1.0 / D), inc="tS")
        mv = C.on("vector", lambda e: e.tensor_tensor(out=slotB[:], in0=meanT[:],
                                                      in1=meanT[:], op=ALU.mult),
                  waits=[("tS", tm)], inc="tV")
        C.on("vector", lambda e: e.drain())
        mv = C.on("vector", lambda e: e.tensor_tensor(out=slotB[:], in0=ex2T[:],
                                                      in1=slotB[:], op=ALU.subtract),
                  inc="tV")
        sv = C.on("scalar", lambda e: e.activation(out=ex2T[:], in_=slotB[:],
                                                   func=AF.Sqrt, bias=epsT[:],
                                                   scale=1.0),
                  waits=[("tV", mv)], inc="tS")
        rv = C.on("vector", lambda e: e.reciprocal(rstdT[:], ex2T[:]),
                  waits=[("tS", sv)], inc="tV")
        C.on("vector", lambda e: e.drain())
        for ep in range(0, 16, 2):
            for j in (0, 1):
                epc = ep + j
                w = [("tV", rv)]
                slt = ("dsE", "dsF")[epc % 2]
                base = (dsE0, dsF0)[epc % 2]
                if epc >= 2:
                    w.append((slt, base + 32 * (epc // 2)))
                elif q >= 1:
                    w.append((slt, base))
                C.on("vector", lambda e, epc=epc: e.tensor_tensor(
                    out=x1fst[epc % 2][:], in0=zb[:, epc * 512:(epc + 1) * 512],
                    in1=meanT[:], op=ALU.subtract), waits=w)
            for j in (0, 1):
                epc = ep + j
                fv = C.on("vector", lambda e, epc=epc: e.tensor_tensor(
                    out=x1fst[epc % 2][:], in0=x1fst[epc % 2][:], in1=rstdT[:],
                    op=ALU.mult), inc="tV")
                bv2 = C.on("scalar", lambda e, epc=epc: e.copy(
                    x1bst[epc % 2][:], x1fst[epc % 2][:]),
                    waits=[("tV", fv)], inc="tS")
                slt = ("dsE", "dsF")[epc % 2]
                C.on("sync", lambda e, epc=epc, q0=q0: e.dma_start(
                    out=x1f_scr[epc][:, q0:q0 + QB], in_=x1fst[epc % 2][:]),
                    waits=[("tV", fv)], inc=slt, inck=16)
                C.on("sync", lambda e, epc=epc, q0=q0: e.dma_start(
                    out=x1b_scr[epc][:, q0:q0 + QB], in_=x1bst[epc % 2][:]),
                    waits=[("tS", bv2)], inc=slt, inck=16)
                fin_prev["fv"] = fv
        fin_prev["tm"] = tm
    C.barrier()

    if debug:
        qTr_d = nc.dram_tensor("qTr_d", [128, 16 * S], F32, kind="ExternalOutput").ap()
        kTr_d = nc.dram_tensor("kTr_d", [128, 4 * S], F32, kind="ExternalOutput").ap()
        avT_d = nc.dram_tensor("avT_d", [128, 16 * QB], F32, kind="ExternalOutput").ap()
        bias_d = nc.dram_tensor("bias_d", [128, NH], F32, kind="ExternalOutput").ap()
        von_d = nc.dram_tensor("von_d", [128, 16 * 520], F32, kind="ExternalOutput").ap()
        for i in range(16):
            C.on("gpsimd", lambda e, i=i: e.dma_start(
                out=qTr_d[:, i * S:(i + 1) * S], in_=qTr[:, i * S:(i + 1) * S]),
                inc="dgB", inck=16)
        for i in range(4):
            C.on("gpsimd", lambda e, i=i: e.dma_start(
                out=kTr_d[:, i * S:(i + 1) * S], in_=kTr[:, i * S:(i + 1) * S]),
                inc="dgB", inck=16)
        C.on("gpsimd", lambda e: e.dma_start(out=avT_d[:], in_=avT[:]),
             inc="dgB", inck=16)
        C.on("gpsimd", lambda e: e.dma_start(out=bias_d[:], in_=biasB[:]),
             inc="dgB", inck=16)
        C.on("gpsimd", lambda e: e.dma_start(out=von_d[:], in_=v_on[:]),
             inc="dgB", inck=16)
        C.barrier()

    for dt in range(16):
        gl = C.on("gpsimd", lambda e, dt=dt: e.dma_start(
            out=gwbf[:, dt * 32:(dt + 1) * 32],
            in_=gw_d[dt * 128:(dt + 1) * 128, :]), inc="dgA", inck=16)
    gts = {}
    sgv = {}
    for sc in range(4):
        for dt in range(16):
            gv = C.on("sync", lambda e, dt=dt, sc=sc: e.dma_start(
                out=gstg[:, dt * QB:(dt + 1) * QB],
                in_=x1b_scr[dt][:, sc * QB:(sc + 1) * QB]),
                waits=[("tT", gts[sc - 1])] if (sc >= 1 and dt == 0) else [],
                inc=("dsA", "dsB")[sc % 2], inck=16)
        w = [(("dsA", "dsB")[sc % 2], gv), ("dgA", gl)]
        if sc >= 2:
            w.append(("tS", sgv[sc - 2]))
        gp_ps = (PS0, PS1)[sc % 2][0:32, 0:512]
        for dt in range(16):
            gts[sc] = C.on("tensor", lambda e, sc=sc, dt=dt, gp_ps=gp_ps: e.matmul(
                gp_ps, gwbf[:, dt * 32:(dt + 1) * 32], gstg[:, dt * QB:(dt + 1) * QB],
                start=(dt == 0), stop=(dt == 15)),
                waits=w if dt == 0 else [], inc="tT" if dt == 15 else None)
        sgv[sc] = C.on("scalar", lambda e, sc=sc, gp_ps=gp_ps: e.activation(
            out=scT[:, sc * 512:(sc + 1) * 512], in_=gp_ps, func=AF.Sigmoid,
            bias=0.0, scale=1.0), waits=[("tT", gts[sc])], inc="tS")
    C.barrier()
    prev_cp = 0
    for tc in range(16):
        t_ps = PS3[:, (tc % 2) * 512:(tc % 2) * 512 + 32]
        tt = C.on("tensor", lambda e, tc=tc, t_ps=t_ps: e.transpose(
            t_ps, scT[:, tc * 128:(tc + 1) * 128], id_f[0:32, 0:32]),
            waits=[("tV", prev_cp)] if tc >= 2 else [], inc="tT")
        C.on("vector", lambda e, tc=tc, t_ps=t_ps: e.tensor_copy(
            out=scN[:, tc * 32:(tc + 1) * 32], in_=t_ps), waits=[("tT", tt)])
        C.on("vector", lambda e: e.drain())
        C.on("vector", lambda e, tc=tc: e.max(out=m8T[:],
                                              in_=scN[:, tc * 32:(tc + 1) * 32]))
        C.on("vector", lambda e: e.drain())
        C.on("vector", lambda e: e.reduce_sum(out=g1T[:], in_=m8T[:, 0:6], axis=AX.X))
        C.on("vector", lambda e: e.drain())
        C.on("vector", lambda e: e.tensor_scalar(out=g1T[:], in0=g1T[:], scalar1=1e-20,
                                                 scalar2=None, op0=ALU.add))
        C.on("vector", lambda e: e.drain())
        C.on("vector", lambda e: e.reciprocal(g1T[:], g1T[:]))
        C.on("vector", lambda e: e.drain())
        C.on("vector", lambda e: e.tensor_scalar(out=wnT[:, 0:6], in0=m8T[:, 0:6],
                                                 scalar1=g1T[:], scalar2=None,
                                                 op0=ALU.mult))
        C.on("vector", lambda e, tc=tc: e.memset(cmbN[:, tc * 32:(tc + 1) * 32], 0.0))
        C.on("vector", lambda e: e.drain())
        for j in range(6):
            C.on("vector", lambda e, tc=tc, j=j: e.tensor_scalar(
                out=mskT[:], in0=scN[:, tc * 32:(tc + 1) * 32],
                scalar1=m8T[:, j:j + 1], scalar2=None, op0=ALU.is_equal))
            C.on("vector", lambda e: e.drain())
            cpv = C.on("vector", lambda e, tc=tc, j=j: e.scalar_tensor_tensor(
                out=cmbN[:, tc * 32:(tc + 1) * 32], in0=mskT[:],
                scalar=wnT[:, j:j + 1], in1=cmbN[:, tc * 32:(tc + 1) * 32],
                op0=ALU.mult, op1=ALU.add), inc="tV" if j == 5 else None)
            C.on("vector", lambda e: e.drain())
        t2_ps = PS3[0:32, (tc % 2) * 512 + 64:(tc % 2) * 512 + 192]
        tb = C.on("tensor", lambda e, tc=tc, t2_ps=t2_ps: e.transpose(
            t2_ps, cmbN[:, tc * 32:(tc + 1) * 32], id_f[:]),
            waits=[("tV", cpv)], inc="tT")
        prev_cp = C.on("vector", lambda e, tc=tc, t2_ps=t2_ps: e.tensor_copy(
            out=cmbT[:, tc * 128:(tc + 1) * 128], in_=t2_ps),
            waits=[("tT", tb)], inc="tV")
    C.barrier()

    def uw_src(ei, dt):
        if ei == 0:
            return su_d[dt * 128:(dt + 1) * 128, 0:FE]
        if ei == 1:
            return su_d[dt * 128:(dt + 1) * 128, FE:2 * FE]
        return wu_d[ei - 2][dt * 128:(dt + 1) * 128, :]

    def gw_src(ei, dt):
        if ei == 0:
            return sg_d[dt * 128:(dt + 1) * 128, 0:FE]
        if ei == 1:
            return sg_d[dt * 128:(dt + 1) * 128, FE:2 * FE]
        return wg_d[ei - 2][dt * 128:(dt + 1) * 128, :]

    def dw_src(ei, ft):
        if ei == 0:
            return sd_d[ft * 128:(ft + 1) * 128, :]
        if ei == 1:
            return sd_d[FSH // 2 + ft * 128:FSH // 2 + (ft + 1) * 128, :]
        return wd_d[ei - 2][ft * 128:(ft + 1) * 128, :]

    consumed = {}
    ps2_user = {0: 0, 1: 0}
    prev_cbmm = 0
    for h2 in range(2):
        tok0 = h2 * HALF
        for dt in range(16):
            C.on("sync", lambda e, dt=dt, tok0=tok0: e.dma_start(
                out=x1bf[:, dt * HALF:(dt + 1) * HALF],
                in_=x1b_scr[dt][:, tok0:tok0 + HALF]), inc="dW", inck=16)
        dwh = C.cnt["dW"]
        u_free = {}
        g_free = {}
        r_free = {}
        tsil_free = {}
        act_v = {}
        x0 = C.cnt["dsC"]
        x1 = C.cnt["dsD"]
        x1sl = {}
        for ei in range(34):
            mb = (h2 * 34 + ei) * 3
            real = ei >= 2
            exp_i = ei - 2
            if real:
                wrow = [("tT", prev_cbmm)] if prev_cbmm else []
                rowv = C.on("sync", lambda e, exp_i=exp_i, tok0=tok0: e.dma_start(
                    out=cbrow[:], in_=cmbT[exp_i:exp_i + 1, tok0:tok0 + HALF]),
                    waits=wrow, inc="dsC", inck=16)
                for sc in range(2):
                    w = [("tV", ps2_user[sc]), ("dsC", rowv)]
                    cb_ps = PS2[:, sc * 512:(sc + 1) * 512]
                    tb = C.on("tensor", lambda e, sc=sc, cb_ps=cb_ps: e.matmul(
                        cb_ps, onesK1[0:1, :], cbrow[0:1, sc * 512:(sc + 1) * 512],
                        start=True, stop=True), waits=w, inc="tT")
                    prev_cbmm = tb
                    ps2_user[sc] = C.on("vector", lambda e, sc=sc, cb_ps=cb_ps:
                                        e.tensor_copy(out=cbT[:, sc * 512:(sc + 1) * 512],
                                                      in_=cb_ps),
                                        waits=[("tT", tb)], inc="tV")
            m = mb + 0
            slot = m % 2
            slx = ("dgA", "dgB")[slot]
            for dt in range(16):
                w = [("tT", consumed[m - 2])] if (m >= 2 and dt == 0) else []
                uv = C.on("gpsimd", lambda e, ei=ei, dt=dt, slot=slot: e.dma_start(
                    out=ring[slot][:, dt * FE:(dt + 1) * FE], in_=uw_src(ei, dt)),
                    waits=w, inc=slx, inck=16)
            for k in range(12):
                sc, fc = k // 6, k % 6
                u_ps = PS0[:, (k % 2) * 512:(k % 2) * 512 + 512]
                w = []
                if k == 0:
                    w = [(slx, uv), ("dW", dwh)]
                if u_free.get(k % 2):
                    w.append(("tV", u_free[k % 2]))
                for dt in range(16):
                    tv = C.on("tensor", lambda e, dt=dt, sc=sc, fc=fc, slot=slot,
                              u_ps=u_ps: e.matmul(
                        u_ps, ring[slot][:, dt * FE + fc * 128:dt * FE + (fc + 1) * 128],
                        x1bf[:, dt * HALF + sc * 512:dt * HALF + (sc + 1) * 512],
                        start=(dt == 0), stop=(dt == 15)),
                        waits=w if dt == 0 else [], inc="tT" if dt == 15 else None)
                if k == 11:
                    consumed[m] = tv
                if real:
                    u_free[k % 2] = C.on("vector", lambda e, sc=sc, fc=fc, u_ps=u_ps:
                                         e.tensor_tensor(
                        out=ucb[:, fc * HALF + sc * 512:fc * HALF + (sc + 1) * 512],
                        in0=u_ps, in1=cbT[:, sc * 512:(sc + 1) * 512], op=ALU.mult),
                        waits=[("tT", tv)], inc="tV")
                else:
                    u_free[k % 2] = C.on("vector", lambda e, sc=sc, fc=fc, u_ps=u_ps:
                                         e.tensor_copy(
                        out=ucb[:, fc * HALF + sc * 512:fc * HALF + (sc + 1) * 512],
                        in_=u_ps), waits=[("tT", tv)], inc="tV")
            m = mb + 1
            slot = m % 2
            slx = ("dgA", "dgB")[slot]
            for dt in range(16):
                w = [("tT", consumed[m - 2])] if (m >= 2 and dt == 0) else []
                gv = C.on("gpsimd", lambda e, ei=ei, dt=dt, slot=slot: e.dma_start(
                    out=ring[slot][:, dt * FE:(dt + 1) * FE], in_=gw_src(ei, dt)),
                    waits=w, inc=slx, inck=16)
            for k in range(12):
                sc, fc = k // 6, k % 6
                g_ps = PS3[:, (k % 2) * 512:(k % 2) * 512 + 512]
                w = []
                if k == 0:
                    w = [(slx, gv)]
                if g_free.get(k % 2):
                    w.append(("tS", g_free[k % 2]))
                for dt in range(16):
                    tv = C.on("tensor", lambda e, dt=dt, sc=sc, fc=fc, slot=slot,
                              g_ps=g_ps: e.matmul(
                        g_ps, ring[slot][:, dt * FE + fc * 128:dt * FE + (fc + 1) * 128],
                        x1bf[:, dt * HALF + sc * 512:dt * HALF + (sc + 1) * 512],
                        start=(dt == 0), stop=(dt == 15)),
                        waits=w if dt == 0 else [], inc="tT" if dt == 15 else None)
                if k == 11:
                    consumed[m] = tv
                w = [("tT", tv)]
                if tsil_free.get(k % 2):
                    w.append(("tV", tsil_free[k % 2]))
                g_free[k % 2] = C.on("scalar", lambda e, k=k, g_ps=g_ps: e.activation(
                    out=tsil[k % 2][:], in_=g_ps, func=AF.Silu, bias=0.0, scale=1.0),
                    waits=w, inc="tS")
                tsil_free[k % 2] = C.on("vector", lambda e, sc=sc, fc=fc, k=k:
                                        e.tensor_tensor(
                    out=ucb[:, fc * HALF + sc * 512:fc * HALF + (sc + 1) * 512],
                    in0=tsil[k % 2][:],
                    in1=ucb[:, fc * HALF + sc * 512:fc * HALF + (sc + 1) * 512],
                    op=ALU.mult), waits=[("tS", g_free[k % 2])], inc="tV")
                if k % 6 == 5:
                    act_v[sc] = tsil_free[k % 2]
            m = mb + 2
            slot = m % 2
            slx = ("dgA", "dgB")[slot]
            for ft in range(6):
                w = [("tT", consumed[m - 2])] if (m >= 2 and ft == 0) else []
                dv = C.on("gpsimd", lambda e, ei=ei, ft=ft, slot=slot: e.dma_start(
                    out=ring[slot][:, ft * D:(ft + 1) * D], in_=dw_src(ei, ft)),
                    waits=w, inc=slx, inck=16)
            for j in range(32):
                dc, sc = j // 2, j % 2
                if ei == 0 and sc == 0:
                    slf = ("dsC", "dsD")[dc % 2]
                    wx = []
                    if dc >= 2:
                        wx.append(("tV", x1sl[dc - 2]))
                    xv = C.on("sync", lambda e, dc=dc, tok0=tok0: e.dma_start(
                        out=x1ft[dc % 2][:], in_=x1f_scr[dc][:, tok0:tok0 + HALF]),
                        waits=wx, inc=slf, inck=16)
                    x1sl[dc] = ("ld", xv)
                r_ps = PS1[:, (j % 2) * 512:(j % 2) * 512 + 512]
                w = []
                if j == 0:
                    w = [(slx, dv), ("tV", act_v[0])]
                if j == 1:
                    w = [("tV", act_v[1])]
                if j >= 2 and r_free.get(j % 2):
                    w.append(("tV", r_free[j % 2]))
                for ft in range(6):
                    tv = C.on("tensor", lambda e, ft=ft, dc=dc, sc=sc, slot=slot,
                              r_ps=r_ps: e.matmul(
                        r_ps, ring[slot][:, ft * D + dc * 128:ft * D + (dc + 1) * 128],
                        ucb[:, ft * HALF + sc * 512:ft * HALF + (sc + 1) * 512],
                        start=(ft == 0), stop=(ft == 5)),
                        waits=w if ft == 0 else [], inc="tT" if ft == 5 else None)
                if j == 31:
                    consumed[m] = tv
                dst = acc[:, dc * HALF + sc * 512:dc * HALF + (sc + 1) * 512]
                if ei == 0:
                    slf = ("dsC", "dsD")[dc % 2]
                    w2 = [("tT", tv), (slf, x1sl[dc][1])]
                    r_free[j % 2] = C.on("vector", lambda e, dst=dst, dc=dc, sc=sc,
                                         r_ps=r_ps: e.tensor_tensor(
                        out=dst, in0=r_ps,
                        in1=x1ft[dc % 2][:, sc * 512:(sc + 1) * 512],
                        op=ALU.add), waits=w2, inc="tV")
                    if sc == 1:
                        x1sl[dc] = r_free[j % 2]
                else:
                    r_free[j % 2] = C.on("vector", lambda e, dst=dst, r_ps=r_ps:
                                         e.tensor_tensor(out=dst, in0=r_ps, in1=dst,
                                                         op=ALU.add),
                                         waits=[("tT", tv)], inc="tV")
        C.barrier()

        sq2 = {}
        st2 = {}
        for j in range(32):
            dc, sc = j // 2, j % 2
            w = []
            if j >= 2:
                w.append(("tT", st2[j - 2]))
            sq2[j] = C.on("scalar", lambda e, dc=dc, sc=sc, j=j: e.square(
                tsil[j % 2][:], acc[:, dc * HALF + sc * 512:dc * HALF + (sc + 1) * 512]),
                waits=w, inc="tS")
            C.on("tensor", lambda e, dc=dc, sc=sc: e.matmul(
                (PS2[:, 0:512], PS2[:, 512:1024])[sc], ones_f[:],
                acc[:, dc * HALF + sc * 512:dc * HALF + (sc + 1) * 512],
                start=(dc == 0), stop=(dc == 15)), inc="tT")
            st2[j] = C.on("tensor", lambda e, dc=dc, sc=sc, j=j: e.matmul(
                (PS3[:, 0:512], PS3[:, 512:1024])[sc], ones_f[:], tsil[j % 2][:],
                start=(dc == 0), stop=(dc == 15)),
                waits=[("tS", sq2[j])], inc="tT")
        tlast = st2[31]
        tm = C.on("scalar", lambda e: e.activation(out=mean2[:, 0:512],
                                                   in_=PS2[:, 0:512], func=AF.Copy,
                                                   bias=0.0, scale=1.0 / D),
                  waits=[("tT", tlast)], inc="tS")
        tm = C.on("scalar", lambda e: e.activation(out=mean2[:, 512:1024],
                                                   in_=PS2[:, 512:1024], func=AF.Copy,
                                                   bias=0.0, scale=1.0 / D), inc="tS")
        tm = C.on("scalar", lambda e: e.activation(out=slB2[:, 0:512],
                                                   in_=PS3[:, 0:512], func=AF.Copy,
                                                   bias=0.0, scale=1.0 / D), inc="tS")
        tm = C.on("scalar", lambda e: e.activation(out=slB2[:, 512:1024],
                                                   in_=PS3[:, 512:1024], func=AF.Copy,
                                                   bias=0.0, scale=1.0 / D), inc="tS")
        mv = C.on("vector", lambda e: e.tensor_tensor(out=rstd2[:], in0=mean2[:],
                                                      in1=mean2[:], op=ALU.mult),
                  waits=[("tS", tm)], inc="tV")
        C.on("vector", lambda e: e.drain())
        mv = C.on("vector", lambda e: e.tensor_tensor(out=slB2[:], in0=slB2[:],
                                                      in1=rstd2[:], op=ALU.subtract),
                  inc="tV")
        sv = C.on("scalar", lambda e: e.activation(out=slB2[:], in_=slB2[:],
                                                   func=AF.Sqrt, bias=epsT[:],
                                                   scale=1.0),
                  waits=[("tV", mv)], inc="tS")
        rv = C.on("vector", lambda e: e.reciprocal(rstd2[:], slB2[:]),
                  waits=[("tS", sv)], inc="tV")
        C.on("vector", lambda e: e.drain())
        fin = {}
        for dp in range(0, 16, 2):
            for dc in (dp, dp + 1):
                C.on("vector", lambda e, dc=dc: e.tensor_tensor(
                    out=acc[:, dc * HALF:(dc + 1) * HALF],
                    in0=acc[:, dc * HALF:(dc + 1) * HALF], in1=mean2[:],
                    op=ALU.subtract), waits=[("tV", rv)])
            for dc in (dp, dp + 1):
                fin[dc] = C.on("vector", lambda e, dc=dc: e.tensor_tensor(
                    out=acc[:, dc * HALF:(dc + 1) * HALF],
                    in0=acc[:, dc * HALF:(dc + 1) * HALF], in1=rstd2[:],
                    op=ALU.mult), inc="tV")
        C.barrier()
        ysl = {}
        prev = None
        for sb in range(8):
            slo = ("dsE", "dsF")[sb % 2]
            for dc in range(16):
                t_ps = (PS0, PS1)[dc % 2][:, 0:128]
                w = []
                if sb >= 2 and dc == 0:
                    w.append((slo, ysl[sb - 2]))
                tt = C.on("tensor", lambda e, sb=sb, dc=dc, t_ps=t_ps: e.transpose(
                    t_ps, acc[:, dc * HALF + sb * 128:dc * HALF + (sb + 1) * 128],
                    id_f[:]), waits=w + ([("tV", prev)] if prev else []), inc="tT")
                prev = C.on("vector", lambda e, sb=sb, dc=dc, t_ps=t_ps: e.tensor_copy(
                    out=ostg[sb % 2][:, dc * 128:(dc + 1) * 128], in_=t_ps),
                    waits=[("tT", tt)], inc="tV")
            ysl[sb] = C.on("sync", lambda e, sb=sb, tok0=tok0: e.dma_start(
                out=y_d[tok0 + sb * 128:tok0 + (sb + 1) * 128, :],
                in_=ostg[sb % 2][:]), waits=[("tV", prev)], inc=slo, inck=16)
        C.barrier()

    de, df = C.cnt["dsE"], C.cnt["dsF"]

    def fin_sync(e):
        e.wait_ge(C.sems["dsE"], de)
        e.wait_ge(C.sems["dsF"], df)
    C.flush()
    blk.sync(fin_sync)

    blk_cm.__exit__(None, None, None)
    return nc


_NC_CACHE = []


def kernel(L0_attn_x, L0_attn_Wq, L0_attn_Wk, L0_attn_Wv, L0_attn_Wo,
           L0_ffn_gate_w, L0_ffn_wg, L0_ffn_wu, L0_ffn_wd,
           L0_ffn_sg, L0_ffn_su, L0_ffn_sd):
    if not _NC_CACHE:
        _NC_CACHE.append(build_program())
    nc = _NC_CACHE[0]
    x = np.ascontiguousarray(np.asarray(L0_attn_x, dtype=np.float32))
    shared = {
        "wq": np.ascontiguousarray(np.asarray(L0_attn_Wq, dtype=np.float32)),
        "wk": np.ascontiguousarray(np.asarray(L0_attn_Wk, dtype=np.float32)),
        "wv": np.ascontiguousarray(np.asarray(L0_attn_Wv, dtype=np.float32)),
        "wo": np.ascontiguousarray(np.asarray(L0_attn_Wo, dtype=np.float32)),
        "gw": np.ascontiguousarray(np.asarray(L0_ffn_gate_w, dtype=np.float32)),
        "wg": np.ascontiguousarray(np.asarray(L0_ffn_wg, dtype=np.float32)),
        "wu": np.ascontiguousarray(np.asarray(L0_ffn_wu, dtype=np.float32)),
        "wd": np.ascontiguousarray(np.asarray(L0_ffn_wd, dtype=np.float32)),
        "sg": np.ascontiguousarray(np.asarray(L0_ffn_sg, dtype=np.float32)),
        "su": np.ascontiguousarray(np.asarray(L0_ffn_su, dtype=np.float32)),
        "sd": np.ascontiguousarray(np.asarray(L0_ffn_sd, dtype=np.float32)),
    }
    in_maps = [dict(shared, x=np.ascontiguousarray(x[b])) for b in range(8)]
    res = run_bass_kernel_spmd(nc, in_maps, core_ids=list(range(8)))
    out = np.stack([np.asarray(r["y"]) for r in res.results], axis=0)
    return out.astype(np.float32)


if __name__ == "__main__":
    build_program()
    print("build ok")


# version 12 (final)
# speedup vs baseline: 1.0703x; 1.0703x over previous
import math
import numpy as np
import concourse.bass as bass
import concourse.mybir as mybir
from concourse.bass_utils import run_bass_kernel_spmd

F32 = mybir.dt.float32
BF16 = mybir.dt.bfloat16
AF = mybir.ActivationFunctionType
ALU = mybir.AluOpType
AX = mybir.AxisListType

S = 2048
D = 2048
HD = 64
NH = 32
NKV = 8
FE = 768
NE = 32
FSH = 1536
QB = 512
HALF = 1024
EPS = 1e-5
SCALE = HD ** -0.5
MAGIC = 12582912.0

_B0 = [h for h in range(NH) if (h >> 2) % 2 == 0]
_B64 = [h for h in range(NH) if (h >> 2) % 2 == 1]
Q_TILE = {}
Q_BASE = {}
for _i, _h in enumerate(_B0):
    Q_TILE[_h], Q_BASE[_h] = _i, 0
for _i, _h in enumerate(_B64):
    Q_TILE[_h], Q_BASE[_h] = _i, 64

ENGINES = ("sync", "gpsimd", "tensor", "vector", "scalar")


class Sched:

    def __init__(self, nc, blk, semnames):
        self.nc = nc
        self.blk = blk
        self.sems = {}
        self.cnt = {}
        self._cms = []
        for n in semnames:
            cm = nc.semaphore(n)
            self.sems[n] = cm.__enter__()
            self._cms.append(cm)
            self.cnt[n] = 0
        self.ev = {e: [] for e in ENGINES}

    def on(self, eng, fn, waits=(), inc=None, inck=1):
        w = [(s, v) for (s, v) in waits if v > 0]
        if inc is not None:
            self.cnt[inc] += inck
        self.ev[eng].append((w, fn, inc, inck))
        return self.cnt[inc] if inc is not None else None

    def flush(self):
        for eng in ENGINES:
            evs = self.ev[eng]
            if not evs:
                continue

            def section(e, evs=evs):
                for (w, fn, inc, inck) in evs:
                    for (s, v) in w:
                        e.wait_ge(self.sems[s], v)
                    ins = fn(e)
                    if inc is not None:
                        ins.then_inc(self.sems[inc], inck)
            getattr(self.blk, eng)(section)
            self.ev[eng] = []

    def barrier(self):
        self.flush()
        snap = {s: v for s, v in self.cnt.items() if v > 0}
        for eng in ENGINES:
            def f(e, snap=snap):
                for s, v in snap.items():
                    e.wait_ge(self.sems[s], v)
            getattr(self.blk, eng)(f)


def build_program(debug=False):
    nc = bass.Bass("TRN2", target_bir_lowering=False, debug=False)

    x_d = nc.dram_tensor("x", [S, D], F32, kind="ExternalInput").ap()
    wq_d = nc.dram_tensor("wq", [D, D], F32, kind="ExternalInput").ap()
    wk_d = nc.dram_tensor("wk", [NKV * HD, D], F32, kind="ExternalInput").ap()
    wv_d = nc.dram_tensor("wv", [NKV * HD, D], F32, kind="ExternalInput").ap()
    wo_d = nc.dram_tensor("wo", [D, D], F32, kind="ExternalInput").ap()
    gw_d = nc.dram_tensor("gw", [D, NE], F32, kind="ExternalInput").ap()
    wg_d = nc.dram_tensor("wg", [NE, D, FE], F32, kind="ExternalInput").ap()
    wu_d = nc.dram_tensor("wu", [NE, D, FE], F32, kind="ExternalInput").ap()
    wd_d = nc.dram_tensor("wd", [NE, FE, D], F32, kind="ExternalInput").ap()
    sg_d = nc.dram_tensor("sg", [D, FSH], F32, kind="ExternalInput").ap()
    su_d = nc.dram_tensor("su", [D, FSH], F32, kind="ExternalInput").ap()
    sd_d = nc.dram_tensor("sd", [FSH, D], F32, kind="ExternalInput").ap()
    y_d = nc.dram_tensor("y", [S, D], F32, kind="ExternalOutput").ap()

    xbf_scr = nc.dram_tensor("xbf_scr", [16, 128, S], BF16, kind="Internal").ap()
    _sk = "ExternalOutput" if debug else "Internal"
    xT_scr = nc.dram_tensor("xT_scr", [16, 128, S], F32, kind=_sk).ap()
    x1b_scr = nc.dram_tensor("x1b_scr", [16, 128, S], BF16, kind="Internal").ap()
    x1f_scr = nc.dram_tensor("x1f_scr", [16, 128, S], F32, kind=_sk).ap()
    woT_scr = nc.dram_tensor("woT_scr", [16, 128, D], BF16, kind="Internal").ap()

    PS0 = nc.alloc_psum_tensor("PS0", [128, 1024], F32)
    PS1 = nc.alloc_psum_tensor("PS1", [128, 1024], F32)
    PS2 = nc.alloc_psum_tensor("PS2", [128, 1024], F32)
    PS3 = nc.alloc_psum_tensor("PS3", [128, 1024], F32)

    FLOOR = 22528
    CEIL = 229376

    def at(name, shape, dtype, off):
        nbytes = shape[-1] * {F32: 4, BF16: 2}[dtype]
        assert off >= FLOOR and off + nbytes <= CEIL, (name, off, nbytes)
        return nc.alloc_sbuf_tensor_at(name, shape, dtype, offset=off)

    ones_f = at("ones_f", [128, 128], F32, 28672)
    id_f = at("id_f", [128, 128], F32, 29184)
    onesK1 = at("onesK1", [1, 128], F32, 29696)
    i128 = at("i128", [128, 1], F32, 30208)
    nl2pi = at("nl2pi", [128, 1], F32, 30240)
    invf = at("invf", [128, 1], F32, 30272)
    epsT = at("epsT", [128, 1], F32, 30304)
    biasB = at("biasB", [128, NH], F32, 30336)
    kTr = at("kTr", [128, 4 * S], BF16, 30720)
    v_on = at("v_on", [128, 16 * 520], BF16, 47104)
    qTr = at("qTr", [128, 16 * S], BF16, 63744)
    cos_t = at("cos_t", [128, S], F32, 129280)
    sin_p = at("sin_p", [128, S], F32, 137472)
    sin_m = at("sin_m", [128, S], F32, 145664)
    sbc = at("sbc", [128, S], F32, 153856)
    tmpA = at("tmpA", [128, S], F32, 162048)
    tmpB = at("tmpB", [128, S], F32, 170240)
    srow = at("srow", [1, S], F32, 178432)
    xstage = [at(f"xstage{i}", [128, D], F32, 153856 + i * 8192) for i in range(2)]
    xtst = [at(f"xtst{i}", [128, 128], F32, 170240 + i * 512) for i in range(4)]
    xbblk = [at(f"xbblk{i}", [128, 128], BF16, 172288 + i * 256) for i in range(4)]
    rtmp = at("rtmp", [128, QB], F32, 153856)
    rtmp2 = at("rtmp2", [128, QB], F32, 155904)
    xq = [at(f"xq{i}", [128, 16 * QB], BF16, (157952, 174336)[i]) for i in range(2)]
    wstage = [at(f"wstage{i}", [128, D], F32, 174336 + i * 8192) for i in range(2)]
    wkT = at("wkT", [128, 64 * 128], BF16, 190720)
    wvT = at("wvT", [128, 64 * 128], BF16, 207104)
    wqT = at("wqT", [128, 128 * 128], BF16, 190720)
    nsq = at("nsq", [128, S], F32, 153856)
    nred = at("nred", [1, 96], F32, 162048)
    nsc = at("nsc", [1, 2], F32, 162560)
    wostg = [at(f"wostg{i}", [128, D], BF16, 153856 + i * 4096) for i in range(2)]
    PT = at("PT", [128, 16 * QB], BF16, 129280)
    avT = at("avT", [128, 16 * QB], BF16, 145664)
    zb = at("zb", [128, 16 * QB], F32, 162048)
    wot = [at(f"wot{i}", [128, D], BF16, 194816 + i * 4096) for i in range(2)]
    x0t = [at(f"x0t{i}", [128, QB], F32, 203008 + i * 2048) for i in range(2)]
    meanT = at("meanT", [128, QB], F32, 207104)
    ex2T = at("ex2T", [128, QB], F32, 209152)
    slotB = at("slotB", [128, QB], F32, 211200)
    rstdT = at("rstdT", [128, QB], F32, 213248)
    zsq = [at(f"zsq{i}", [128, QB], F32, 215296 + i * 2048) for i in range(2)]
    x1fst = [at(f"x1fst{i}", [128, QB], F32, 219392 + i * 2048) for i in range(2)]
    x1bst = [at(f"x1bst{i}", [128, QB], BF16, 223488 + i * 1024) for i in range(2)]
    recT = at("recT", [1, QB], F32, 225536)
    recT2 = at("recT2", [1, QB], F32, 24576)
    rbs2 = at("rbs2", [64, QB], F32, 22528)
    rbs = at("rbs", [64, QB], F32, 26624)
    acc = at("acc", [128, 16 * HALF], F32, 30720)
    x1bf = at("x1bf", [128, 16 * HALF], BF16, 96256)
    ring = [at(f"ring{i}", [128, 16 * FE], BF16, 129280 + i * 24576) for i in range(2)]
    ucb = at("ucb", [128, 6 * HALF], BF16, 178432)
    tsil = [at(f"tsil{i}", [128, QB], F32, 190720 + i * 2048) for i in range(2)]
    cbT = at("cbT", [128, HALF], F32, 194816)
    cmbT = at("cmbT", [32, S], F32, 198912)
    x1ft = [at(f"x1ft{i}", [128, HALF], F32, 207104 + i * 4096) for i in range(2)]
    mean2 = at("mean2", [128, HALF], F32, 215296)
    cbrow = at("cbrow", [1, HALF], F32, 219392 - 4096)
    slB2 = at("slB2", [128, HALF], F32, 219392)
    rstd2 = at("rstd2", [128, HALF], F32, 223488)
    gwbf = at("gwbf", [128, 16 * NE], BF16, 129280)
    scT = at("scT", [32, S], F32, 130304)
    scN = at("scN", [128, 16 * NE], F32, 138496)
    m8T = at("m8T", [128, 8], F32, 140544)
    wnT = at("wnT", [128, 8], F32, 141056)
    mskT = at("mskT", [128, NE], F32, 141568)
    g1T = at("g1T", [128, 1], F32, 141696)
    gstg = [at(f"gstg{i}", [128, 16 * QB], BF16, (141824, 160256)[i]) for i in range(2)]
    cmbN = at("cmbN", [128, 16 * NE], F32, 158208)
    ostg = [at(f"ostg{i}", [128, D], F32, (178432, 190720)[i]) for i in range(2)]

    sems = ["dsA", "dsB", "dsC", "dsD", "dsE", "dsF", "dgA", "dgB", "dW", "dOut",
            "tT", "tS", "tV", "tG",
            "dsW0", "dsW1", "dsW2", "dsW3", "dsX0", "dsX1", "dsX2", "dsX3"]
    blk_cm = nc.Block()
    blk = blk_cm.__enter__()
    C = Sched(nc, blk, sems)

    C.on("gpsimd", lambda e: e.iota(i128[0:32, :], pattern=[[0, 1]], base=0,
                                    channel_multiplier=1,
                                    allow_small_or_imprecise_dtypes=True), inc="tG")
    C.on("gpsimd", lambda e: e.iota(srow[:], pattern=[[1, S]], base=0,
                                    channel_multiplier=0,
                                    allow_small_or_imprecise_dtypes=True), inc="tG")
    og = C.on("gpsimd", lambda e: e.memset(ones_f[:], 1.0), inc="tG")
    C.on("gpsimd", lambda e: e.memset(onesK1[:], 1.0), inc="tG")
    C.on("gpsimd", lambda e: e.memset(nl2pi[:], -math.log(2 * math.pi)), inc="tG")
    C.on("gpsimd", lambda e: e.memset(epsT[:], EPS), inc="tG")
    tg = C.on("gpsimd", lambda e: e.affine_select(id_f[:], ones_f[:],
                                                  pattern=[[1, 128]],
                                                  compare_op=ALU.is_equal, fill=0.0,
                                                  base=0, channel_multiplier=-1),
              waits=[("tG", og)], inc="tG")
    dv = C.on("sync", lambda e: e.dma_start(out=i128[32:64, :], in_=i128[0:32, :]),
              waits=[("tG", tg)], inc="dW", inck=16)
    dv = C.on("sync", lambda e: e.dma_start(out=i128[64:128, :], in_=i128[0:64, :]),
              waits=[("dW", dv)], inc="dW", inck=16)
    dv = C.on("sync", lambda e: e.dma_start(out=sbc[0:1, :], in_=srow[:]),
              waits=[("dW", dv)], inc="dW", inck=16)
    p = 1
    while p < 128:
        dv = C.on("sync", lambda e, p=p: e.dma_start(out=sbc[p:2 * p, :],
                                                     in_=sbc[0:p, :]),
                  waits=[("dW", dv)], inc="dW", inck=16)
        p *= 2
    dw0 = C.cnt["dW"]
    ts0 = C.on("scalar", lambda e: e.activation(out=invf[:], in_=i128[:], func=AF.Exp,
                                                bias=nl2pi[:],
                                                scale=-math.log(10000.0) / 32.0),
               waits=[("dW", dw0)], inc="tS")
    C.on("vector", lambda e: e.tensor_scalar(out=sbc[:], in0=sbc[:], scalar1=invf[:],
                                             scalar2=None, op0=ALU.mult),
         waits=[("tS", ts0), ("dW", dw0)])
    C.on("vector", lambda e: e.drain())
    C.on("vector", lambda e: e.tensor_scalar(out=tmpA[:], in0=sbc[:], scalar1=MAGIC,
                                             scalar2=-MAGIC, op0=ALU.add, op1=ALU.add))
    C.on("vector", lambda e: e.drain())
    C.on("vector", lambda e: e.tensor_tensor(out=tmpA[:], in0=sbc[:], in1=tmpA[:],
                                             op=ALU.subtract))
    C.on("vector", lambda e: e.tensor_scalar(out=tmpB[:], in0=sbc[:], scalar1=0.25,
                                             scalar2=None, op0=ALU.add))
    C.on("vector", lambda e: e.drain())
    C.on("vector", lambda e: e.tensor_scalar(out=sbc[:], in0=tmpB[:], scalar1=MAGIC,
                                             scalar2=-MAGIC, op0=ALU.add, op1=ALU.add))
    C.on("vector", lambda e: e.drain())
    tv0 = C.on("vector", lambda e: e.tensor_tensor(out=tmpB[:], in0=tmpB[:], in1=sbc[:],
                                                   op=ALU.subtract), inc="tV")
    C.on("scalar", lambda e: e.activation(out=sin_p[:], in_=tmpA[:], func=AF.Sin,
                                          bias=0.0, scale=2 * math.pi),
         waits=[("tV", tv0)])
    ts1 = C.on("scalar", lambda e: e.activation(out=cos_t[:], in_=tmpB[:], func=AF.Sin,
                                                bias=0.0, scale=2 * math.pi), inc="tS")
    C.on("vector", lambda e: e.tensor_scalar(out=sin_m[:], in0=sin_p[:], scalar1=-1.0,
                                             scalar2=None, op0=ALU.mult),
         waits=[("tS", ts1)])
    C.barrier()

    ldv = {}
    evv = {}
    evs = {}
    wrv = {}
    wbv = {}
    bi = 0
    for st in range(16):
        sl = ("dsA", "dsB")[st % 2]
        w = [("tT", ldv[st - 2])] if st >= 2 else []
        dv = C.on("sync", lambda e, st=st, sl=sl: e.dma_start(
            out=xstage[st % 2][:], in_=x_d[st * 128:(st + 1) * 128, :]),
            waits=w, inc=sl, inck=16)
        for dt in range(16):
            ps = (PS0, PS1)[bi % 2][:, 0:128]
            w = [(sl, dv)] if dt == 0 else []
            if bi >= 2:
                w.append(("tV", evv[bi - 2]))
            ldv[st] = C.on("tensor", lambda e, st=st, dt=dt, ps=ps: e.transpose(
                ps, xstage[st % 2][:, dt * 128:(dt + 1) * 128], id_f[:]),
                waits=w, inc="tT")
            w = [("tT", ldv[st])]
            if bi >= 4:
                w.append((f"dsW{bi % 4}", wrv[bi - 4]))
                w.append((f"dsX{bi % 4}", wbv[bi - 4]))
            C.on("vector", lambda e, bi=bi, ps=ps: e.tensor_copy(
                out=xtst[bi % 4][:], in_=ps), waits=w)
            evv[bi] = C.on("vector", lambda e, bi=bi, ps=ps: e.tensor_copy(
                out=xbblk[bi % 4][:], in_=ps), inc="tV")
            wrv[bi] = C.on("sync", lambda e, st=st, dt=dt, bi=bi: e.dma_start(
                out=xT_scr[dt][:, st * 128:(st + 1) * 128], in_=xtst[bi % 4][:]),
                waits=[("tV", evv[bi])], inc=f"dsW{bi % 4}", inck=16)
            wbv[bi] = C.on("scalar", lambda e, st=st, dt=dt, bi=bi: e.dma_start(
                out=xbf_scr[dt][:, st * 128:(st + 1) * 128], in_=xbblk[bi % 4][:]),
                waits=[("tV", evv[bi])], inc=f"dsX{bi % 4}", inck=16)
            bi += 1
    C.barrier()

    ldv = {}
    evv = {}
    bi = 0
    for r in range(8):
        sl = ("dsA", "dsB")[r % 2]
        w = [("tT", ldv[r - 2])] if r >= 2 else []
        src, rr = (wk_d, r) if r < 4 else (wv_d, r - 4)
        dv = C.on("sync", lambda e, src=src, rr=rr, r=r: e.dma_start(
            out=wstage[r % 2][:], in_=src[rr * 128:(rr + 1) * 128, :]),
            waits=w, inc=sl, inck=16)
        for dt in range(16):
            ps = (PS0, PS1)[bi % 2][:, 0:128]
            w = [(sl, dv)] if dt == 0 else []
            if bi >= 2:
                w.append(("tV", evv[bi - 2]))
            ldv[r] = C.on("tensor", lambda e, r=r, dt=dt, ps=ps: e.transpose(
                ps, wstage[r % 2][:, dt * 128:(dt + 1) * 128], id_f[:]),
                waits=w, inc="tT")
            if r < 4:
                dst = wkT[:, (r * 16 + dt) * 128:(r * 16 + dt + 1) * 128]
            else:
                dst = wvT[:, (dt * 4 + (r - 4)) * 128:(dt * 4 + (r - 4) + 1) * 128]
            evv[bi] = C.on("vector", lambda e, dst=dst, ps=ps: e.tensor_copy(
                out=dst, in_=ps), waits=[("tT", ldv[r])], inc="tV")
            bi += 1
    C.barrier()

    def rope_emit(eng_list, ps, dsts, sc):
        cs = slice(sc * QB, (sc + 1) * QB)
        C.on("vector", lambda e: e.tensor_tensor(out=rtmp[:], in0=ps,
                                                 in1=cos_t[:, cs], op=ALU.mult),
             waits=list(eng_list) + [("tV", C.cnt["tV"])])
        for b in (0, 64):
            lo, hi = b, b + 32
            C.on("vector", lambda e, lo=lo, hi=hi: e.tensor_tensor(
                out=rtmp2[lo:lo + 32, :], in0=ps[hi:hi + 32, :],
                in1=sin_m[hi:hi + 32, cs], op=ALU.mult))
            C.on("vector", lambda e, lo=lo, hi=hi: e.tensor_tensor(
                out=rtmp2[hi:hi + 32, :], in0=ps[lo:lo + 32, :],
                in1=sin_p[lo:lo + 32, cs], op=ALU.mult))
        C.on("vector", lambda e: e.drain())
        if isinstance(dsts, tuple):
            dA, dB = dsts
            C.on("vector", lambda e: e.tensor_tensor(out=dA, in0=rtmp[0:64, :],
                                                     in1=rtmp2[0:64, :], op=ALU.add))
            return C.on("vector", lambda e: e.tensor_tensor(
                out=dB, in0=rtmp[64:128, :], in1=rtmp2[64:128, :], op=ALU.add),
                inc="tV")
        return C.on("vector", lambda e: e.tensor_tensor(out=dsts, in0=rtmp[:],
                                                        in1=rtmp2[:], op=ALU.add),
                    inc="tV")

    evv = {}
    ci = 0
    ldq = {}
    for sc in range(4):
        sl = ("dsA", "dsB")[sc % 2]
        xqb = xq[sc % 2]
        for dt in range(16):
            w = []
            if sc >= 2 and dt == 0:
                w.append(("tT", ldq[sc - 2]))
            dv = C.on("sync", lambda e, dt=dt, sc=sc, xqb=xqb: e.dma_start(
                out=xqb[:, dt * QB:(dt + 1) * QB],
                in_=xbf_scr[dt][:, sc * QB:(sc + 1) * QB]),
                waits=w, inc=sl, inck=16)
        for ec in range(4):
            ps = (PS0, PS1)[ci % 2][:, 0:512]
            w = [(sl, dv)] if ec == 0 else []
            if ci >= 2:
                w.append(("tV", evv[ci - 2]))
            for dt in range(16):
                tv = C.on("tensor", lambda e, ec=ec, dt=dt, ps=ps, xqb=xqb: e.matmul(
                    ps, wkT[:, (ec * 16 + dt) * 128:(ec * 16 + dt + 1) * 128],
                    xqb[:, dt * QB:(dt + 1) * QB],
                    start=(dt == 0), stop=(dt == 15)),
                    waits=w if dt == 0 else [], inc="tT" if dt == 15 else None)
            ldq[sc] = tv
            dst = kTr[:, ec * S + sc * QB: ec * S + (sc + 1) * QB]
            evv[ci] = rope_emit([("tT", tv)], ps, dst, sc)
            ci += 1
        for lq in range(4):
            lc = sc * 4 + lq
            ps = (PS0, PS1)[ci % 2][:, 0:512]
            w = []
            if ci >= 2:
                w.append(("tV", evv[ci - 2]))
            for dt in range(16):
                tv = C.on("tensor", lambda e, lc=lc, lq=lq, dt=dt, ps=ps, xqb=xqb:
                          e.matmul(
                    ps, xqb[:, dt * QB + lq * 128: dt * QB + (lq + 1) * 128],
                    wvT[:, dt * 512:(dt + 1) * 512],
                    start=(dt == 0), stop=(dt == 15)),
                    waits=w if dt == 0 else [], inc="tT" if dt == 15 else None)
            ldq[sc] = tv
            vv = v_on.ap().rearrange("p (l h c) -> p l h c", l=16, c=65)
            C.on("vector", lambda e, lc=lc, ps=ps: e.tensor_copy(
                out=vv[:, lc, :, 0:64],
                in_=ps.rearrange("p (h c) -> p h c", c=64)),
                waits=[("tT", tv)])
            evv[ci] = C.on("vector", lambda e, lc=lc: e.memset(
                vv[:, lc, :, 64:65], 1.0), inc="tV")
            ci += 1
    C.barrier()

    for hec in range(2):
        ldv = {}
        evv2 = {}
        bi = 0
        for r in range(8):
            sl = ("dsA", "dsB")[r % 2]
            w = [("tT", ldv[r - 2])] if r >= 2 else []
            rr = hec * 8 + r
            dv = C.on("sync", lambda e, rr=rr, r=r: e.dma_start(
                out=wstage[r % 2][:], in_=wq_d[rr * 128:(rr + 1) * 128, :]),
                waits=w, inc=sl, inck=16)
            for dt in range(16):
                ps = (PS0, PS1)[bi % 2][:, 0:128]
                w = [(sl, dv)] if dt == 0 else []
                if bi >= 2:
                    w.append(("tV", evv2[bi - 2]))
                ldv[r] = C.on("tensor", lambda e, r=r, dt=dt, ps=ps: e.transpose(
                    ps, wstage[r % 2][:, dt * 128:(dt + 1) * 128], id_f[:]),
                    waits=w, inc="tT")
                dst = wqT[:, (r * 16 + dt) * 128:(r * 16 + dt + 1) * 128]
                evv2[bi] = C.on("vector", lambda e, dst=dst, ps=ps: e.tensor_copy(
                    out=dst, in_=ps), waits=[("tT", ldv[r])], inc="tV")
                bi += 1
        C.barrier()
        evv = {}
        ci = 0
        ldq = {}
        for sc in range(4):
            sl = ("dsA", "dsB")[sc % 2]
            xqb = xq[sc % 2]
            for dt in range(16):
                w = []
                if sc >= 2 and dt == 0:
                    w.append(("tT", ldq[sc - 2]))
                dv = C.on("sync", lambda e, dt=dt, sc=sc, xqb=xqb: e.dma_start(
                    out=xqb[:, dt * QB:(dt + 1) * QB],
                    in_=xbf_scr[dt][:, sc * QB:(sc + 1) * QB]),
                    waits=w, inc=sl, inck=16)
            for ec in range(8):
                ps = (PS0, PS1)[ci % 2][:, 0:512]
                w = [(sl, dv)] if ec == 0 else []
                if ci >= 2:
                    w.append(("tV", evv[ci - 2]))
                for dt in range(16):
                    tv = C.on("tensor", lambda e, ec=ec, dt=dt, ps=ps, xqb=xqb:
                              e.matmul(
                        ps, wqT[:, (ec * 16 + dt) * 128:(ec * 16 + dt + 1) * 128],
                        xqb[:, dt * QB:(dt + 1) * QB],
                        start=(dt == 0), stop=(dt == 15)),
                        waits=w if dt == 0 else [], inc="tT" if dt == 15 else None)
                ldq[sc] = tv
                ecg = hec * 8 + ec
                hA, hB = 2 * ecg, 2 * ecg + 1
                dA = qTr[Q_BASE[hA]:Q_BASE[hA] + 64,
                         Q_TILE[hA] * S + sc * QB: Q_TILE[hA] * S + (sc + 1) * QB]
                dB = qTr[Q_BASE[hB]:Q_BASE[hB] + 64,
                         Q_TILE[hB] * S + sc * QB: Q_TILE[hB] * S + (sc + 1) * QB]
                evv[ci] = rope_emit([("tT", tv)], ps, (dA, dB), sc)
                ci += 1
        C.barrier()

    tiles = [kTr[:, i * S:(i + 1) * S] for i in range(4)] + \
            [qTr[:, i * S:(i + 1) * S] for i in range(16)]
    sqv = {}
    mmv = {}
    rdv = {}
    for t, tile in enumerate(tiles):
        w = [("tT", mmv[(t - 1, 3)])] if t >= 1 else []
        sqv[t] = C.on("scalar", lambda e, tile=tile: e.square(
            nsq[:, 0:2048], tile), waits=w, inc="tS")
        for c in range(4):
            gi = t * 4 + c
            ps = PS2[0:1, (gi % 2) * 512:(gi % 2) * 512 + 512]
            w2 = [("tS", sqv[t])] if c == 0 else []
            if gi >= 2:
                w2.append(("tV", rdv[gi - 2]))
            mmv[(t, c)] = C.on("tensor", lambda e, c=c, ps=ps: e.matmul(
                ps, ones_f[:, 0:1], nsq[:, c * 512:(c + 1) * 512],
                start=True, stop=True), waits=w2, inc="tT")
            rdv[gi] = C.on("vector", lambda e, gi=gi, ps=ps: e.reduce_max(
                out=nred[0:1, gi:gi + 1], in_=ps, axis=AX.X),
                waits=[("tT", mmv[(t, c)])], inc="tV")
    C.on("vector", lambda e: e.drain())
    C.on("vector", lambda e: e.reduce_max(out=nsc[0:1, 0:1], in_=nred[0:1, 0:16],
                                          axis=AX.X))
    C.on("vector", lambda e: e.drain())
    C.on("vector", lambda e: e.reduce_max(out=nsc[0:1, 1:2], in_=nred[0:1, 16:80],
                                          axis=AX.X))
    C.on("vector", lambda e: e.drain())
    mv = C.on("vector", lambda e: e.tensor_tensor(out=nsc[0:1, 1:2],
                                                  in0=nsc[0:1, 1:2],
                                                  in1=nsc[0:1, 0:1], op=ALU.mult),
              inc="tV")
    sv = C.on("scalar", lambda e: e.activation(out=nsc[0:1, 1:2], in_=nsc[0:1, 1:2],
                                               func=AF.Sqrt, bias=0.0, scale=1.0),
              waits=[("tV", mv)], inc="tS")
    nv = C.on("vector", lambda e: e.tensor_scalar(out=nsc[0:1, 1:2],
                                                  in0=nsc[0:1, 1:2], scalar1=-SCALE,
                                                  scalar2=None, op0=ALU.mult),
              waits=[("tS", sv)], inc="tV")
    tb = C.on("tensor", lambda e: e.matmul(PS2[0:128, 512:513], onesK1[0:1, :],
                                           nsc[0:1, 1:2], start=True, stop=True),
              waits=[("tV", nv)], inc="tT")
    C.on("vector", lambda e: e.tensor_copy(out=biasB[0:128, 0:1],
                                           in_=PS2[0:128, 512:513]),
         waits=[("tT", tb)], inc="tV")
    C.on("vector", lambda e: e.drain())
    C.on("vector", lambda e: e.tensor_copy(
        out=biasB[:, 1:NH], in_=biasB[0:128, 0:1].broadcast_to([128, NH - 1])),
        inc="tV")
    C.barrier()

    ldv = {}
    evv = {}
    wsl = {}
    bi = 0
    for epc in range(16):
        sl = ("dsA", "dsB")[epc % 2]
        w = [("tT", ldv[epc - 2])] if epc >= 2 else []
        dv = C.on("sync", lambda e, epc=epc: e.dma_start(
            out=wstage[epc % 2][:], in_=wo_d[epc * 128:(epc + 1) * 128, :]),
            waits=w, inc=sl, inck=16)
        for dt in range(16):
            ps = (PS0, PS1)[bi % 2][:, 0:128]
            w = [(sl, dv)] if dt == 0 else []
            if bi >= 2:
                w.append(("tV", evv[bi - 2]))
            ldv[epc] = C.on("tensor", lambda e, epc=epc, dt=dt, ps=ps: e.transpose(
                ps, wstage[epc % 2][:, dt * 128:(dt + 1) * 128], id_f[:]),
                waits=w, inc="tT")
            w = [("tT", ldv[epc])]
            if epc >= 2 and dt == 0:
                w.append((("dsE", "dsF")[epc % 2], wsl[epc - 2]))
            hA, hB = 2 * dt, 2 * dt + 1
            C.on("vector", lambda e, epc=epc, dt=dt, ps=ps, hA=hA: e.tensor_copy(
                out=wostg[epc % 2][Q_BASE[hA]:Q_BASE[hA] + 64,
                                   Q_TILE[hA] * 128:(Q_TILE[hA] + 1) * 128],
                in_=ps[0:64, :]), waits=w)
            evv[bi] = C.on("vector", lambda e, epc=epc, dt=dt, ps=ps, hB=hB:
                           e.tensor_copy(
                out=wostg[epc % 2][Q_BASE[hB]:Q_BASE[hB] + 64,
                                   Q_TILE[hB] * 128:(Q_TILE[hB] + 1) * 128],
                in_=ps[64:128, :]), inc="tV")
            bi += 1
        wsl[epc] = C.on("sync", lambda e, epc=epc: e.dma_start(
            out=woT_scr[epc][:], in_=wostg[epc % 2][:]),
            waits=[("tV", evv[bi - 1])], inc=("dsE", "dsF")[epc % 2], inck=16)
    C.barrier()

    if debug:
        cos_d = nc.dram_tensor("cos_d", [128, S], F32, kind="ExternalOutput").ap()
        sin_d = nc.dram_tensor("sin_d", [128, S], F32, kind="ExternalOutput").ap()
        invf_d = nc.dram_tensor("invf_d", [128, 1], F32, kind="ExternalOutput").ap()
        C.on("gpsimd", lambda e: e.dma_start(out=cos_d[:], in_=cos_t[:]),
             inc="dgB", inck=16)
        C.on("gpsimd", lambda e: e.dma_start(out=sin_d[:], in_=sin_p[:]),
             inc="dgB", inck=16)
        C.on("gpsimd", lambda e: e.dma_start(out=invf_d[:], in_=invf[:]),
             inc="dgB", inck=16)
        C.barrier()

    spv, expv, avpv, recv, rbv, copyv, multv = {}, {}, {}, {}, {}, {}, {}
    oT, szT, sz2T, zV, sqS = {}, {}, {}, {}, {}
    wosl, x0sl = {}, {}
    fin_prev = {"tm": 0, "fv": 0}
    for q in range(4):
        q0 = q * QB
        bT = C.cnt["tT"]
        bS = C.cnt["tS"]
        bV = C.cnt["tV"]
        kq = q * NH
        rb_ps = PS3[0:64, 0:512]
        pairs = list(zip(_B0, _B64))
        k_of = {}
        for _p, (_a, _b) in enumerate(pairs):
            k_of[_a], k_of[_b] = kq + 2 * _p, kq + 2 * _p + 1
        stA = [PS0[:, 0:512], PS0[:, 512:1024]]
        stB = [PS1[:, 0:512], PS3[:, 512:1024]]
        avp3 = [PS1[0:65, 512:1024], PS2[0:65, 0:512], PS2[0:65, 512:1024]]

        def pt_slot(h, lc):
            base = 0 if Q_BASE[h] == 0 else 4
            return PT[:, (base + lc % 4) * 512:(base + lc % 4 + 1) * 512]

        def emit_av(h, lc, do_inc):
            kh = h // 4
            k = k_of[h]
            w = [("tS", expv[(h, lc)])]
            if lc == 0 and k >= 3:
                w.append(("tV", multv[k - 3]))
            if lc == 0 and q >= 1 and (k - kq) < 3 and k % 3 in (1, 2):
                w.append(("tS", fin_prev["tm"]))
            avpv[(h, lc)] = C.on("tensor", lambda e, h=h, lc=lc, kh=kh, k=k:
                                 e.matmul(
                avp3[k % 3],
                v_on[:, lc * 520 + kh * 65: lc * 520 + (kh + 1) * 65],
                pt_slot(h, lc),
                start=(lc == 0), stop=(lc == 15)), waits=w,
                inc="tT" if do_inc else None)

        for pp in range(17):
            for lc in range(16):
                if pp < 16:
                    for si, h in enumerate(pairs[pp]):
                        kh = h // 4
                        w = []
                        if lc >= 2:
                            w.append(("tS", expv[(h, lc - 2)]))
                        elif pp >= 1:
                            w.append(("tS", expv[(pairs[pp - 1][si], lc + 14)]))
                        elif q >= 1:
                            w.append(("tS", expv[(pairs[15][si], lc + 14)]))
                            if si == 1 and lc == 1:
                                w.append(("tT", oT[15]))
                        kap = kTr[(kh % 2) * 64:(kh % 2) * 64 + 64,
                                  (kh // 2) * S + lc * 128:(kh // 2) * S + (lc + 1) * 128]
                        qap = qTr[Q_BASE[h]:Q_BASE[h] + 64,
                                  Q_TILE[h] * S + q0:Q_TILE[h] * S + q0 + QB]
                        st = (stA, stB)[Q_BASE[h] // 64][lc % 2]
                        jv = C.on("tensor", lambda e, kap=kap, qap=qap,
                                  st=st: e.matmul(
                            st, kap, qap, start=True, stop=True),
                            waits=w, inc="tT" if si == 1 else None)
                    spv[(pairs[pp][0], lc)] = jv
                    spv[(pairs[pp][1], lc)] = jv
                if lc == 2 and pp >= 1:
                    hA, hB = pairs[pp - 1]
                    rcA, rcB = recT, recT2
                    rsA, rsB = rbs, rbs2
                    for h, rc in ((hA, rcA), (hB, rcB)):
                        k = k_of[h]
                        w = [("tT", avpv[(h, 15)])]
                        if k >= 4:
                            w.append(("tT", rbv[k - 2]))
                        recv[k] = C.on("vector", lambda e, k=k, rc=rc: e.reciprocal(
                            rc[0:1, :], avp3[k % 3][64:65, 0:512]), waits=w, inc="tV")
                    kA, kB = k_of[hA], k_of[hB]
                    rbv[kA] = C.on("tensor", lambda e, rcA=rcA: e.matmul(
                        rb_ps, onesK1[0:1, 0:64], rcA[0:1, :], start=True, stop=True),
                        waits=[("tV", recv[kA])] +
                              ([("tV", copyv[kB - 2])] if kB >= 3 else []), inc="tT")
                    copyv[kA] = C.on("vector", lambda e, rsA=rsA: e.tensor_copy(
                        out=rsA[:], in_=rb_ps),
                        waits=[("tT", rbv[kA])] +
                              ([("tV", multv[kA - 2])] if kA >= 2 else []), inc="tV")
                    rbv[kB] = C.on("tensor", lambda e, rcB=rcB: e.matmul(
                        rb_ps, onesK1[0:1, 0:64], rcB[0:1, :], start=True, stop=True),
                        waits=[("tV", recv[kB]), ("tV", copyv[kA])], inc="tT")
                    copyv[kB] = C.on("vector", lambda e, rsB=rsB: e.tensor_copy(
                        out=rsB[:], in_=rb_ps),
                        waits=[("tT", rbv[kB])] +
                              ([("tV", multv[kB - 2])] if kB >= 2 else []), inc="tV")
                    for h, rs in ((hA, rsA), (hB, rsB)):
                        k = k_of[h]
                        dst = avT[Q_BASE[h]:Q_BASE[h] + 64,
                                  Q_TILE[h] * 512:(Q_TILE[h] + 1) * 512]
                        multv[k] = C.on("vector", lambda e, k=k, dst=dst, rs=rs:
                                        e.tensor_tensor(
                            out=dst, in0=avp3[k % 3][0:64, 0:512], in1=rs[:],
                            op=ALU.mult), waits=[("tV", copyv[k])], inc="tV")
                if lc >= 2 and pp < 16:
                    emit_av(pairs[pp][0], lc - 2, False)
                    emit_av(pairs[pp][1], lc - 2, True)
                    avpv[(pairs[pp][0], lc - 2)] = avpv[(pairs[pp][1], lc - 2)]
                elif lc < 2 and pp >= 1:
                    emit_av(pairs[pp - 1][0], lc + 14, False)
                    emit_av(pairs[pp - 1][1], lc + 14, True)
                    avpv[(pairs[pp - 1][0], lc + 14)] = avpv[(pairs[pp - 1][1], lc + 14)]
                if pp < 16:
                    for h in pairs[pp]:
                        w = [("tT", spv[(h, lc)])]
                        if lc >= 4:
                            w.append(("tT", avpv[(h, lc - 4)]))
                        elif pp >= 1:
                            hp = pairs[pp - 1][Q_BASE[h] // 64]
                            w.append(("tT", avpv[(hp, lc + 12)]))
                        elif q >= 1:
                            w.append(("tT", avpv[(pairs[15][Q_BASE[h] // 64],
                                                  lc + 12)]))
                        st = (stA, stB)[Q_BASE[h] // 64][lc % 2]
                        expv[(h, lc)] = C.on("scalar", lambda e, h=h, lc=lc, st=st:
                                             e.activation(
                            out=pt_slot(h, lc), in_=st, func=AF.Exp,
                            bias=biasB[:, h:h + 1], scale=SCALE),
                            waits=w, inc="tS")
        assert C.cnt["tS"] == bS + 512
        assert C.cnt["tV"] == bV + 96
        bV96 = multv[kq + 31]
        bT2 = C.cnt["tT"]
        dsA0, dsB0 = C.cnt["dsA"], C.cnt["dsB"]
        dsC0, dsD0 = C.cnt["dsC"], C.cnt["dsD"]
        dsE0, dsF0 = C.cnt["dsE"], C.cnt["dsF"]

        def stats_pair(epc):
            w = [("tV", zV[epc])]
            if epc == 0:
                w.append(("tS", fin_prev["tm"]))
            szT[epc] = C.on("tensor", lambda e, epc=epc: e.matmul(
                PS2[:, 0:512], ones_f[:], zb[:, epc * 512:(epc + 1) * 512],
                start=(epc == 0), stop=(epc == 15)),
                waits=w, inc="tT")
            sz2T[epc] = C.on("tensor", lambda e, epc=epc: e.matmul(
                PS2[:, 512:1024], ones_f[:], zsq[epc % 2][:],
                start=(epc == 0), stop=(epc == 15)),
                waits=[("tS", sqS[epc])], inc="tT")

        for epc in range(16):
            slw = ("dsA", "dsB")[epc % 2]
            if epc >= 2:
                w = [("tT", oT[epc - 2])]
            elif q >= 1:
                w = [("tT", oT[epc + 14])]
            else:
                w = []
            wosl[epc] = C.on("sync", lambda e, epc=epc: e.dma_start(
                out=wot[epc % 2][:], in_=woT_scr[epc][:]), waits=w, inc=slw, inck=16)
            slx = ("dsC", "dsD")[epc % 2]
            if epc >= 2:
                w = [("tV", zV[epc - 2])]
            elif q >= 1:
                w = [("tV", zV[epc + 14])]
            else:
                w = []
            x0sl[epc] = C.on("sync", lambda e, epc=epc, q0=q0: e.dma_start(
                out=x0t[epc % 2][:], in_=xT_scr[epc][:, q0:q0 + QB]),
                waits=w, inc=slx, inck=16)
            op = PS3[:, (epc % 2) * 512:(epc % 2) * 512 + 512]
            w = [(slw, wosl[epc])]
            if epc == 0:
                w.append(("tV", bV96))
            if epc >= 2:
                w.append(("tV", zV[epc - 2]))
            elif q >= 1:
                w.append(("tV", zV[epc + 14]))
            for et in range(16):
                oT[epc] = C.on("tensor", lambda e, epc=epc, et=et, op=op: e.matmul(
                    op, wot[epc % 2][:, et * 128:(et + 1) * 128],
                    avT[:, et * 512:(et + 1) * 512],
                    start=(et == 0), stop=(et == 15)),
                    waits=w if et == 0 else [], inc="tT" if et == 15 else None)
            zV[epc] = C.on("vector", lambda e, epc=epc, op=op: e.tensor_tensor(
                out=zb[:, epc * 512:(epc + 1) * 512], in0=op, in1=x0t[epc % 2][:],
                op=ALU.add), waits=[("tT", oT[epc]), (slx, x0sl[epc])], inc="tV")
            w = [("tV", zV[epc])]
            if epc >= 2:
                w.append(("tT", sz2T[epc - 2]))
            elif q >= 1:
                w.append(("tT", sz2T[epc + 14]))
            sqS[epc] = C.on("scalar", lambda e, epc=epc: e.square(
                zsq[epc % 2][:], zb[:, epc * 512:(epc + 1) * 512]), waits=w, inc="tS")
            if epc >= 1:
                stats_pair(epc - 1)
        stats_pair(15)
        tm = C.on("scalar", lambda e: e.activation(out=meanT[:], in_=PS2[:, 0:512],
                                                   func=AF.Copy, bias=0.0,
                                                   scale=1.0 / D),
                  waits=[("tT", sz2T[15]), ("tV", fin_prev["fv"])], inc="tS")
        tm = C.on("scalar", lambda e: e.activation(out=ex2T[:], in_=PS2[:, 512:1024],
                                                   func=AF.Copy, bias=0.0,
                                                   scale=1.0 / D), inc="tS")
        mv = C.on("vector", lambda e: e.tensor_tensor(out=slotB[:], in0=meanT[:],
                                                      in1=meanT[:], op=ALU.mult),
                  waits=[("tS", tm)], inc="tV")
        C.on("vector", lambda e: e.drain())
        mv = C.on("vector", lambda e: e.tensor_tensor(out=slotB[:], in0=ex2T[:],
                                                      in1=slotB[:], op=ALU.subtract),
                  inc="tV")
        sv = C.on("scalar", lambda e: e.activation(out=ex2T[:], in_=slotB[:],
                                                   func=AF.Sqrt, bias=epsT[:],
                                                   scale=1.0),
                  waits=[("tV", mv)], inc="tS")
        rv = C.on("vector", lambda e: e.reciprocal(rstdT[:], ex2T[:]),
                  waits=[("tS", sv)], inc="tV")
        C.on("vector", lambda e: e.drain())
        for ep in range(0, 16, 2):
            for j in (0, 1):
                epc = ep + j
                w = [("tV", rv)]
                slt = ("dsE", "dsF")[epc % 2]
                base = (dsE0, dsF0)[epc % 2]
                if epc >= 2:
                    w.append((slt, base + 32 * (epc // 2)))
                elif q >= 1:
                    w.append((slt, base))
                C.on("vector", lambda e, epc=epc: e.tensor_tensor(
                    out=x1fst[epc % 2][:], in0=zb[:, epc * 512:(epc + 1) * 512],
                    in1=meanT[:], op=ALU.subtract), waits=w)
            for j in (0, 1):
                epc = ep + j
                fv = C.on("vector", lambda e, epc=epc: e.tensor_tensor(
                    out=x1fst[epc % 2][:], in0=x1fst[epc % 2][:], in1=rstdT[:],
                    op=ALU.mult), inc="tV")
                bv2 = C.on("scalar", lambda e, epc=epc: e.copy(
                    x1bst[epc % 2][:], x1fst[epc % 2][:]),
                    waits=[("tV", fv)], inc="tS")
                slt = ("dsE", "dsF")[epc % 2]
                C.on("sync", lambda e, epc=epc, q0=q0: e.dma_start(
                    out=x1f_scr[epc][:, q0:q0 + QB], in_=x1fst[epc % 2][:]),
                    waits=[("tV", fv)], inc=slt, inck=16)
                C.on("sync", lambda e, epc=epc, q0=q0: e.dma_start(
                    out=x1b_scr[epc][:, q0:q0 + QB], in_=x1bst[epc % 2][:]),
                    waits=[("tS", bv2)], inc=slt, inck=16)
                fin_prev["fv"] = fv
        fin_prev["tm"] = tm
    C.barrier()

    if debug:
        qTr_d = nc.dram_tensor("qTr_d", [128, 16 * S], F32, kind="ExternalOutput").ap()
        kTr_d = nc.dram_tensor("kTr_d", [128, 4 * S], F32, kind="ExternalOutput").ap()
        avT_d = nc.dram_tensor("avT_d", [128, 16 * QB], F32, kind="ExternalOutput").ap()
        bias_d = nc.dram_tensor("bias_d", [128, NH], F32, kind="ExternalOutput").ap()
        von_d = nc.dram_tensor("von_d", [128, 16 * 520], F32, kind="ExternalOutput").ap()
        for i in range(16):
            C.on("gpsimd", lambda e, i=i: e.dma_start(
                out=qTr_d[:, i * S:(i + 1) * S], in_=qTr[:, i * S:(i + 1) * S]),
                inc="dgB", inck=16)
        for i in range(4):
            C.on("gpsimd", lambda e, i=i: e.dma_start(
                out=kTr_d[:, i * S:(i + 1) * S], in_=kTr[:, i * S:(i + 1) * S]),
                inc="dgB", inck=16)
        C.on("gpsimd", lambda e: e.dma_start(out=avT_d[:], in_=avT[:]),
             inc="dgB", inck=16)
        C.on("gpsimd", lambda e: e.dma_start(out=bias_d[:], in_=biasB[:]),
             inc="dgB", inck=16)
        C.on("gpsimd", lambda e: e.dma_start(out=von_d[:], in_=v_on[:]),
             inc="dgB", inck=16)
        C.barrier()

    for dt in range(16):
        gl = C.on("gpsimd", lambda e, dt=dt: e.dma_start(
            out=gwbf[:, dt * 32:(dt + 1) * 32],
            in_=gw_d[dt * 128:(dt + 1) * 128, :]), inc="dgA", inck=16)
    gts = {}
    sgv = {}
    for sc in range(4):
        gb = gstg[sc % 2]
        for dt in range(16):
            gv = C.on("sync", lambda e, dt=dt, sc=sc, gb=gb: e.dma_start(
                out=gb[:, dt * QB:(dt + 1) * QB],
                in_=x1b_scr[dt][:, sc * QB:(sc + 1) * QB]),
                waits=[("tT", gts[sc - 2])] if (sc >= 2 and dt == 0) else [],
                inc=("dsA", "dsB")[sc % 2], inck=16)
        w = [(("dsA", "dsB")[sc % 2], gv), ("dgA", gl)]
        if sc >= 2:
            w.append(("tS", sgv[sc - 2]))
        gp_ps = (PS0, PS1)[sc % 2][0:32, 0:512]
        for dt in range(16):
            gts[sc] = C.on("tensor", lambda e, sc=sc, dt=dt, gp_ps=gp_ps, gb=gb:
                           e.matmul(
                gp_ps, gwbf[:, dt * 32:(dt + 1) * 32], gb[:, dt * QB:(dt + 1) * QB],
                start=(dt == 0), stop=(dt == 15)),
                waits=w if dt == 0 else [], inc="tT" if dt == 15 else None)
        sgv[sc] = C.on("scalar", lambda e, sc=sc, gp_ps=gp_ps: e.activation(
            out=scT[:, sc * 512:(sc + 1) * 512], in_=gp_ps, func=AF.Sigmoid,
            bias=0.0, scale=1.0), waits=[("tT", gts[sc])], inc="tS")
    C.barrier()
    prev_cp = 0
    for tc in range(16):
        t_ps = PS3[:, (tc % 2) * 512:(tc % 2) * 512 + 32]
        tt = C.on("tensor", lambda e, tc=tc, t_ps=t_ps: e.transpose(
            t_ps, scT[:, tc * 128:(tc + 1) * 128], id_f[0:32, 0:32]),
            waits=[("tV", prev_cp)] if tc >= 2 else [], inc="tT")
        C.on("vector", lambda e, tc=tc, t_ps=t_ps: e.tensor_copy(
            out=scN[:, tc * 32:(tc + 1) * 32], in_=t_ps), waits=[("tT", tt)])
        C.on("vector", lambda e: e.drain())
        C.on("vector", lambda e, tc=tc: e.max(out=m8T[:],
                                              in_=scN[:, tc * 32:(tc + 1) * 32]))
        C.on("vector", lambda e: e.drain())
        C.on("vector", lambda e: e.reduce_sum(out=g1T[:], in_=m8T[:, 0:6], axis=AX.X))
        C.on("vector", lambda e: e.drain())
        C.on("vector", lambda e: e.tensor_scalar(out=g1T[:], in0=g1T[:], scalar1=1e-20,
                                                 scalar2=None, op0=ALU.add))
        C.on("vector", lambda e: e.drain())
        C.on("vector", lambda e: e.reciprocal(g1T[:], g1T[:]))
        C.on("vector", lambda e: e.drain())
        C.on("vector", lambda e: e.tensor_scalar(out=wnT[:, 0:6], in0=m8T[:, 0:6],
                                                 scalar1=g1T[:], scalar2=None,
                                                 op0=ALU.mult))
        C.on("vector", lambda e, tc=tc: e.memset(cmbN[:, tc * 32:(tc + 1) * 32], 0.0))
        C.on("vector", lambda e: e.drain())
        for j in range(6):
            C.on("vector", lambda e, tc=tc, j=j: e.tensor_scalar(
                out=mskT[:], in0=scN[:, tc * 32:(tc + 1) * 32],
                scalar1=m8T[:, j:j + 1], scalar2=None, op0=ALU.is_equal))
            C.on("vector", lambda e: e.drain())
            cpv = C.on("vector", lambda e, tc=tc, j=j: e.scalar_tensor_tensor(
                out=cmbN[:, tc * 32:(tc + 1) * 32], in0=mskT[:],
                scalar=wnT[:, j:j + 1], in1=cmbN[:, tc * 32:(tc + 1) * 32],
                op0=ALU.mult, op1=ALU.add), inc="tV" if j == 5 else None)
            C.on("vector", lambda e: e.drain())
        t2_ps = PS3[0:32, (tc % 2) * 512 + 64:(tc % 2) * 512 + 192]
        tb = C.on("tensor", lambda e, tc=tc, t2_ps=t2_ps: e.transpose(
            t2_ps, cmbN[:, tc * 32:(tc + 1) * 32], id_f[:]),
            waits=[("tV", cpv)], inc="tT")
        prev_cp = C.on("vector", lambda e, tc=tc, t2_ps=t2_ps: e.tensor_copy(
            out=cmbT[:, tc * 128:(tc + 1) * 128], in_=t2_ps),
            waits=[("tT", tb)], inc="tV")
    C.barrier()

    def uw_src(ei, dt):
        if ei == 0:
            return su_d[dt * 128:(dt + 1) * 128, 0:FE]
        if ei == 1:
            return su_d[dt * 128:(dt + 1) * 128, FE:2 * FE]
        return wu_d[ei - 2][dt * 128:(dt + 1) * 128, :]

    def gw_src(ei, dt):
        if ei == 0:
            return sg_d[dt * 128:(dt + 1) * 128, 0:FE]
        if ei == 1:
            return sg_d[dt * 128:(dt + 1) * 128, FE:2 * FE]
        return wg_d[ei - 2][dt * 128:(dt + 1) * 128, :]

    def dw_src(ei, ft):
        if ei == 0:
            return sd_d[ft * 128:(ft + 1) * 128, :]
        if ei == 1:
            return sd_d[FSH // 2 + ft * 128:FSH // 2 + (ft + 1) * 128, :]
        return wd_d[ei - 2][ft * 128:(ft + 1) * 128, :]

    consumed = {}
    ps2_user = {0: 0, 1: 0}
    prev_cbmm = 0
    for h2 in range(2):
        tok0 = h2 * HALF
        for dt in range(16):
            C.on("sync", lambda e, dt=dt, tok0=tok0: e.dma_start(
                out=x1bf[:, dt * HALF:(dt + 1) * HALF],
                in_=x1b_scr[dt][:, tok0:tok0 + HALF]), inc="dW", inck=16)
        dwh = C.cnt["dW"]
        u_free = {}
        g_free = {}
        r_free = {}
        tsil_free = {}
        act_v = {}
        x0 = C.cnt["dsC"]
        x1 = C.cnt["dsD"]
        x1sl = {}
        for ei in range(34):
            mb = (h2 * 34 + ei) * 3
            real = ei >= 2
            exp_i = ei - 2
            if real:
                wrow = [("tT", prev_cbmm)] if prev_cbmm else []
                rowv = C.on("sync", lambda e, exp_i=exp_i, tok0=tok0: e.dma_start(
                    out=cbrow[:], in_=cmbT[exp_i:exp_i + 1, tok0:tok0 + HALF]),
                    waits=wrow, inc="dsC", inck=16)
                for sc in range(2):
                    w = [("tV", ps2_user[sc]), ("dsC", rowv)]
                    cb_ps = PS2[:, sc * 512:(sc + 1) * 512]
                    tb = C.on("tensor", lambda e, sc=sc, cb_ps=cb_ps: e.matmul(
                        cb_ps, onesK1[0:1, :], cbrow[0:1, sc * 512:(sc + 1) * 512],
                        start=True, stop=True), waits=w, inc="tT")
                    prev_cbmm = tb
                    ps2_user[sc] = C.on("vector", lambda e, sc=sc, cb_ps=cb_ps:
                                        e.tensor_copy(out=cbT[:, sc * 512:(sc + 1) * 512],
                                                      in_=cb_ps),
                                        waits=[("tT", tb)], inc="tV")
            m = mb + 0
            slot = m % 2
            slx = ("dgA", "dgB")[slot]
            for dt in range(16):
                w = [("tT", consumed[m - 2])] if (m >= 2 and dt == 0) else []
                uv = C.on("gpsimd", lambda e, ei=ei, dt=dt, slot=slot: e.dma_start(
                    out=ring[slot][:, dt * FE:(dt + 1) * FE], in_=uw_src(ei, dt)),
                    waits=w, inc=slx, inck=16)
            for k in range(12):
                sc, fc = k // 6, k % 6
                u_ps = PS0[:, (k % 2) * 512:(k % 2) * 512 + 512]
                w = []
                if k == 0:
                    w = [(slx, uv), ("dW", dwh)]
                if u_free.get(k % 2):
                    w.append(("tV", u_free[k % 2]))
                for dt in range(16):
                    tv = C.on("tensor", lambda e, dt=dt, sc=sc, fc=fc, slot=slot,
                              u_ps=u_ps: e.matmul(
                        u_ps, ring[slot][:, dt * FE + fc * 128:dt * FE + (fc + 1) * 128],
                        x1bf[:, dt * HALF + sc * 512:dt * HALF + (sc + 1) * 512],
                        start=(dt == 0), stop=(dt == 15)),
                        waits=w if dt == 0 else [], inc="tT" if dt == 15 else None)
                if k == 11:
                    consumed[m] = tv
                if real:
                    u_free[k % 2] = C.on("vector", lambda e, sc=sc, fc=fc, u_ps=u_ps:
                                         e.tensor_tensor(
                        out=ucb[:, fc * HALF + sc * 512:fc * HALF + (sc + 1) * 512],
                        in0=u_ps, in1=cbT[:, sc * 512:(sc + 1) * 512], op=ALU.mult),
                        waits=[("tT", tv)], inc="tV")
                else:
                    u_free[k % 2] = C.on("vector", lambda e, sc=sc, fc=fc, u_ps=u_ps:
                                         e.tensor_copy(
                        out=ucb[:, fc * HALF + sc * 512:fc * HALF + (sc + 1) * 512],
                        in_=u_ps), waits=[("tT", tv)], inc="tV")
            m = mb + 1
            slot = m % 2
            slx = ("dgA", "dgB")[slot]
            for dt in range(16):
                w = [("tT", consumed[m - 2])] if (m >= 2 and dt == 0) else []
                gv = C.on("gpsimd", lambda e, ei=ei, dt=dt, slot=slot: e.dma_start(
                    out=ring[slot][:, dt * FE:(dt + 1) * FE], in_=gw_src(ei, dt)),
                    waits=w, inc=slx, inck=16)
            for k in range(12):
                sc, fc = k // 6, k % 6
                g_ps = PS3[:, (k % 2) * 512:(k % 2) * 512 + 512]
                w = []
                if k == 0:
                    w = [(slx, gv)]
                if g_free.get(k % 2):
                    w.append(("tS", g_free[k % 2]))
                for dt in range(16):
                    tv = C.on("tensor", lambda e, dt=dt, sc=sc, fc=fc, slot=slot,
                              g_ps=g_ps: e.matmul(
                        g_ps, ring[slot][:, dt * FE + fc * 128:dt * FE + (fc + 1) * 128],
                        x1bf[:, dt * HALF + sc * 512:dt * HALF + (sc + 1) * 512],
                        start=(dt == 0), stop=(dt == 15)),
                        waits=w if dt == 0 else [], inc="tT" if dt == 15 else None)
                if k == 11:
                    consumed[m] = tv
                w = [("tT", tv)]
                if tsil_free.get(k % 2):
                    w.append(("tV", tsil_free[k % 2]))
                g_free[k % 2] = C.on("scalar", lambda e, k=k, g_ps=g_ps: e.activation(
                    out=tsil[k % 2][:], in_=g_ps, func=AF.Silu, bias=0.0, scale=1.0),
                    waits=w, inc="tS")
                tsil_free[k % 2] = C.on("vector", lambda e, sc=sc, fc=fc, k=k:
                                        e.tensor_tensor(
                    out=ucb[:, fc * HALF + sc * 512:fc * HALF + (sc + 1) * 512],
                    in0=tsil[k % 2][:],
                    in1=ucb[:, fc * HALF + sc * 512:fc * HALF + (sc + 1) * 512],
                    op=ALU.mult), waits=[("tS", g_free[k % 2])], inc="tV")
                if k % 6 == 5:
                    act_v[sc] = tsil_free[k % 2]
            m = mb + 2
            slot = m % 2
            slx = ("dgA", "dgB")[slot]
            for ft in range(6):
                w = [("tT", consumed[m - 2])] if (m >= 2 and ft == 0) else []
                dv = C.on("gpsimd", lambda e, ei=ei, ft=ft, slot=slot: e.dma_start(
                    out=ring[slot][:, ft * D:(ft + 1) * D], in_=dw_src(ei, ft)),
                    waits=w, inc=slx, inck=16)
            for j in range(32):
                dc, sc = j // 2, j % 2
                if ei == 0 and sc == 0:
                    slf = ("dsC", "dsD")[dc % 2]
                    wx = []
                    if dc >= 2:
                        wx.append(("tV", x1sl[dc - 2]))
                    xv = C.on("sync", lambda e, dc=dc, tok0=tok0: e.dma_start(
                        out=x1ft[dc % 2][:], in_=x1f_scr[dc][:, tok0:tok0 + HALF]),
                        waits=wx, inc=slf, inck=16)
                    x1sl[dc] = ("ld", xv)
                r_ps = PS1[:, (j % 2) * 512:(j % 2) * 512 + 512]
                w = []
                if j == 0:
                    w = [(slx, dv), ("tV", act_v[0])]
                if j == 1:
                    w = [("tV", act_v[1])]
                if j >= 2 and r_free.get(j % 2):
                    w.append(("tV", r_free[j % 2]))
                for ft in range(6):
                    tv = C.on("tensor", lambda e, ft=ft, dc=dc, sc=sc, slot=slot,
                              r_ps=r_ps: e.matmul(
                        r_ps, ring[slot][:, ft * D + dc * 128:ft * D + (dc + 1) * 128],
                        ucb[:, ft * HALF + sc * 512:ft * HALF + (sc + 1) * 512],
                        start=(ft == 0), stop=(ft == 5)),
                        waits=w if ft == 0 else [], inc="tT" if ft == 5 else None)
                if j == 31:
                    consumed[m] = tv
                dst = acc[:, dc * HALF + sc * 512:dc * HALF + (sc + 1) * 512]
                if ei == 0:
                    slf = ("dsC", "dsD")[dc % 2]
                    w2 = [("tT", tv), (slf, x1sl[dc][1])]
                    r_free[j % 2] = C.on("vector", lambda e, dst=dst, dc=dc, sc=sc,
                                         r_ps=r_ps: e.tensor_tensor(
                        out=dst, in0=r_ps,
                        in1=x1ft[dc % 2][:, sc * 512:(sc + 1) * 512],
                        op=ALU.add), waits=w2, inc="tV")
                    if sc == 1:
                        x1sl[dc] = r_free[j % 2]
                else:
                    r_free[j % 2] = C.on("vector", lambda e, dst=dst, r_ps=r_ps:
                                         e.tensor_tensor(out=dst, in0=r_ps, in1=dst,
                                                         op=ALU.add),
                                         waits=[("tT", tv)], inc="tV")
        C.barrier()

        sq2 = {}
        st2 = {}
        for j in range(32):
            dc, sc = j // 2, j % 2
            w = []
            if j >= 2:
                w.append(("tT", st2[j - 2]))
            sq2[j] = C.on("scalar", lambda e, dc=dc, sc=sc, j=j: e.square(
                tsil[j % 2][:], acc[:, dc * HALF + sc * 512:dc * HALF + (sc + 1) * 512]),
                waits=w, inc="tS")
            C.on("tensor", lambda e, dc=dc, sc=sc: e.matmul(
                (PS2[:, 0:512], PS2[:, 512:1024])[sc], ones_f[:],
                acc[:, dc * HALF + sc * 512:dc * HALF + (sc + 1) * 512],
                start=(dc == 0), stop=(dc == 15)), inc="tT")
            st2[j] = C.on("tensor", lambda e, dc=dc, sc=sc, j=j: e.matmul(
                (PS3[:, 0:512], PS3[:, 512:1024])[sc], ones_f[:], tsil[j % 2][:],
                start=(dc == 0), stop=(dc == 15)),
                waits=[("tS", sq2[j])], inc="tT")
        tlast = st2[31]
        tm = C.on("scalar", lambda e: e.activation(out=mean2[:, 0:512],
                                                   in_=PS2[:, 0:512], func=AF.Copy,
                                                   bias=0.0, scale=1.0 / D),
                  waits=[("tT", tlast)], inc="tS")
        tm = C.on("scalar", lambda e: e.activation(out=mean2[:, 512:1024],
                                                   in_=PS2[:, 512:1024], func=AF.Copy,
                                                   bias=0.0, scale=1.0 / D), inc="tS")
        tm = C.on("scalar", lambda e: e.activation(out=slB2[:, 0:512],
                                                   in_=PS3[:, 0:512], func=AF.Copy,
                                                   bias=0.0, scale=1.0 / D), inc="tS")
        tm = C.on("scalar", lambda e: e.activation(out=slB2[:, 512:1024],
                                                   in_=PS3[:, 512:1024], func=AF.Copy,
                                                   bias=0.0, scale=1.0 / D), inc="tS")
        mv = C.on("vector", lambda e: e.tensor_tensor(out=rstd2[:], in0=mean2[:],
                                                      in1=mean2[:], op=ALU.mult),
                  waits=[("tS", tm)], inc="tV")
        C.on("vector", lambda e: e.drain())
        mv = C.on("vector", lambda e: e.tensor_tensor(out=slB2[:], in0=slB2[:],
                                                      in1=rstd2[:], op=ALU.subtract),
                  inc="tV")
        sv = C.on("scalar", lambda e: e.activation(out=slB2[:], in_=slB2[:],
                                                   func=AF.Sqrt, bias=epsT[:],
                                                   scale=1.0),
                  waits=[("tV", mv)], inc="tS")
        rv = C.on("vector", lambda e: e.reciprocal(rstd2[:], slB2[:]),
                  waits=[("tS", sv)], inc="tV")
        C.on("vector", lambda e: e.drain())
        fin = {}
        for dp in range(0, 16, 2):
            for dc in (dp, dp + 1):
                C.on("vector", lambda e, dc=dc: e.tensor_tensor(
                    out=acc[:, dc * HALF:(dc + 1) * HALF],
                    in0=acc[:, dc * HALF:(dc + 1) * HALF], in1=mean2[:],
                    op=ALU.subtract), waits=[("tV", rv)])
            for dc in (dp, dp + 1):
                fin[dc] = C.on("vector", lambda e, dc=dc: e.tensor_tensor(
                    out=acc[:, dc * HALF:(dc + 1) * HALF],
                    in0=acc[:, dc * HALF:(dc + 1) * HALF], in1=rstd2[:],
                    op=ALU.mult), inc="tV")
        C.barrier()
        ysl = {}
        prev = None
        for sb in range(8):
            slo = ("dsE", "dsF")[sb % 2]
            for dc in range(16):
                t_ps = (PS0, PS1)[dc % 2][:, 0:128]
                w = []
                if sb >= 2 and dc == 0:
                    w.append((slo, ysl[sb - 2]))
                tt = C.on("tensor", lambda e, sb=sb, dc=dc, t_ps=t_ps: e.transpose(
                    t_ps, acc[:, dc * HALF + sb * 128:dc * HALF + (sb + 1) * 128],
                    id_f[:]), waits=w + ([("tV", prev)] if prev else []), inc="tT")
                prev = C.on("vector", lambda e, sb=sb, dc=dc, t_ps=t_ps: e.tensor_copy(
                    out=ostg[sb % 2][:, dc * 128:(dc + 1) * 128], in_=t_ps),
                    waits=[("tT", tt)], inc="tV")
            ysl[sb] = C.on("sync", lambda e, sb=sb, tok0=tok0: e.dma_start(
                out=y_d[tok0 + sb * 128:tok0 + (sb + 1) * 128, :],
                in_=ostg[sb % 2][:]), waits=[("tV", prev)], inc=slo, inck=16)
        C.barrier()

    de, df = C.cnt["dsE"], C.cnt["dsF"]

    def fin_sync(e):
        e.wait_ge(C.sems["dsE"], de)
        e.wait_ge(C.sems["dsF"], df)
    C.flush()
    blk.sync(fin_sync)

    blk_cm.__exit__(None, None, None)
    return nc


_NC_CACHE = []


def kernel(L0_attn_x, L0_attn_Wq, L0_attn_Wk, L0_attn_Wv, L0_attn_Wo,
           L0_ffn_gate_w, L0_ffn_wg, L0_ffn_wu, L0_ffn_wd,
           L0_ffn_sg, L0_ffn_su, L0_ffn_sd):
    if not _NC_CACHE:
        _NC_CACHE.append(build_program())
    nc = _NC_CACHE[0]
    x = np.ascontiguousarray(np.asarray(L0_attn_x, dtype=np.float32))
    shared = {
        "wq": np.ascontiguousarray(np.asarray(L0_attn_Wq, dtype=np.float32)),
        "wk": np.ascontiguousarray(np.asarray(L0_attn_Wk, dtype=np.float32)),
        "wv": np.ascontiguousarray(np.asarray(L0_attn_Wv, dtype=np.float32)),
        "wo": np.ascontiguousarray(np.asarray(L0_attn_Wo, dtype=np.float32)),
        "gw": np.ascontiguousarray(np.asarray(L0_ffn_gate_w, dtype=np.float32)),
        "wg": np.ascontiguousarray(np.asarray(L0_ffn_wg, dtype=np.float32)),
        "wu": np.ascontiguousarray(np.asarray(L0_ffn_wu, dtype=np.float32)),
        "wd": np.ascontiguousarray(np.asarray(L0_ffn_wd, dtype=np.float32)),
        "sg": np.ascontiguousarray(np.asarray(L0_ffn_sg, dtype=np.float32)),
        "su": np.ascontiguousarray(np.asarray(L0_ffn_su, dtype=np.float32)),
        "sd": np.ascontiguousarray(np.asarray(L0_ffn_sd, dtype=np.float32)),
    }
    in_maps = [dict(shared, x=np.ascontiguousarray(x[b])) for b in range(8)]
    res = run_bass_kernel_spmd(nc, in_maps, core_ids=list(range(8)))
    out = np.stack([np.asarray(r["y"]) for r in res.results], axis=0)
    return out.astype(np.float32)


if __name__ == "__main__":
    build_program()
    print("build ok")


# version 15
# speedup vs baseline: 1.0737x; 1.0032x over previous
import math
import numpy as np
import concourse.bass as bass
import concourse.mybir as mybir
from concourse.bass_utils import run_bass_kernel_spmd

F32 = mybir.dt.float32
BF16 = mybir.dt.bfloat16
AF = mybir.ActivationFunctionType
ALU = mybir.AluOpType
AX = mybir.AxisListType

S = 2048
D = 2048
HD = 64
NH = 32
NKV = 8
FE = 768
NE = 32
FSH = 1536
QB = 512
HALF = 1024
EPS = 1e-5
SCALE = HD ** -0.5
MAGIC = 12582912.0

_B0 = [h for h in range(NH) if (h >> 2) % 2 == 0]
_B64 = [h for h in range(NH) if (h >> 2) % 2 == 1]
Q_TILE = {}
Q_BASE = {}
for _i, _h in enumerate(_B0):
    Q_TILE[_h], Q_BASE[_h] = _i, 0
for _i, _h in enumerate(_B64):
    Q_TILE[_h], Q_BASE[_h] = _i, 64

ENGINES = ("sync", "gpsimd", "tensor", "vector", "scalar")


class Sched:

    def __init__(self, nc, blk, semnames):
        self.nc = nc
        self.blk = blk
        self.sems = {}
        self.cnt = {}
        self._cms = []
        for n in semnames:
            cm = nc.semaphore(n)
            self.sems[n] = cm.__enter__()
            self._cms.append(cm)
            self.cnt[n] = 0
        self.ev = {e: [] for e in ENGINES}

    def on(self, eng, fn, waits=(), inc=None, inck=1):
        w = [(s, v) for (s, v) in waits if v > 0]
        if inc is not None:
            self.cnt[inc] += inck
        self.ev[eng].append((w, fn, inc, inck))
        return self.cnt[inc] if inc is not None else None

    def flush(self):
        for eng in ENGINES:
            evs = self.ev[eng]
            if not evs:
                continue

            def section(e, evs=evs):
                for (w, fn, inc, inck) in evs:
                    for (s, v) in w:
                        e.wait_ge(self.sems[s], v)
                    ins = fn(e)
                    if inc is not None:
                        ins.then_inc(self.sems[inc], inck)
            getattr(self.blk, eng)(section)
            self.ev[eng] = []

    def barrier(self):
        self.flush()
        snap = {s: v for s, v in self.cnt.items() if v > 0}
        for eng in ENGINES:
            def f(e, snap=snap):
                for s, v in snap.items():
                    e.wait_ge(self.sems[s], v)
            getattr(self.blk, eng)(f)


def build_program(debug=False):
    nc = bass.Bass("TRN2", target_bir_lowering=False, debug=False)

    x_d = nc.dram_tensor("x", [S, D], F32, kind="ExternalInput").ap()
    wq_d = nc.dram_tensor("wq", [D, D], F32, kind="ExternalInput").ap()
    wk_d = nc.dram_tensor("wk", [NKV * HD, D], F32, kind="ExternalInput").ap()
    wv_d = nc.dram_tensor("wv", [NKV * HD, D], F32, kind="ExternalInput").ap()
    wo_d = nc.dram_tensor("wo", [D, D], F32, kind="ExternalInput").ap()
    gw_d = nc.dram_tensor("gw", [D, NE], F32, kind="ExternalInput").ap()
    wg_d = nc.dram_tensor("wg", [NE, D, FE], F32, kind="ExternalInput").ap()
    wu_d = nc.dram_tensor("wu", [NE, D, FE], F32, kind="ExternalInput").ap()
    wd_d = nc.dram_tensor("wd", [NE, FE, D], F32, kind="ExternalInput").ap()
    sg_d = nc.dram_tensor("sg", [D, FSH], F32, kind="ExternalInput").ap()
    su_d = nc.dram_tensor("su", [D, FSH], F32, kind="ExternalInput").ap()
    sd_d = nc.dram_tensor("sd", [FSH, D], F32, kind="ExternalInput").ap()
    y_d = nc.dram_tensor("y", [S, D], F32, kind="ExternalOutput").ap()

    xbf_scr = nc.dram_tensor("xbf_scr", [16, 128, S], BF16, kind="Internal").ap()
    _sk = "ExternalOutput" if debug else "Internal"
    xT_scr = nc.dram_tensor("xT_scr", [16, 128, S], F32, kind=_sk).ap()
    x1b_scr = nc.dram_tensor("x1b_scr", [16, 128, S], BF16, kind="Internal").ap()
    x1f_scr = nc.dram_tensor("x1f_scr", [16, 128, S], F32, kind=_sk).ap()
    woT_scr = nc.dram_tensor("woT_scr", [16, 128, D], BF16, kind="Internal").ap()

    PS0 = nc.alloc_psum_tensor("PS0", [128, 1024], F32)
    PS1 = nc.alloc_psum_tensor("PS1", [128, 1024], F32)
    PS2 = nc.alloc_psum_tensor("PS2", [128, 1024], F32)
    PS3 = nc.alloc_psum_tensor("PS3", [128, 1024], F32)

    FLOOR = 22528
    CEIL = 229376

    def at(name, shape, dtype, off):
        nbytes = shape[-1] * {F32: 4, BF16: 2}[dtype]
        assert off >= FLOOR and off + nbytes <= CEIL, (name, off, nbytes)
        return nc.alloc_sbuf_tensor_at(name, shape, dtype, offset=off)

    ones_f = at("ones_f", [128, 128], F32, 28672)
    id_f = at("id_f", [128, 128], F32, 29184)
    onesK1 = at("onesK1", [1, 128], F32, 29696)
    i128 = at("i128", [128, 1], F32, 30208)
    nl2pi = at("nl2pi", [128, 1], F32, 30240)
    invf = at("invf", [128, 1], F32, 30272)
    epsT = at("epsT", [128, 1], F32, 30304)
    biasB = at("biasB", [128, NH], F32, 30336)
    kTr = at("kTr", [128, 4 * S], BF16, 30720)
    v_on = at("v_on", [128, 16 * 520], BF16, 47104)
    qTr = at("qTr", [128, 16 * S], BF16, 63744)
    cos_t = at("cos_t", [128, S], F32, 129280)
    sin_p = at("sin_p", [128, S], F32, 137472)
    sin_m = at("sin_m", [128, S], F32, 145664)
    sbc = at("sbc", [128, S], F32, 153856)
    tmpA = at("tmpA", [128, S], F32, 162048)
    tmpB = at("tmpB", [128, S], F32, 170240)
    srow = at("srow", [1, S], F32, 178432)
    xstage = [at(f"xstage{i}", [128, D], F32, 153856 + i * 8192) for i in range(2)]
    xtst = [at(f"xtst{i}", [128, 128], F32, 170240 + i * 512) for i in range(4)]
    xbblk = [at(f"xbblk{i}", [128, 128], BF16, 172288 + i * 256) for i in range(4)]
    rtmp = at("rtmp", [128, QB], F32, 153856)
    rtmp2 = at("rtmp2", [128, QB], F32, 155904)
    xq = [at(f"xq{i}", [128, 16 * QB], BF16, (157952, 174336)[i]) for i in range(2)]
    wstage = [at(f"wstage{i}", [128, D], F32, 174336 + i * 8192) for i in range(2)]
    wkT = at("wkT", [128, 64 * 128], BF16, 190720)
    wvT = at("wvT", [128, 64 * 128], BF16, 207104)
    wqT = at("wqT", [128, 128 * 128], BF16, 190720)
    nsq = at("nsq", [128, S], F32, 153856)
    nred = at("nred", [1, 96], F32, 162048)
    nsc = at("nsc", [1, 2], F32, 162560)
    wostg = [at(f"wostg{i}", [128, D], BF16, 153856 + i * 4096) for i in range(2)]
    PT = at("PT", [128, 16 * QB], BF16, 129280)
    avT = at("avT", [128, 16 * QB], BF16, 145664)
    zb = at("zb", [128, 16 * QB], F32, 162048)
    wot = [at(f"wot{i}", [128, D], BF16, 194816 + i * 4096) for i in range(2)]
    x0t = [at(f"x0t{i}", [128, QB], F32, 203008 + i * 2048) for i in range(2)]
    meanT = at("meanT", [128, QB], F32, 207104)
    ex2T = at("ex2T", [128, QB], F32, 209152)
    slotB = at("slotB", [128, QB], F32, 211200)
    rstdT = at("rstdT", [128, QB], F32, 213248)
    zsq = [at(f"zsq{i}", [128, QB], F32, 215296 + i * 2048) for i in range(2)]
    x1fst = [at(f"x1fst{i}", [128, QB], F32, 219392 + i * 2048) for i in range(2)]
    x1bst = [at(f"x1bst{i}", [128, QB], BF16, 223488 + i * 1024) for i in range(2)]
    recT = at("recT", [1, QB], F32, 225536)
    recT2 = at("recT2", [1, QB], F32, 24576)
    rbs2 = at("rbs2", [64, QB], F32, 22528)
    rbs = at("rbs", [64, QB], F32, 26624)
    acc = at("acc", [128, 16 * HALF], F32, 30720)
    x1bf = at("x1bf", [128, 16 * HALF], BF16, 96256)
    ring = [at(f"ring{i}", [128, 16 * FE], BF16, 129280 + i * 24576) for i in range(2)]
    ucb = at("ucb", [128, 6 * HALF], BF16, 178432)
    tsil = [at(f"tsil{i}", [128, QB], F32, 190720 + i * 2048) for i in range(2)]
    cbT = at("cbT", [128, HALF], F32, 194816)
    cmbT = at("cmbT", [32, S], F32, 198912)
    x1ft = [at(f"x1ft{i}", [128, HALF], F32, 207104 + i * 4096) for i in range(2)]
    mean2 = at("mean2", [128, HALF], F32, 215296)
    cbrow = at("cbrow", [1, HALF], F32, 219392 - 4096)
    slB2 = at("slB2", [128, HALF], F32, 219392)
    rstd2 = at("rstd2", [128, HALF], F32, 223488)
    gwbf = at("gwbf", [128, 16 * NE], BF16, 129280)
    scT = at("scT", [32, S], F32, 130304)
    scN = at("scN", [128, 16 * NE], F32, 138496)
    m8T = at("m8T", [128, 8], F32, 140544)
    wnT = at("wnT", [128, 8], F32, 141056)
    mskT = at("mskT", [128, NE], F32, 141568)
    g1T = at("g1T", [128, 1], F32, 141696)
    gstg = [at(f"gstg{i}", [128, 16 * QB], BF16, (141824, 160256)[i]) for i in range(2)]
    cmbN = at("cmbN", [128, 16 * NE], F32, 158208)
    ostg = [at(f"ostg{i}", [128, D], F32, (178432, 190720)[i]) for i in range(2)]

    sems = ["dsA", "dsB", "dsC", "dsD", "dsE", "dsF", "dgA", "dgB", "dW", "dOut",
            "tT", "tS", "tV", "tG",
            "dsW0", "dsW1", "dsW2", "dsW3", "dsX0", "dsX1", "dsX2", "dsX3"]
    blk_cm = nc.Block()
    blk = blk_cm.__enter__()
    C = Sched(nc, blk, sems)

    C.on("gpsimd", lambda e: e.iota(i128[0:32, :], pattern=[[0, 1]], base=0,
                                    channel_multiplier=1,
                                    allow_small_or_imprecise_dtypes=True), inc="tG")
    C.on("gpsimd", lambda e: e.iota(srow[:], pattern=[[1, S]], base=0,
                                    channel_multiplier=0,
                                    allow_small_or_imprecise_dtypes=True), inc="tG")
    og = C.on("gpsimd", lambda e: e.memset(ones_f[:], 1.0), inc="tG")
    C.on("gpsimd", lambda e: e.memset(onesK1[:], 1.0), inc="tG")
    C.on("gpsimd", lambda e: e.memset(nl2pi[:], -math.log(2 * math.pi)), inc="tG")
    C.on("gpsimd", lambda e: e.memset(epsT[:], EPS), inc="tG")
    tg = C.on("gpsimd", lambda e: e.affine_select(id_f[:], ones_f[:],
                                                  pattern=[[1, 128]],
                                                  compare_op=ALU.is_equal, fill=0.0,
                                                  base=0, channel_multiplier=-1),
              waits=[("tG", og)], inc="tG")
    dv = C.on("sync", lambda e: e.dma_start(out=i128[32:64, :], in_=i128[0:32, :]),
              waits=[("tG", tg)], inc="dW", inck=16)
    dv = C.on("sync", lambda e: e.dma_start(out=i128[64:128, :], in_=i128[0:64, :]),
              waits=[("dW", dv)], inc="dW", inck=16)
    dv = C.on("sync", lambda e: e.dma_start(out=sbc[0:1, :], in_=srow[:]),
              waits=[("dW", dv)], inc="dW", inck=16)
    p = 1
    while p < 128:
        dv = C.on("sync", lambda e, p=p: e.dma_start(out=sbc[p:2 * p, :],
                                                     in_=sbc[0:p, :]),
                  waits=[("dW", dv)], inc="dW", inck=16)
        p *= 2
    dw0 = C.cnt["dW"]
    ts0 = C.on("scalar", lambda e: e.activation(out=invf[:], in_=i128[:], func=AF.Exp,
                                                bias=nl2pi[:],
                                                scale=-math.log(10000.0) / 32.0),
               waits=[("dW", dw0)], inc="tS")
    C.on("vector", lambda e: e.tensor_scalar(out=sbc[:], in0=sbc[:], scalar1=invf[:],
                                             scalar2=None, op0=ALU.mult),
         waits=[("tS", ts0), ("dW", dw0)])
    C.on("vector", lambda e: e.drain())
    C.on("vector", lambda e: e.tensor_scalar(out=tmpA[:], in0=sbc[:], scalar1=MAGIC,
                                             scalar2=-MAGIC, op0=ALU.add, op1=ALU.add))
    C.on("vector", lambda e: e.drain())
    C.on("vector", lambda e: e.tensor_tensor(out=tmpA[:], in0=sbc[:], in1=tmpA[:],
                                             op=ALU.subtract))
    C.on("vector", lambda e: e.tensor_scalar(out=tmpB[:], in0=sbc[:], scalar1=0.25,
                                             scalar2=None, op0=ALU.add))
    C.on("vector", lambda e: e.drain())
    C.on("vector", lambda e: e.tensor_scalar(out=sbc[:], in0=tmpB[:], scalar1=MAGIC,
                                             scalar2=-MAGIC, op0=ALU.add, op1=ALU.add))
    C.on("vector", lambda e: e.drain())
    tv0 = C.on("vector", lambda e: e.tensor_tensor(out=tmpB[:], in0=tmpB[:], in1=sbc[:],
                                                   op=ALU.subtract), inc="tV")
    C.on("scalar", lambda e: e.activation(out=sin_p[:], in_=tmpA[:], func=AF.Sin,
                                          bias=0.0, scale=2 * math.pi),
         waits=[("tV", tv0)])
    ts1 = C.on("scalar", lambda e: e.activation(out=cos_t[:], in_=tmpB[:], func=AF.Sin,
                                                bias=0.0, scale=2 * math.pi), inc="tS")
    C.on("vector", lambda e: e.tensor_scalar(out=sin_m[:], in0=sin_p[:], scalar1=-1.0,
                                             scalar2=None, op0=ALU.mult),
         waits=[("tS", ts1)])
    C.barrier()

    ldv = {}
    evv = {}
    evs = {}
    wrv = {}
    wbv = {}
    bi = 0
    for st in range(16):
        sl = ("dsA", "dsB")[st % 2]
        w = [("tT", ldv[st - 2])] if st >= 2 else []
        dv = C.on("sync", lambda e, st=st, sl=sl: e.dma_start(
            out=xstage[st % 2][:], in_=x_d[st * 128:(st + 1) * 128, :]),
            waits=w, inc=sl, inck=16)
        for dt in range(16):
            ps = (PS0, PS1)[bi % 2][:, 0:128]
            w = [(sl, dv)] if dt == 0 else []
            if bi >= 2:
                w.append(("tV", evv[bi - 2]))
            ldv[st] = C.on("tensor", lambda e, st=st, dt=dt, ps=ps: e.transpose(
                ps, xstage[st % 2][:, dt * 128:(dt + 1) * 128], id_f[:]),
                waits=w, inc="tT")
            w = [("tT", ldv[st])]
            if bi >= 4:
                w.append((f"dsW{bi % 4}", wrv[bi - 4]))
                w.append((f"dsX{bi % 4}", wbv[bi - 4]))
            C.on("vector", lambda e, bi=bi, ps=ps: e.tensor_copy(
                out=xtst[bi % 4][:], in_=ps), waits=w)
            evv[bi] = C.on("vector", lambda e, bi=bi, ps=ps: e.tensor_copy(
                out=xbblk[bi % 4][:], in_=ps), inc="tV")
            wrv[bi] = C.on("sync", lambda e, st=st, dt=dt, bi=bi: e.dma_start(
                out=xT_scr[dt][:, st * 128:(st + 1) * 128], in_=xtst[bi % 4][:]),
                waits=[("tV", evv[bi])], inc=f"dsW{bi % 4}", inck=16)
            wbv[bi] = C.on("scalar", lambda e, st=st, dt=dt, bi=bi: e.dma_start(
                out=xbf_scr[dt][:, st * 128:(st + 1) * 128], in_=xbblk[bi % 4][:]),
                waits=[("tV", evv[bi])], inc=f"dsX{bi % 4}", inck=16)
            bi += 1
    C.barrier()

    ldv = {}
    evv = {}
    bi = 0
    for r in range(8):
        sl = ("dsA", "dsB")[r % 2]
        w = [("tT", ldv[r - 2])] if r >= 2 else []
        src, rr = (wk_d, r) if r < 4 else (wv_d, r - 4)
        dv = C.on("sync", lambda e, src=src, rr=rr, r=r: e.dma_start(
            out=wstage[r % 2][:], in_=src[rr * 128:(rr + 1) * 128, :]),
            waits=w, inc=sl, inck=16)
        for dt in range(16):
            ps = (PS0, PS1)[bi % 2][:, 0:128]
            w = [(sl, dv)] if dt == 0 else []
            if bi >= 2:
                w.append(("tV", evv[bi - 2]))
            ldv[r] = C.on("tensor", lambda e, r=r, dt=dt, ps=ps: e.transpose(
                ps, wstage[r % 2][:, dt * 128:(dt + 1) * 128], id_f[:]),
                waits=w, inc="tT")
            if r < 4:
                dst = wkT[:, (r * 16 + dt) * 128:(r * 16 + dt + 1) * 128]
            else:
                dst = wvT[:, (dt * 4 + (r - 4)) * 128:(dt * 4 + (r - 4) + 1) * 128]
            evv[bi] = C.on("vector", lambda e, dst=dst, ps=ps: e.tensor_copy(
                out=dst, in_=ps), waits=[("tT", ldv[r])], inc="tV")
            bi += 1
    C.barrier()

    def rope_emit(eng_list, ps, dsts, sc):
        cs = slice(sc * QB, (sc + 1) * QB)
        C.on("vector", lambda e: e.tensor_tensor(out=rtmp[:], in0=ps,
                                                 in1=cos_t[:, cs], op=ALU.mult),
             waits=list(eng_list) + [("tV", C.cnt["tV"])])
        for b in (0, 64):
            lo, hi = b, b + 32
            C.on("vector", lambda e, lo=lo, hi=hi: e.tensor_tensor(
                out=rtmp2[lo:lo + 32, :], in0=ps[hi:hi + 32, :],
                in1=sin_m[hi:hi + 32, cs], op=ALU.mult))
            C.on("vector", lambda e, lo=lo, hi=hi: e.tensor_tensor(
                out=rtmp2[hi:hi + 32, :], in0=ps[lo:lo + 32, :],
                in1=sin_p[lo:lo + 32, cs], op=ALU.mult))
        C.on("vector", lambda e: e.drain())
        if isinstance(dsts, tuple):
            dA, dB = dsts
            C.on("vector", lambda e: e.tensor_tensor(out=dA, in0=rtmp[0:64, :],
                                                     in1=rtmp2[0:64, :], op=ALU.add))
            return C.on("vector", lambda e: e.tensor_tensor(
                out=dB, in0=rtmp[64:128, :], in1=rtmp2[64:128, :], op=ALU.add),
                inc="tV")
        return C.on("vector", lambda e: e.tensor_tensor(out=dsts, in0=rtmp[:],
                                                        in1=rtmp2[:], op=ALU.add),
                    inc="tV")

    evv = {}
    ci = 0
    ldq = {}
    for sc in range(4):
        sl = ("dsA", "dsB")[sc % 2]
        xqb = xq[sc % 2]
        for dt in range(16):
            w = []
            if sc >= 2 and dt == 0:
                w.append(("tT", ldq[sc - 2]))
            dv = C.on("sync", lambda e, dt=dt, sc=sc, xqb=xqb: e.dma_start(
                out=xqb[:, dt * QB:(dt + 1) * QB],
                in_=xbf_scr[dt][:, sc * QB:(sc + 1) * QB]),
                waits=w, inc=sl, inck=16)
        for ec in range(4):
            ps = (PS0, PS1)[ci % 2][:, 0:512]
            w = [(sl, dv)] if ec == 0 else []
            if ci >= 2:
                w.append(("tV", evv[ci - 2]))
            for dt in range(16):
                tv = C.on("tensor", lambda e, ec=ec, dt=dt, ps=ps, xqb=xqb: e.matmul(
                    ps, wkT[:, (ec * 16 + dt) * 128:(ec * 16 + dt + 1) * 128],
                    xqb[:, dt * QB:(dt + 1) * QB],
                    start=(dt == 0), stop=(dt == 15)),
                    waits=w if dt == 0 else [], inc="tT" if dt == 15 else None)
            ldq[sc] = tv
            dst = kTr[:, ec * S + sc * QB: ec * S + (sc + 1) * QB]
            evv[ci] = rope_emit([("tT", tv)], ps, dst, sc)
            ci += 1
        for lq in range(4):
            lc = sc * 4 + lq
            ps = (PS0, PS1)[ci % 2][:, 0:512]
            w = []
            if ci >= 2:
                w.append(("tV", evv[ci - 2]))
            for dt in range(16):
                tv = C.on("tensor", lambda e, lc=lc, lq=lq, dt=dt, ps=ps, xqb=xqb:
                          e.matmul(
                    ps, xqb[:, dt * QB + lq * 128: dt * QB + (lq + 1) * 128],
                    wvT[:, dt * 512:(dt + 1) * 512],
                    start=(dt == 0), stop=(dt == 15)),
                    waits=w if dt == 0 else [], inc="tT" if dt == 15 else None)
            ldq[sc] = tv
            vv = v_on.ap().rearrange("p (l h c) -> p l h c", l=16, c=65)
            C.on("vector", lambda e, lc=lc, ps=ps: e.tensor_copy(
                out=vv[:, lc, :, 0:64],
                in_=ps.rearrange("p (h c) -> p h c", c=64)),
                waits=[("tT", tv)])
            evv[ci] = C.on("vector", lambda e, lc=lc: e.memset(
                vv[:, lc, :, 64:65], 1.0), inc="tV")
            ci += 1
    C.barrier()

    for hec in range(2):
        ldv = {}
        evv2 = {}
        bi = 0
        for r in range(8):
            sl = ("dsA", "dsB")[r % 2]
            w = [("tT", ldv[r - 2])] if r >= 2 else []
            rr = hec * 8 + r
            dv = C.on("sync", lambda e, rr=rr, r=r: e.dma_start(
                out=wstage[r % 2][:], in_=wq_d[rr * 128:(rr + 1) * 128, :]),
                waits=w, inc=sl, inck=16)
            for dt in range(16):
                ps = (PS0, PS1)[bi % 2][:, 0:128]
                w = [(sl, dv)] if dt == 0 else []
                if bi >= 2:
                    w.append(("tV", evv2[bi - 2]))
                ldv[r] = C.on("tensor", lambda e, r=r, dt=dt, ps=ps: e.transpose(
                    ps, wstage[r % 2][:, dt * 128:(dt + 1) * 128], id_f[:]),
                    waits=w, inc="tT")
                dst = wqT[:, (r * 16 + dt) * 128:(r * 16 + dt + 1) * 128]
                evv2[bi] = C.on("vector", lambda e, dst=dst, ps=ps: e.tensor_copy(
                    out=dst, in_=ps), waits=[("tT", ldv[r])], inc="tV")
                bi += 1
        C.barrier()
        evv = {}
        ci = 0
        ldq = {}
        for sc in range(4):
            sl = ("dsA", "dsB")[sc % 2]
            xqb = xq[sc % 2]
            for dt in range(16):
                w = []
                if sc >= 2 and dt == 0:
                    w.append(("tT", ldq[sc - 2]))
                dv = C.on("sync", lambda e, dt=dt, sc=sc, xqb=xqb: e.dma_start(
                    out=xqb[:, dt * QB:(dt + 1) * QB],
                    in_=xbf_scr[dt][:, sc * QB:(sc + 1) * QB]),
                    waits=w, inc=sl, inck=16)
            for ec in range(8):
                ps = (PS0, PS1)[ci % 2][:, 0:512]
                w = [(sl, dv)] if ec == 0 else []
                if ci >= 2:
                    w.append(("tV", evv[ci - 2]))
                for dt in range(16):
                    tv = C.on("tensor", lambda e, ec=ec, dt=dt, ps=ps, xqb=xqb:
                              e.matmul(
                        ps, wqT[:, (ec * 16 + dt) * 128:(ec * 16 + dt + 1) * 128],
                        xqb[:, dt * QB:(dt + 1) * QB],
                        start=(dt == 0), stop=(dt == 15)),
                        waits=w if dt == 0 else [], inc="tT" if dt == 15 else None)
                ldq[sc] = tv
                ecg = hec * 8 + ec
                hA, hB = 2 * ecg, 2 * ecg + 1
                dA = qTr[Q_BASE[hA]:Q_BASE[hA] + 64,
                         Q_TILE[hA] * S + sc * QB: Q_TILE[hA] * S + (sc + 1) * QB]
                dB = qTr[Q_BASE[hB]:Q_BASE[hB] + 64,
                         Q_TILE[hB] * S + sc * QB: Q_TILE[hB] * S + (sc + 1) * QB]
                evv[ci] = rope_emit([("tT", tv)], ps, (dA, dB), sc)
                ci += 1
        C.barrier()

    tiles = [kTr[:, i * S:(i + 1) * S] for i in range(4)] + \
            [qTr[:, i * S:(i + 1) * S] for i in range(16)]
    sqv = {}
    mmv = {}
    rdv = {}
    for t, tile in enumerate(tiles):
        w = [("tT", mmv[(t - 1, 3)])] if t >= 1 else []
        sqv[t] = C.on("scalar", lambda e, tile=tile: e.square(
            nsq[:, 0:2048], tile), waits=w, inc="tS")
        for c in range(4):
            gi = t * 4 + c
            ps = PS2[0:1, (gi % 2) * 512:(gi % 2) * 512 + 512]
            w2 = [("tS", sqv[t])] if c == 0 else []
            if gi >= 2:
                w2.append(("tV", rdv[gi - 2]))
            mmv[(t, c)] = C.on("tensor", lambda e, c=c, ps=ps: e.matmul(
                ps, ones_f[:, 0:1], nsq[:, c * 512:(c + 1) * 512],
                start=True, stop=True), waits=w2, inc="tT")
            rdv[gi] = C.on("vector", lambda e, gi=gi, ps=ps: e.reduce_max(
                out=nred[0:1, gi:gi + 1], in_=ps, axis=AX.X),
                waits=[("tT", mmv[(t, c)])], inc="tV")
    C.on("vector", lambda e: e.drain())
    C.on("vector", lambda e: e.reduce_max(out=nsc[0:1, 0:1], in_=nred[0:1, 0:16],
                                          axis=AX.X))
    C.on("vector", lambda e: e.drain())
    C.on("vector", lambda e: e.reduce_max(out=nsc[0:1, 1:2], in_=nred[0:1, 16:80],
                                          axis=AX.X))
    C.on("vector", lambda e: e.drain())
    mv = C.on("vector", lambda e: e.tensor_tensor(out=nsc[0:1, 1:2],
                                                  in0=nsc[0:1, 1:2],
                                                  in1=nsc[0:1, 0:1], op=ALU.mult),
              inc="tV")
    sv = C.on("scalar", lambda e: e.activation(out=nsc[0:1, 1:2], in_=nsc[0:1, 1:2],
                                               func=AF.Sqrt, bias=0.0, scale=1.0),
              waits=[("tV", mv)], inc="tS")
    nv = C.on("vector", lambda e: e.tensor_scalar(out=nsc[0:1, 1:2],
                                                  in0=nsc[0:1, 1:2], scalar1=-SCALE,
                                                  scalar2=None, op0=ALU.mult),
              waits=[("tS", sv)], inc="tV")
    tb = C.on("tensor", lambda e: e.matmul(PS2[0:128, 512:513], onesK1[0:1, :],
                                           nsc[0:1, 1:2], start=True, stop=True),
              waits=[("tV", nv)], inc="tT")
    C.on("vector", lambda e: e.tensor_copy(out=biasB[0:128, 0:1],
                                           in_=PS2[0:128, 512:513]),
         waits=[("tT", tb)], inc="tV")
    C.on("vector", lambda e: e.drain())
    C.on("vector", lambda e: e.tensor_copy(
        out=biasB[:, 1:NH], in_=biasB[0:128, 0:1].broadcast_to([128, NH - 1])),
        inc="tV")
    C.barrier()

    ldv = {}
    evv = {}
    wsl = {}
    bi = 0
    for epc in range(16):
        sl = ("dsA", "dsB")[epc % 2]
        w = [("tT", ldv[epc - 2])] if epc >= 2 else []
        dv = C.on("sync", lambda e, epc=epc: e.dma_start(
            out=wstage[epc % 2][:], in_=wo_d[epc * 128:(epc + 1) * 128, :]),
            waits=w, inc=sl, inck=16)
        for dt in range(16):
            ps = (PS0, PS1)[bi % 2][:, 0:128]
            w = [(sl, dv)] if dt == 0 else []
            if bi >= 2:
                w.append(("tV", evv[bi - 2]))
            ldv[epc] = C.on("tensor", lambda e, epc=epc, dt=dt, ps=ps: e.transpose(
                ps, wstage[epc % 2][:, dt * 128:(dt + 1) * 128], id_f[:]),
                waits=w, inc="tT")
            w = [("tT", ldv[epc])]
            if epc >= 2 and dt == 0:
                w.append((("dsE", "dsF")[epc % 2], wsl[epc - 2]))
            hA, hB = 2 * dt, 2 * dt + 1
            C.on("vector", lambda e, epc=epc, dt=dt, ps=ps, hA=hA: e.tensor_copy(
                out=wostg[epc % 2][Q_BASE[hA]:Q_BASE[hA] + 64,
                                   Q_TILE[hA] * 128:(Q_TILE[hA] + 1) * 128],
                in_=ps[0:64, :]), waits=w)
            evv[bi] = C.on("vector", lambda e, epc=epc, dt=dt, ps=ps, hB=hB:
                           e.tensor_copy(
                out=wostg[epc % 2][Q_BASE[hB]:Q_BASE[hB] + 64,
                                   Q_TILE[hB] * 128:(Q_TILE[hB] + 1) * 128],
                in_=ps[64:128, :]), inc="tV")
            bi += 1
        wsl[epc] = C.on("sync", lambda e, epc=epc: e.dma_start(
            out=woT_scr[epc][:], in_=wostg[epc % 2][:]),
            waits=[("tV", evv[bi - 1])], inc=("dsE", "dsF")[epc % 2], inck=16)
    C.barrier()

    if debug:
        cos_d = nc.dram_tensor("cos_d", [128, S], F32, kind="ExternalOutput").ap()
        sin_d = nc.dram_tensor("sin_d", [128, S], F32, kind="ExternalOutput").ap()
        invf_d = nc.dram_tensor("invf_d", [128, 1], F32, kind="ExternalOutput").ap()
        C.on("gpsimd", lambda e: e.dma_start(out=cos_d[:], in_=cos_t[:]),
             inc="dgB", inck=16)
        C.on("gpsimd", lambda e: e.dma_start(out=sin_d[:], in_=sin_p[:]),
             inc="dgB", inck=16)
        C.on("gpsimd", lambda e: e.dma_start(out=invf_d[:], in_=invf[:]),
             inc="dgB", inck=16)
        C.barrier()

    spv, expv, avpv, recv, rbv, copyv, multv = {}, {}, {}, {}, {}, {}, {}
    oT, szT, sz2T, zV, sqS = {}, {}, {}, {}, {}
    wosl, x0sl = {}, {}
    fin_prev = {"tm": 0, "fv": 0}
    for q in range(4):
        q0 = q * QB
        bT = C.cnt["tT"]
        bS = C.cnt["tS"]
        bV = C.cnt["tV"]
        kq = q * NH
        rb_ps = PS3[0:64, 0:512]
        pairs = list(zip(_B0, _B64))
        k_of = {}
        for _p, (_a, _b) in enumerate(pairs):
            k_of[_a], k_of[_b] = kq + 2 * _p, kq + 2 * _p + 1
        stA = [PS0[:, 0:512], PS0[:, 512:1024]]
        stB = [PS1[:, 0:512], PS3[:, 512:1024]]
        avp3 = [PS1[0:65, 512:1024], PS2[0:65, 0:512], PS2[0:65, 512:1024]]

        def pt_slot(h, lc):
            base = 0 if Q_BASE[h] == 0 else 4
            return PT[:, (base + lc % 4) * 512:(base + lc % 4 + 1) * 512]

        def emit_av(h, lc, do_inc):
            kh = h // 4
            k = k_of[h]
            w = [("tS", expv[(h, lc)])]
            if lc == 0 and k >= 3:
                w.append(("tV", multv[k - 3]))
            if lc == 0 and q >= 1 and (k - kq) < 3 and k % 3 in (1, 2):
                w.append(("tS", fin_prev["tm"]))
            avpv[(h, lc)] = C.on("tensor", lambda e, h=h, lc=lc, kh=kh, k=k:
                                 e.matmul(
                avp3[k % 3],
                v_on[:, lc * 520 + kh * 65: lc * 520 + (kh + 1) * 65],
                pt_slot(h, lc),
                start=(lc == 0), stop=(lc == 15)), waits=w,
                inc="tT" if do_inc else None)

        for pp in range(17):
            for lc in range(16):
                if pp < 16:
                    for si, h in enumerate(pairs[pp]):
                        kh = h // 4
                        w = []
                        if lc >= 2:
                            w.append(("tS", expv[(h, lc - 2)]))
                        elif pp >= 1:
                            w.append(("tS", expv[(pairs[pp - 1][si], lc + 14)]))
                        elif q >= 1:
                            w.append(("tS", expv[(pairs[15][si], lc + 14)]))
                            if si == 1 and lc == 1:
                                w.append(("tT", oT[15]))
                        kap = kTr[(kh % 2) * 64:(kh % 2) * 64 + 64,
                                  (kh // 2) * S + lc * 128:(kh // 2) * S + (lc + 1) * 128]
                        qap = qTr[Q_BASE[h]:Q_BASE[h] + 64,
                                  Q_TILE[h] * S + q0:Q_TILE[h] * S + q0 + QB]
                        st = (stA, stB)[Q_BASE[h] // 64][lc % 2]
                        jv = C.on("tensor", lambda e, kap=kap, qap=qap,
                                  st=st: e.matmul(
                            st, kap, qap, start=True, stop=True),
                            waits=w, inc="tT" if si == 1 else None)
                    spv[(pairs[pp][0], lc)] = jv
                    spv[(pairs[pp][1], lc)] = jv
                if lc == 2 and pp >= 1:
                    hA, hB = pairs[pp - 1]
                    rcA, rcB = recT, recT2
                    rsA, rsB = rbs, rbs2
                    for h, rc in ((hA, rcA), (hB, rcB)):
                        k = k_of[h]
                        w = [("tT", avpv[(h, 15)])]
                        if k >= 4:
                            w.append(("tT", rbv[k - 2]))
                        recv[k] = C.on("vector", lambda e, k=k, rc=rc: e.reciprocal(
                            rc[0:1, :], avp3[k % 3][64:65, 0:512]), waits=w, inc="tV")
                    kA, kB = k_of[hA], k_of[hB]
                    rbv[kA] = C.on("tensor", lambda e, rcA=rcA: e.matmul(
                        rb_ps, onesK1[0:1, 0:64], rcA[0:1, :], start=True, stop=True),
                        waits=[("tV", recv[kA])] +
                              ([("tV", copyv[kB - 2])] if kB >= 3 else []), inc="tT")
                    copyv[kA] = C.on("vector", lambda e, rsA=rsA: e.tensor_copy(
                        out=rsA[:], in_=rb_ps),
                        waits=[("tT", rbv[kA])] +
                              ([("tV", multv[kA - 2])] if kA >= 2 else []), inc="tV")
                    rbv[kB] = C.on("tensor", lambda e, rcB=rcB: e.matmul(
                        rb_ps, onesK1[0:1, 0:64], rcB[0:1, :], start=True, stop=True),
                        waits=[("tV", recv[kB]), ("tV", copyv[kA])], inc="tT")
                    copyv[kB] = C.on("vector", lambda e, rsB=rsB: e.tensor_copy(
                        out=rsB[:], in_=rb_ps),
                        waits=[("tT", rbv[kB])] +
                              ([("tV", multv[kB - 2])] if kB >= 2 else []), inc="tV")
                    for h, rs in ((hA, rsA), (hB, rsB)):
                        k = k_of[h]
                        dst = avT[Q_BASE[h]:Q_BASE[h] + 64,
                                  Q_TILE[h] * 512:(Q_TILE[h] + 1) * 512]
                        multv[k] = C.on("vector", lambda e, k=k, dst=dst, rs=rs:
                                        e.tensor_tensor(
                            out=dst, in0=avp3[k % 3][0:64, 0:512], in1=rs[:],
                            op=ALU.mult), waits=[("tV", copyv[k])], inc="tV")
                if lc >= 2 and pp < 16:
                    emit_av(pairs[pp][0], lc - 2, False)
                    emit_av(pairs[pp][1], lc - 2, True)
                    avpv[(pairs[pp][0], lc - 2)] = avpv[(pairs[pp][1], lc - 2)]
                elif lc < 2 and pp >= 1:
                    emit_av(pairs[pp - 1][0], lc + 14, False)
                    emit_av(pairs[pp - 1][1], lc + 14, True)
                    avpv[(pairs[pp - 1][0], lc + 14)] = avpv[(pairs[pp - 1][1], lc + 14)]
                if pp < 16:
                    for h in pairs[pp]:
                        w = [("tT", spv[(h, lc)])]
                        if lc >= 4:
                            w.append(("tT", avpv[(h, lc - 4)]))
                        elif pp >= 1:
                            hp = pairs[pp - 1][Q_BASE[h] // 64]
                            w.append(("tT", avpv[(hp, lc + 12)]))
                        elif q >= 1:
                            w.append(("tT", avpv[(pairs[15][Q_BASE[h] // 64],
                                                  lc + 12)]))
                        st = (stA, stB)[Q_BASE[h] // 64][lc % 2]
                        expv[(h, lc)] = C.on("scalar", lambda e, h=h, lc=lc, st=st:
                                             e.activation(
                            out=pt_slot(h, lc), in_=st, func=AF.Exp,
                            bias=biasB[:, h:h + 1], scale=SCALE),
                            waits=w, inc="tS")
        assert C.cnt["tS"] == bS + 512
        assert C.cnt["tV"] == bV + 96
        bV96 = multv[kq + 31]
        bT2 = C.cnt["tT"]
        dsA0, dsB0 = C.cnt["dsA"], C.cnt["dsB"]
        dsC0, dsD0 = C.cnt["dsC"], C.cnt["dsD"]
        dsE0, dsF0 = C.cnt["dsE"], C.cnt["dsF"]

        def stats_pair(epc):
            w = [("tV", zV[epc])]
            if epc == 0:
                w.append(("tS", fin_prev["tm"]))
            szT[epc] = C.on("tensor", lambda e, epc=epc: e.matmul(
                PS2[:, 0:512], ones_f[:], zb[:, epc * 512:(epc + 1) * 512],
                start=(epc == 0), stop=(epc == 15)),
                waits=w, inc="tT")
            sz2T[epc] = C.on("tensor", lambda e, epc=epc: e.matmul(
                PS2[:, 512:1024], ones_f[:], zsq[epc % 2][:],
                start=(epc == 0), stop=(epc == 15)),
                waits=[("tS", sqS[epc])], inc="tT")

        for epc in range(16):
            slw = ("dsA", "dsB")[epc % 2]
            if epc >= 2:
                w = [("tT", oT[epc - 2])]
            elif q >= 1:
                w = [("tT", oT[epc + 14])]
            else:
                w = []
            wosl[epc] = C.on("sync", lambda e, epc=epc: e.dma_start(
                out=wot[epc % 2][:], in_=woT_scr[epc][:]), waits=w, inc=slw, inck=16)
            slx = ("dsC", "dsD")[epc % 2]
            if epc >= 2:
                w = [("tV", zV[epc - 2])]
            elif q >= 1:
                w = [("tV", zV[epc + 14])]
            else:
                w = []
            x0sl[epc] = C.on("sync", lambda e, epc=epc, q0=q0: e.dma_start(
                out=x0t[epc % 2][:], in_=xT_scr[epc][:, q0:q0 + QB]),
                waits=w, inc=slx, inck=16)
            op = PS3[:, (epc % 2) * 512:(epc % 2) * 512 + 512]
            w = [(slw, wosl[epc])]
            if epc == 0:
                w.append(("tV", bV96))
            if epc >= 2:
                w.append(("tV", zV[epc - 2]))
            elif q >= 1:
                w.append(("tV", zV[epc + 14]))
            for et in range(16):
                oT[epc] = C.on("tensor", lambda e, epc=epc, et=et, op=op: e.matmul(
                    op, wot[epc % 2][:, et * 128:(et + 1) * 128],
                    avT[:, et * 512:(et + 1) * 512],
                    start=(et == 0), stop=(et == 15)),
                    waits=w if et == 0 else [], inc="tT" if et == 15 else None)
            zV[epc] = C.on("vector", lambda e, epc=epc, op=op: e.tensor_tensor(
                out=zb[:, epc * 512:(epc + 1) * 512], in0=op, in1=x0t[epc % 2][:],
                op=ALU.add), waits=[("tT", oT[epc]), (slx, x0sl[epc])], inc="tV")
            w = [("tV", zV[epc])]
            if epc >= 2:
                w.append(("tT", sz2T[epc - 2]))
            elif q >= 1:
                w.append(("tT", sz2T[epc + 14]))
            sqS[epc] = C.on("scalar", lambda e, epc=epc: e.square(
                zsq[epc % 2][:], zb[:, epc * 512:(epc + 1) * 512]), waits=w, inc="tS")
            if epc >= 1:
                stats_pair(epc - 1)
        stats_pair(15)
        tm = C.on("scalar", lambda e: e.activation(out=meanT[:], in_=PS2[:, 0:512],
                                                   func=AF.Copy, bias=0.0,
                                                   scale=1.0 / D),
                  waits=[("tT", sz2T[15]), ("tV", fin_prev["fv"])], inc="tS")
        tm = C.on("scalar", lambda e: e.activation(out=ex2T[:], in_=PS2[:, 512:1024],
                                                   func=AF.Copy, bias=0.0,
                                                   scale=1.0 / D), inc="tS")
        mv = C.on("vector", lambda e: e.tensor_tensor(out=slotB[:], in0=meanT[:],
                                                      in1=meanT[:], op=ALU.mult),
                  waits=[("tS", tm)], inc="tV")
        C.on("vector", lambda e: e.drain())
        mv = C.on("vector", lambda e: e.tensor_tensor(out=slotB[:], in0=ex2T[:],
                                                      in1=slotB[:], op=ALU.subtract),
                  inc="tV")
        sv = C.on("scalar", lambda e: e.activation(out=ex2T[:], in_=slotB[:],
                                                   func=AF.Sqrt, bias=epsT[:],
                                                   scale=1.0),
                  waits=[("tV", mv)], inc="tS")
        rv = C.on("vector", lambda e: e.reciprocal(rstdT[:], ex2T[:]),
                  waits=[("tS", sv)], inc="tV")
        C.on("vector", lambda e: e.drain())
        for ep in range(0, 16, 2):
            for j in (0, 1):
                epc = ep + j
                w = [("tV", rv)]
                slt = ("dsE", "dsF")[epc % 2]
                base = (dsE0, dsF0)[epc % 2]
                if epc >= 2:
                    w.append((slt, base + 32 * (epc // 2)))
                elif q >= 1:
                    w.append((slt, base))
                C.on("vector", lambda e, epc=epc: e.tensor_tensor(
                    out=x1fst[epc % 2][:], in0=zb[:, epc * 512:(epc + 1) * 512],
                    in1=meanT[:], op=ALU.subtract), waits=w)
            for j in (0, 1):
                epc = ep + j
                fv = C.on("vector", lambda e, epc=epc: e.tensor_tensor(
                    out=x1fst[epc % 2][:], in0=x1fst[epc % 2][:], in1=rstdT[:],
                    op=ALU.mult), inc="tV")
                bv2 = C.on("scalar", lambda e, epc=epc: e.copy(
                    x1bst[epc % 2][:], x1fst[epc % 2][:]),
                    waits=[("tV", fv)], inc="tS")
                slt = ("dsE", "dsF")[epc % 2]
                C.on("sync", lambda e, epc=epc, q0=q0: e.dma_start(
                    out=x1f_scr[epc][:, q0:q0 + QB], in_=x1fst[epc % 2][:]),
                    waits=[("tV", fv)], inc=slt, inck=16)
                C.on("sync", lambda e, epc=epc, q0=q0: e.dma_start(
                    out=x1b_scr[epc][:, q0:q0 + QB], in_=x1bst[epc % 2][:]),
                    waits=[("tS", bv2)], inc=slt, inck=16)
                fin_prev["fv"] = fv
        fin_prev["tm"] = tm
    C.barrier()

    if debug:
        qTr_d = nc.dram_tensor("qTr_d", [128, 16 * S], F32, kind="ExternalOutput").ap()
        kTr_d = nc.dram_tensor("kTr_d", [128, 4 * S], F32, kind="ExternalOutput").ap()
        avT_d = nc.dram_tensor("avT_d", [128, 16 * QB], F32, kind="ExternalOutput").ap()
        bias_d = nc.dram_tensor("bias_d", [128, NH], F32, kind="ExternalOutput").ap()
        von_d = nc.dram_tensor("von_d", [128, 16 * 520], F32, kind="ExternalOutput").ap()
        for i in range(16):
            C.on("gpsimd", lambda e, i=i: e.dma_start(
                out=qTr_d[:, i * S:(i + 1) * S], in_=qTr[:, i * S:(i + 1) * S]),
                inc="dgB", inck=16)
        for i in range(4):
            C.on("gpsimd", lambda e, i=i: e.dma_start(
                out=kTr_d[:, i * S:(i + 1) * S], in_=kTr[:, i * S:(i + 1) * S]),
                inc="dgB", inck=16)
        C.on("gpsimd", lambda e: e.dma_start(out=avT_d[:], in_=avT[:]),
             inc="dgB", inck=16)
        C.on("gpsimd", lambda e: e.dma_start(out=bias_d[:], in_=biasB[:]),
             inc="dgB", inck=16)
        C.on("gpsimd", lambda e: e.dma_start(out=von_d[:], in_=v_on[:]),
             inc="dgB", inck=16)
        C.barrier()

    for dt in range(16):
        gl = C.on("gpsimd", lambda e, dt=dt: e.dma_start(
            out=gwbf[:, dt * 32:(dt + 1) * 32],
            in_=gw_d[dt * 128:(dt + 1) * 128, :]), inc="dgA", inck=16)
    gts = {}
    sgv = {}
    for sc in range(4):
        gb = gstg[sc % 2]
        for dt in range(16):
            gv = C.on("sync", lambda e, dt=dt, sc=sc, gb=gb: e.dma_start(
                out=gb[:, dt * QB:(dt + 1) * QB],
                in_=x1b_scr[dt][:, sc * QB:(sc + 1) * QB]),
                waits=[("tT", gts[sc - 2])] if (sc >= 2 and dt == 0) else [],
                inc=("dsA", "dsB")[sc % 2], inck=16)
        w = [(("dsA", "dsB")[sc % 2], gv), ("dgA", gl)]
        if sc >= 2:
            w.append(("tS", sgv[sc - 2]))
        gp_ps = (PS0, PS1)[sc % 2][0:32, 0:512]
        for dt in range(16):
            gts[sc] = C.on("tensor", lambda e, sc=sc, dt=dt, gp_ps=gp_ps, gb=gb:
                           e.matmul(
                gp_ps, gwbf[:, dt * 32:(dt + 1) * 32], gb[:, dt * QB:(dt + 1) * QB],
                start=(dt == 0), stop=(dt == 15)),
                waits=w if dt == 0 else [], inc="tT" if dt == 15 else None)
        sgv[sc] = C.on("scalar", lambda e, sc=sc, gp_ps=gp_ps: e.activation(
            out=scT[:, sc * 512:(sc + 1) * 512], in_=gp_ps, func=AF.Sigmoid,
            bias=0.0, scale=1.0), waits=[("tT", gts[sc])], inc="tS")
    prev_cp = 0
    for tc in range(16):
        t_ps = PS3[:, (tc % 2) * 512:(tc % 2) * 512 + 32]
        w = [("tV", prev_cp)] if tc >= 2 else [("tS", sgv[3])]
        tt = C.on("tensor", lambda e, tc=tc, t_ps=t_ps: e.transpose(
            t_ps, scT[:, tc * 128:(tc + 1) * 128], id_f[0:32, 0:32]),
            waits=w, inc="tT")
        C.on("vector", lambda e, tc=tc, t_ps=t_ps: e.tensor_copy(
            out=scN[:, tc * 32:(tc + 1) * 32], in_=t_ps), waits=[("tT", tt)])
        C.on("vector", lambda e: e.drain())
        C.on("vector", lambda e, tc=tc: e.max(out=m8T[:],
                                              in_=scN[:, tc * 32:(tc + 1) * 32]))
        C.on("vector", lambda e: e.drain())
        C.on("vector", lambda e: e.reduce_sum(out=g1T[:], in_=m8T[:, 0:6], axis=AX.X))
        C.on("vector", lambda e: e.drain())
        C.on("vector", lambda e: e.tensor_scalar(out=g1T[:], in0=g1T[:], scalar1=1e-20,
                                                 scalar2=None, op0=ALU.add))
        C.on("vector", lambda e: e.drain())
        C.on("vector", lambda e: e.reciprocal(g1T[:], g1T[:]))
        C.on("vector", lambda e: e.drain())
        C.on("vector", lambda e: e.tensor_scalar(out=wnT[:, 0:6], in0=m8T[:, 0:6],
                                                 scalar1=g1T[:], scalar2=None,
                                                 op0=ALU.mult))
        C.on("vector", lambda e, tc=tc: e.memset(cmbN[:, tc * 32:(tc + 1) * 32], 0.0))
        C.on("vector", lambda e: e.drain())
        for j in range(6):
            C.on("vector", lambda e, tc=tc, j=j: e.tensor_scalar(
                out=mskT[:], in0=scN[:, tc * 32:(tc + 1) * 32],
                scalar1=m8T[:, j:j + 1], scalar2=None, op0=ALU.is_equal))
            C.on("vector", lambda e: e.drain())
            cpv = C.on("vector", lambda e, tc=tc, j=j: e.scalar_tensor_tensor(
                out=cmbN[:, tc * 32:(tc + 1) * 32], in0=mskT[:],
                scalar=wnT[:, j:j + 1], in1=cmbN[:, tc * 32:(tc + 1) * 32],
                op0=ALU.mult, op1=ALU.add), inc="tV" if j == 5 else None)
            C.on("vector", lambda e: e.drain())
        t2_ps = PS3[0:32, (tc % 2) * 512 + 64:(tc % 2) * 512 + 192]
        tb = C.on("tensor", lambda e, tc=tc, t2_ps=t2_ps: e.transpose(
            t2_ps, cmbN[:, tc * 32:(tc + 1) * 32], id_f[:]),
            waits=[("tV", cpv)], inc="tT")
        prev_cp = C.on("vector", lambda e, tc=tc, t2_ps=t2_ps: e.tensor_copy(
            out=cmbT[:, tc * 128:(tc + 1) * 128], in_=t2_ps),
            waits=[("tT", tb)], inc="tV")
    topk_tV = prev_cp

    def uw_src(ei, dt):
        if ei == 0:
            return su_d[dt * 128:(dt + 1) * 128, 0:FE]
        if ei == 1:
            return su_d[dt * 128:(dt + 1) * 128, FE:2 * FE]
        return wu_d[ei - 2][dt * 128:(dt + 1) * 128, :]

    def gw_src(ei, dt):
        if ei == 0:
            return sg_d[dt * 128:(dt + 1) * 128, 0:FE]
        if ei == 1:
            return sg_d[dt * 128:(dt + 1) * 128, FE:2 * FE]
        return wg_d[ei - 2][dt * 128:(dt + 1) * 128, :]

    def dw_src(ei, ft):
        if ei == 0:
            return sd_d[ft * 128:(ft + 1) * 128, :]
        if ei == 1:
            return sd_d[FSH // 2 + ft * 128:FSH // 2 + (ft + 1) * 128, :]
        return wd_d[ei - 2][ft * 128:(ft + 1) * 128, :]

    consumed = {}
    ps2_user = {0: 0, 1: 0}
    prev_cbmm = 0
    for h2 in range(2):
        tok0 = h2 * HALF
        for dt in range(16):
            C.on("sync", lambda e, dt=dt, tok0=tok0: e.dma_start(
                out=x1bf[:, dt * HALF:(dt + 1) * HALF],
                in_=x1b_scr[dt][:, tok0:tok0 + HALF]), inc="dW", inck=16)
        dwh = C.cnt["dW"]
        u_free = {}
        g_free = {}
        r_free = {}
        tsil_free = {}
        act_v = {}
        x0 = C.cnt["dsC"]
        x1 = C.cnt["dsD"]
        x1sl = {}
        for ei in range(34):
            mb = (h2 * 34 + ei) * 3
            real = ei >= 2
            exp_i = ei - 2
            if real:
                wrow = [("tT", prev_cbmm)] if prev_cbmm else [("tV", topk_tV)]
                rowv = C.on("sync", lambda e, exp_i=exp_i, tok0=tok0: e.dma_start(
                    out=cbrow[:], in_=cmbT[exp_i:exp_i + 1, tok0:tok0 + HALF]),
                    waits=wrow, inc="dsC", inck=16)
                for sc in range(2):
                    w = [("tV", ps2_user[sc]), ("dsC", rowv)]
                    cb_ps = PS2[:, sc * 512:(sc + 1) * 512]
                    tb = C.on("tensor", lambda e, sc=sc, cb_ps=cb_ps: e.matmul(
                        cb_ps, onesK1[0:1, :], cbrow[0:1, sc * 512:(sc + 1) * 512],
                        start=True, stop=True), waits=w, inc="tT")
                    prev_cbmm = tb
                    ps2_user[sc] = C.on("vector", lambda e, sc=sc, cb_ps=cb_ps:
                                        e.tensor_copy(out=cbT[:, sc * 512:(sc + 1) * 512],
                                                      in_=cb_ps),
                                        waits=[("tT", tb)], inc="tV")
            m = mb + 0
            slot = m % 2
            slx = ("dgA", "dgB")[slot]
            for dt in range(16):
                if m >= 2 and dt == 0:
                    w = [("tT", consumed[m - 2])]
                elif dt == 0:
                    w = [("tV", topk_tV)]
                else:
                    w = []
                uv = C.on("gpsimd", lambda e, ei=ei, dt=dt, slot=slot: e.dma_start(
                    out=ring[slot][:, dt * FE:(dt + 1) * FE], in_=uw_src(ei, dt)),
                    waits=w, inc=slx, inck=16)
            for k in range(12):
                sc, fc = k // 6, k % 6
                u_ps = PS0[:, (k % 2) * 512:(k % 2) * 512 + 512]
                w = []
                if k == 0:
                    w = [(slx, uv), ("dW", dwh)]
                if u_free.get(k % 2):
                    w.append(("tV", u_free[k % 2]))
                for dt in range(16):
                    tv = C.on("tensor", lambda e, dt=dt, sc=sc, fc=fc, slot=slot,
                              u_ps=u_ps: e.matmul(
                        u_ps, ring[slot][:, dt * FE + fc * 128:dt * FE + (fc + 1) * 128],
                        x1bf[:, dt * HALF + sc * 512:dt * HALF + (sc + 1) * 512],
                        start=(dt == 0), stop=(dt == 15)),
                        waits=w if dt == 0 else [], inc="tT" if dt == 15 else None)
                if k == 11:
                    consumed[m] = tv
                if real:
                    u_free[k % 2] = C.on("vector", lambda e, sc=sc, fc=fc, u_ps=u_ps:
                                         e.tensor_tensor(
                        out=ucb[:, fc * HALF + sc * 512:fc * HALF + (sc + 1) * 512],
                        in0=u_ps, in1=cbT[:, sc * 512:(sc + 1) * 512], op=ALU.mult),
                        waits=[("tT", tv)], inc="tV")
                else:
                    u_free[k % 2] = C.on("vector", lambda e, sc=sc, fc=fc, u_ps=u_ps:
                                         e.tensor_copy(
                        out=ucb[:, fc * HALF + sc * 512:fc * HALF + (sc + 1) * 512],
                        in_=u_ps), waits=[("tT", tv)], inc="tV")
            m = mb + 1
            slot = m % 2
            slx = ("dgA", "dgB")[slot]
            for dt in range(16):
                if m >= 2 and dt == 0:
                    w = [("tT", consumed[m - 2])]
                elif dt == 0:
                    w = [("tV", topk_tV)]
                else:
                    w = []
                gv = C.on("gpsimd", lambda e, ei=ei, dt=dt, slot=slot: e.dma_start(
                    out=ring[slot][:, dt * FE:(dt + 1) * FE], in_=gw_src(ei, dt)),
                    waits=w, inc=slx, inck=16)
            for k in range(12):
                sc, fc = k // 6, k % 6
                g_ps = PS3[:, (k % 2) * 512:(k % 2) * 512 + 512]
                w = []
                if k == 0:
                    w = [(slx, gv)]
                if g_free.get(k % 2):
                    w.append(("tS", g_free[k % 2]))
                for dt in range(16):
                    tv = C.on("tensor", lambda e, dt=dt, sc=sc, fc=fc, slot=slot,
                              g_ps=g_ps: e.matmul(
                        g_ps, ring[slot][:, dt * FE + fc * 128:dt * FE + (fc + 1) * 128],
                        x1bf[:, dt * HALF + sc * 512:dt * HALF + (sc + 1) * 512],
                        start=(dt == 0), stop=(dt == 15)),
                        waits=w if dt == 0 else [], inc="tT" if dt == 15 else None)
                if k == 11:
                    consumed[m] = tv
                w = [("tT", tv)]
                if tsil_free.get(k % 2):
                    w.append(("tV", tsil_free[k % 2]))
                g_free[k % 2] = C.on("scalar", lambda e, k=k, g_ps=g_ps: e.activation(
                    out=tsil[k % 2][:], in_=g_ps, func=AF.Silu, bias=0.0, scale=1.0),
                    waits=w, inc="tS")
                tsil_free[k % 2] = C.on("vector", lambda e, sc=sc, fc=fc, k=k:
                                        e.tensor_tensor(
                    out=ucb[:, fc * HALF + sc * 512:fc * HALF + (sc + 1) * 512],
                    in0=tsil[k % 2][:],
                    in1=ucb[:, fc * HALF + sc * 512:fc * HALF + (sc + 1) * 512],
                    op=ALU.mult), waits=[("tS", g_free[k % 2])], inc="tV")
                if k % 6 == 5:
                    act_v[sc] = tsil_free[k % 2]
            m = mb + 2
            slot = m % 2
            slx = ("dgA", "dgB")[slot]
            for ft in range(6):
                w = [("tT", consumed[m - 2])] if (m >= 2 and ft == 0) else []
                dv = C.on("gpsimd", lambda e, ei=ei, ft=ft, slot=slot: e.dma_start(
                    out=ring[slot][:, ft * D:(ft + 1) * D], in_=dw_src(ei, ft)),
                    waits=w, inc=slx, inck=16)
            for j in range(32):
                dc, sc = j // 2, j % 2
                if ei == 0 and sc == 0:
                    slf = ("dsC", "dsD")[dc % 2]
                    wx = []
                    if dc >= 2:
                        wx.append(("tV", x1sl[dc - 2]))
                    xv = C.on("sync", lambda e, dc=dc, tok0=tok0: e.dma_start(
                        out=x1ft[dc % 2][:], in_=x1f_scr[dc][:, tok0:tok0 + HALF]),
                        waits=wx, inc=slf, inck=16)
                    x1sl[dc] = ("ld", xv)
                r_ps = PS1[:, (j % 2) * 512:(j % 2) * 512 + 512]
                w = []
                if j == 0:
                    w = [(slx, dv), ("tV", act_v[0])]
                if j == 1:
                    w = [("tV", act_v[1])]
                if j >= 2 and r_free.get(j % 2):
                    w.append(("tV", r_free[j % 2]))
                for ft in range(6):
                    tv = C.on("tensor", lambda e, ft=ft, dc=dc, sc=sc, slot=slot,
                              r_ps=r_ps: e.matmul(
                        r_ps, ring[slot][:, ft * D + dc * 128:ft * D + (dc + 1) * 128],
                        ucb[:, ft * HALF + sc * 512:ft * HALF + (sc + 1) * 512],
                        start=(ft == 0), stop=(ft == 5)),
                        waits=w if ft == 0 else [], inc="tT" if ft == 5 else None)
                if j == 31:
                    consumed[m] = tv
                dst = acc[:, dc * HALF + sc * 512:dc * HALF + (sc + 1) * 512]
                if ei == 0:
                    slf = ("dsC", "dsD")[dc % 2]
                    w2 = [("tT", tv), (slf, x1sl[dc][1])]
                    r_free[j % 2] = C.on("vector", lambda e, dst=dst, dc=dc, sc=sc,
                                         r_ps=r_ps: e.tensor_tensor(
                        out=dst, in0=r_ps,
                        in1=x1ft[dc % 2][:, sc * 512:(sc + 1) * 512],
                        op=ALU.add), waits=w2, inc="tV")
                    if sc == 1:
                        x1sl[dc] = r_free[j % 2]
                else:
                    r_free[j % 2] = C.on("vector", lambda e, dst=dst, r_ps=r_ps:
                                         e.tensor_tensor(out=dst, in0=r_ps, in1=dst,
                                                         op=ALU.add),
                                         waits=[("tT", tv)], inc="tV")
        C.barrier()

        sq2 = {}
        st2 = {}
        for j in range(32):
            dc, sc = j // 2, j % 2
            w = []
            if j >= 2:
                w.append(("tT", st2[j - 2]))
            sq2[j] = C.on("scalar", lambda e, dc=dc, sc=sc, j=j: e.square(
                tsil[j % 2][:], acc[:, dc * HALF + sc * 512:dc * HALF + (sc + 1) * 512]),
                waits=w, inc="tS")
            C.on("tensor", lambda e, dc=dc, sc=sc: e.matmul(
                (PS2[:, 0:512], PS2[:, 512:1024])[sc], ones_f[:],
                acc[:, dc * HALF + sc * 512:dc * HALF + (sc + 1) * 512],
                start=(dc == 0), stop=(dc == 15)), inc="tT")
            st2[j] = C.on("tensor", lambda e, dc=dc, sc=sc, j=j: e.matmul(
                (PS3[:, 0:512], PS3[:, 512:1024])[sc], ones_f[:], tsil[j % 2][:],
                start=(dc == 0), stop=(dc == 15)),
                waits=[("tS", sq2[j])], inc="tT")
        tlast = st2[31]
        tm = C.on("scalar", lambda e: e.activation(out=mean2[:, 0:512],
                                                   in_=PS2[:, 0:512], func=AF.Copy,
                                                   bias=0.0, scale=1.0 / D),
                  waits=[("tT", tlast)], inc="tS")
        tm = C.on("scalar", lambda e: e.activation(out=mean2[:, 512:1024],
                                                   in_=PS2[:, 512:1024], func=AF.Copy,
                                                   bias=0.0, scale=1.0 / D), inc="tS")
        tm = C.on("scalar", lambda e: e.activation(out=slB2[:, 0:512],
                                                   in_=PS3[:, 0:512], func=AF.Copy,
                                                   bias=0.0, scale=1.0 / D), inc="tS")
        tm = C.on("scalar", lambda e: e.activation(out=slB2[:, 512:1024],
                                                   in_=PS3[:, 512:1024], func=AF.Copy,
                                                   bias=0.0, scale=1.0 / D), inc="tS")
        mv = C.on("vector", lambda e: e.tensor_tensor(out=rstd2[:], in0=mean2[:],
                                                      in1=mean2[:], op=ALU.mult),
                  waits=[("tS", tm)], inc="tV")
        C.on("vector", lambda e: e.drain())
        mv = C.on("vector", lambda e: e.tensor_tensor(out=slB2[:], in0=slB2[:],
                                                      in1=rstd2[:], op=ALU.subtract),
                  inc="tV")
        sv = C.on("scalar", lambda e: e.activation(out=slB2[:], in_=slB2[:],
                                                   func=AF.Sqrt, bias=epsT[:],
                                                   scale=1.0),
                  waits=[("tV", mv)], inc="tS")
        rv = C.on("vector", lambda e: e.reciprocal(rstd2[:], slB2[:]),
                  waits=[("tS", sv)], inc="tV")
        C.on("vector", lambda e: e.drain())
        fin = {}
        for dp in range(0, 16, 2):
            for dc in (dp, dp + 1):
                C.on("vector", lambda e, dc=dc: e.tensor_tensor(
                    out=acc[:, dc * HALF:(dc + 1) * HALF],
                    in0=acc[:, dc * HALF:(dc + 1) * HALF], in1=mean2[:],
                    op=ALU.subtract), waits=[("tV", rv)])
            for dc in (dp, dp + 1):
                fin[dc] = C.on("vector", lambda e, dc=dc: e.tensor_tensor(
                    out=acc[:, dc * HALF:(dc + 1) * HALF],
                    in0=acc[:, dc * HALF:(dc + 1) * HALF], in1=rstd2[:],
                    op=ALU.mult), inc="tV")
        C.barrier()
        ysl = {}
        prev = None
        for sb in range(8):
            slo = ("dsE", "dsF")[sb % 2]
            for dc in range(16):
                t_ps = (PS0, PS1)[dc % 2][:, 0:128]
                w = []
                if sb >= 2 and dc == 0:
                    w.append((slo, ysl[sb - 2]))
                tt = C.on("tensor", lambda e, sb=sb, dc=dc, t_ps=t_ps: e.transpose(
                    t_ps, acc[:, dc * HALF + sb * 128:dc * HALF + (sb + 1) * 128],
                    id_f[:]), waits=w + ([("tV", prev)] if prev else []), inc="tT")
                prev = C.on("vector", lambda e, sb=sb, dc=dc, t_ps=t_ps: e.tensor_copy(
                    out=ostg[sb % 2][:, dc * 128:(dc + 1) * 128], in_=t_ps),
                    waits=[("tT", tt)], inc="tV")
            ysl[sb] = C.on("sync", lambda e, sb=sb, tok0=tok0: e.dma_start(
                out=y_d[tok0 + sb * 128:tok0 + (sb + 1) * 128, :],
                in_=ostg[sb % 2][:]), waits=[("tV", prev)], inc=slo, inck=16)
        C.barrier()

    de, df = C.cnt["dsE"], C.cnt["dsF"]

    def fin_sync(e):
        e.wait_ge(C.sems["dsE"], de)
        e.wait_ge(C.sems["dsF"], df)
    C.flush()
    blk.sync(fin_sync)

    blk_cm.__exit__(None, None, None)
    return nc


_NC_CACHE = []


def kernel(L0_attn_x, L0_attn_Wq, L0_attn_Wk, L0_attn_Wv, L0_attn_Wo,
           L0_ffn_gate_w, L0_ffn_wg, L0_ffn_wu, L0_ffn_wd,
           L0_ffn_sg, L0_ffn_su, L0_ffn_sd):
    if not _NC_CACHE:
        _NC_CACHE.append(build_program())
    nc = _NC_CACHE[0]
    x = np.ascontiguousarray(np.asarray(L0_attn_x, dtype=np.float32))
    shared = {
        "wq": np.ascontiguousarray(np.asarray(L0_attn_Wq, dtype=np.float32)),
        "wk": np.ascontiguousarray(np.asarray(L0_attn_Wk, dtype=np.float32)),
        "wv": np.ascontiguousarray(np.asarray(L0_attn_Wv, dtype=np.float32)),
        "wo": np.ascontiguousarray(np.asarray(L0_attn_Wo, dtype=np.float32)),
        "gw": np.ascontiguousarray(np.asarray(L0_ffn_gate_w, dtype=np.float32)),
        "wg": np.ascontiguousarray(np.asarray(L0_ffn_wg, dtype=np.float32)),
        "wu": np.ascontiguousarray(np.asarray(L0_ffn_wu, dtype=np.float32)),
        "wd": np.ascontiguousarray(np.asarray(L0_ffn_wd, dtype=np.float32)),
        "sg": np.ascontiguousarray(np.asarray(L0_ffn_sg, dtype=np.float32)),
        "su": np.ascontiguousarray(np.asarray(L0_ffn_su, dtype=np.float32)),
        "sd": np.ascontiguousarray(np.asarray(L0_ffn_sd, dtype=np.float32)),
    }
    in_maps = [dict(shared, x=np.ascontiguousarray(x[b])) for b in range(8)]
    res = run_bass_kernel_spmd(nc, in_maps, core_ids=list(range(8)))
    out = np.stack([np.asarray(r["y"]) for r in res.results], axis=0)
    return out.astype(np.float32)


if __name__ == "__main__":
    build_program()
    print("build ok")
